# version 16
# baseline (speedup 1.0000x reference)
import contextlib
import numpy as np
import concourse.bass as bass
import concourse.mybir as mybir
from concourse.bass_utils import run_bass_kernel_spmd

F32 = mybir.dt.float32
BF16 = mybir.dt.bfloat16
ALU = mybir.AluOpType
AF = mybir.ActivationFunctionType
AX = mybir.AxisListType.X

T = 2048
TH = 1024
D = 1024
DC = 8
H0 = 4096
HC0 = 32
I1 = 3072
IC1 = 24
NTQ = 4
RND = 12582912.0
CAP = 60.0
NCORES = 8
EPS_LN = 1e-5

DEBUG_STOP = None
DEBUG_DUMPS = ()

CHANS = ("t", "v", "s", "g", "q", "gq", "aq", "c")
ENG_OF = {"t": "t", "v": "v", "s": "s", "g": "g", "q": "q", "gq": "g", "aq": "s", "c": "g"}
ASYNC = ("q", "gq", "aq", "c")
NSUB = {"q": 16, "gq": 8, "aq": 4}


class Op:
    __slots__ = ("chan", "fn", "deps", "seq", "sig", "note", "sub")

    def __init__(self, chan, fn, deps, note):
        self.chan = chan
        self.fn = fn
        self.deps = deps
        self.seq = None
        self.sig = False
        self.note = note
        self.sub = 0


def is_dma(chan):
    return chan in ("q", "gq", "aq")


class P:
    def __init__(self, nc):
        self.nc = nc
        self.ops = []
        self.open_cm = {}
        self.sb = {}
        self.dr = {}
        self.seg = {}

    def alloc(self, name, shape, dtype, psum=False, side="left"):
        if psum:
            cm = self.nc.psum_tensor(name, shape, dtype)
        else:
            cm = self.nc.sbuf_tensor(name, shape, dtype, side=side)
        h = cm.__enter__()
        self.open_cm[name] = cm
        self.sb[name] = h
        return h

    def free(self, *names):
        for name in names:
            self.open_cm.pop(name).__exit__(None, None, None)

    def op(self, chan, fn, deps=(), note=""):
        o = Op(chan, fn, [d for d in deps if d is not None], note)
        self.ops.append(o)
        return o

    def dma(self, chan, out, in_, deps=(), note="dma", transpose=False):
        assert is_dma(chan)
        def fn(e):
            return e.dma_start(out=out, in_=in_, transpose=transpose)
        return self.op(chan, fn, deps, note)

    def barrier_deps(self):
        last = {}
        for o in self.ops:
            last[o.chan] = o
        return list(last.values())

    def emit(self, sems):
        for o in self.ops:
            if is_dma(o.chan) or o.chan in ("c", "v", "s", "g"):
                o.sig = True
            for d in o.deps:
                d.sig = True
        counters = {c: 0 for c in CHANS}
        subcnt = {}
        for o in self.ops:
            if not o.sig:
                continue
            if is_dma(o.chan):
                idx = counters[o.chan]
                counters[o.chan] += 1
                ns = NSUB[o.chan]
                o.sub = idx % ns
                k = subcnt.get((o.chan, o.sub), 0) + 1
                subcnt[(o.chan, o.sub)] = k
                o.seq = 16 * k
            else:
                counters[o.chan] += 1
                o.seq = counters[o.chan]
        eng_ops = {e: [] for e in ("t", "v", "s", "g", "q")}
        for o in self.ops:
            eng_ops[ENG_OF[o.chan]].append(o)

        def semof(chan, sub):
            return sems[chan][sub] if is_dma(chan) else sems[chan]

        def emit_engine(e, handle):
            waited = {}
            for o in eng_ops[e]:
                if o.chan in ("v", "s", "g") and o.seq is not None and o.seq > 1:
                    key = (o.chan, 0)
                    if o.seq - 1 > waited.get(key, 0):
                        handle.wait_ge(semof(o.chan, 0), o.seq - 1)
                        waited[key] = o.seq - 1
                for d in o.deps:
                    key = (d.chan, d.sub)
                    if d.seq > waited.get(key, 0):
                        handle.wait_ge(semof(d.chan, d.sub), d.seq)
                        waited[key] = d.seq
                if o.sig and is_dma(o.chan) and o.seq > 16:
                    key = (o.chan, o.sub)
                    if o.seq - 16 > waited.get(key, 0):
                        handle.wait_ge(semof(o.chan, o.sub), o.seq - 16)
                        waited[key] = o.seq - 16
                ins = o.fn(handle)
                if o.sig:
                    assert ins is not None, o.note
                    ins.then_inc(semof(o.chan, o.sub), 16 if is_dma(o.chan) else 1)

        with self.nc.Block() as b:
            @b.tensor
            def _(t):
                emit_engine("t", t)

            @b.vector
            def _(v):
                emit_engine("v", v)

            @b.scalar
            def _(s):
                emit_engine("s", s)

            @b.gpsimd
            def _(g):
                emit_engine("g", g)

            @b.sync
            def _(q):
                emit_engine("q", q)


def dap(dr_tensor, offset, pattern):
    return bass.AP(dr_tensor, offset, pattern)


def ts(i, w):
    return slice(i * w, (i + 1) * w)


S_SW1, S_SW2, S_RW1, S_RW2 = 0, 1, 2, 3
S_RS0, S_S0, S_RS1, S_S1 = 4, 5, 6, 7
S_RS2, S_S2, S_RS3, S_S3 = 8, 9, 10, 11
S_RS4, S_S4, S_S265536, S_S3S1 = 12, 13, 14, 15


def build(debug_stop=None, debug_dumps=(), debug_dram=()):
    nc = bass.Bass("TRN2", target_bir_lowering=False)
    p = P(nc)
    dr = p.dr
    dr["x"] = nc.dram_tensor("x", [T, D], F32, kind="ExternalInput")
    dr["out"] = nc.dram_tensor("out", [T, D], F32, kind="ExternalOutput")
    for nm in ["bvq_c", "bvk_c", "bvv_c", "td_c", "tf_c", "tmk_c", "tmv_c", "tmr_c"]:
        dr[nm] = nc.dram_tensor(nm, [128, 8], F32, kind="ExternalInput")
    for nm in ["cw0_c", "cw1_c", "cw2_c", "cb_c"]:
        dr[nm] = nc.dram_tensor(nm, [128, 32], F32, kind="ExternalInput")
    dr["w1_sh"] = nc.dram_tensor("w1_sh", [512, 1024], F32, kind="ExternalInput")
    dr["w2_sh"] = nc.dram_tensor("w2_sh", [128, 4096], F32, kind="ExternalInput")
    dr["wup_sh"] = nc.dram_tensor("wup_sh", [384, 1024], F32, kind="ExternalInput")
    dr["wgate_sh"] = nc.dram_tensor("wgate_sh", [384, 1024], F32, kind="ExternalInput")
    dr["wdown_sh"] = nc.dram_tensor("wdown_sh", [128, 3072], F32, kind="ExternalInput")
    for nm in ["wk_sh", "wv_sh", "wr_sh", "wo_sh"]:
        dr[nm] = nc.dram_tensor(nm, [128, 1024], F32, kind="ExternalInput")

    seg = p.seg
    off = 0
    for nm, n in [("w1T", 1024 * 512), ("w2T", 4096 * 128), ("upT", 1024 * 384),
                  ("gateT", 1024 * 384), ("downT", 3072 * 128), ("kT", 1024 * 128),
                  ("vT", 1024 * 128), ("rT", 1024 * 128), ("oT", 1024 * 128)]:
        seg[nm] = off
        off += n
    p.AGN = off
    dr["ag_in"] = nc.dram_tensor("ag_in", [p.AGN], BF16)
    dr["ag_out"] = nc.dram_tensor("ag_out", [NCORES * p.AGN], BF16)
    dr["ar_in"] = nc.dram_tensor("ar_in", [8], F32)
    dr["ar_out"] = nc.dram_tensor("ar_out", [8], F32)
    dr["hsil_sp"] = nc.dram_tensor("hsil_sp", [H0 * T], BF16)
    dr["up_sp"] = nc.dram_tensor("up_sp", [I1 * T], F32)
    dr["gate_sp"] = nc.dram_tensor("gate_sp", [I1 * T], F32)
    dr["gate2_sp"] = nc.dram_tensor("gate2_sp", [I1 * T], F32)
    dr["o_sp"] = nc.dram_tensor("o_sp", [I1 * T], F32)
    dr["x1_sp"] = nc.dram_tensor("x1_sp", [D * T], F32)
    dr["xn_sp"] = nc.dram_tensor("xn_sp", [D * T], F32)
    dr["xn1_sp"] = nc.dram_tensor("xn1_sp", [D * T], F32)
    dr["s1_sp"] = nc.dram_tensor("s1_sp", [D * T], F32)

    p.alloc("identF", [128, 128], F32)
    p.alloc("identB", [128, 128], BF16)
    p.alloc("ones", [128, 128], F32)
    for nm in ["sqk_c", "sv_c", "w_c", "nu_c", "tmk", "tmv", "tmr"]:
        p.alloc(nm, [128, 8], F32)
    for nm in ["cw0", "cw1", "cw2", "cb"]:
        p.alloc(nm, [128, 32], F32)
    p.alloc("scal", [128, 16], F32)
    p.alloc("epsc", [128, 1], F32)
    p.alloc("rndc", [128, 2], F32)
    p.alloc("stat", [128, 64], F32)
    p.alloc("tiny", [128, 16], F32)
    p.alloc("tiny2", [128, 16], F32)
    p.alloc("tinyb", [128, 16], BF16)
    p.alloc("mxc", [128, 64], F32)
    p.alloc("flatrow", [1, 512], F32)
    p.alloc("flat16", [16, 128], F32)
    p.alloc("f1", [128, T], F32)
    p.alloc("f2", [128, T], F32)
    p.alloc("f3", [128, T], F32)
    p.alloc("g1s", [128, T], BF16)
    p.alloc("g2s", [128, T], BF16)
    p.alloc("wt0", [128, DC, 128], BF16)
    p.alloc("wt1", [128, DC, 128], BF16)
    p.alloc("wt2", [128, HC0, 128], BF16, side="right")
    p.alloc("wt3", [128, HC0, 128], BF16, side="right")
    p.alloc("pA", [128, 512], F32, psum=True)
    p.alloc("pB", [128, 512], F32, psum=True)
    p.alloc("pC", [128, 512], F32, psum=True)
    p.alloc("pD", [128, 512], F32, psum=True)
    p.alloc("pE", [128, 512], F32, psum=True)
    p.alloc("pT", [128, 128], F32, psum=True)
    p.alloc("pT2", [128, 128], F32, psum=True)
    p.alloc("pTb", [128, 128], BF16, psum=True)

    st = {}
    ph0(p, st)
    ph1(p, st)
    if debug_stop != "p1":
        ph2(p, st)
        if debug_stop != "p2":
            ph3(p, st)
            if debug_stop != "p3":
                ph4(p, st)
                if debug_stop != "p4":
                    ph5(p, st)
                    if debug_stop != "p5":
                        ph6(p, st)

    if debug_dram:
        bd = p.barrier_deps()
        dmp = []
        for nm in debug_dram:
            t = dr[nm]
            n = int(np.prod(t.shape))
            dt = nc.dram_tensor("ddump_" + nm, [n], t.dtype, kind="ExternalOutput")
            dmp.append(p.dma("q", dt.ap(), t.ap(), deps=bd, note="dma_ddump"))
        p.op("g", lambda e: e.nop(), deps=dmp, note="ddump_wait")
    if debug_dumps:
        bd = p.barrier_deps()
        dmp = []
        for nm in debug_dumps:
            t = p.sb[nm]
            shape = list(t.shape)
            flat = [shape[0], int(np.prod(shape[1:]))]
            dt = nc.dram_tensor("dump_" + nm, flat, t.dtype, kind="ExternalOutput")
            ap = t[:, :] if len(shape) == 2 else t.ap().rearrange("p a b -> p (a b)")
            dmp.append(p.dma("q", dt.ap(), ap, deps=bd, note="dma_dump"))
        p.op("g", lambda e: e.nop(), deps=dmp, note="dump_wait")

    if "out_dmas" in st:
        p.op("g", lambda e: e.nop(), deps=st["out_dmas"][-64:], note="final_wait")

    stk = contextlib.ExitStack()
    sems = {}
    for c in CHANS:
        if c in NSUB:
            sems[c] = [stk.enter_context(nc.semaphore(f"sem_{c}{i}"))
                       for i in range(NSUB[c])]
        else:
            sems[c] = stk.enter_context(nc.semaphore("sem_" + c))
    p.emit(sems)
    for name in reversed(list(p.open_cm)):
        p.open_cm[name].__exit__(None, None, None)
    p.open_cm.clear()
    stk.close()
    return nc, p


def quant3_col_g(p, out_ap, in_ap, scale_col, lo, hi, deps, tmp):
    return quant3_col(p, out_ap, in_ap, scale_col, lo, hi, deps, tmp)


def quant3_col(p, out_ap, in_ap, scale_col, lo, hi, deps, tmp, chan3="v"):
    S = p.sb
    o1 = p.op("s", lambda e: e.activation(tmp, in_ap, AF.Copy, scale=scale_col,
                                          bias=RND), deps, "q3a")
    o2 = p.op("s", lambda e: e.activation(tmp, tmp, AF.Copy, bias=-RND), [o1], "q3b")
    return p.op("v", lambda e: e.tensor_scalar(out_ap, tmp, hi, lo, ALU.min, ALU.max),
                [o2], "q3c")


def quant3_row(p, out_ap, in_ap, row_ap, lo, hi, deps, tmp, chans=None):
    S = p.sb
    o1 = p.op("v", lambda e: e.tensor_mul(tmp, in_ap, row_ap), deps, "qra")
    o2 = p.op("s", lambda e: e.activation(tmp, tmp, AF.Copy, bias=RND), [o1], "qrb1")
    o3 = p.op("s", lambda e: e.activation(tmp, tmp, AF.Copy, bias=-RND), [o2], "qrb2")
    return p.op("v", lambda e: e.tensor_scalar(out_ap, tmp, hi, lo, ALU.min, ALU.max),
                [o3], "qrc")


def bcast_p0(p, src_row, dst, n, deps, tag):
    S = p.sb
    mm = p.op("t", lambda e: e.matmul(S["pT2"][:, 0:n], S["ones"][0:1, :], src_row,
                                      start=True, stop=True), deps, "bc_mm_" + tag)
    return p.op("v", lambda e: e.tensor_copy(dst, S["pT2"][:, 0:n]), [mm], "bc_cp_" + tag)


def row_bcast(p, st, cols16, dst, deps, tag):
    S = p.sb
    guard = getattr(p, "rb_guard", [])
    tp = p.op("t", lambda e: e.transpose(S["pT2"][0:16, 0:128], cols16, S["identF"][:, :]),
              list(deps) + [st["consts"]] + guard, "rb_tp_" + tag)
    cp = p.op("v", lambda e: e.tensor_copy(S["flat16"][0:16, :], S["pT2"][0:16, 0:128]),
              [tp], "rb_cp_" + tag)
    last = None
    allfls = []
    for tq in range(NTQ):
        fls = []
        for r in range(4):
            i = tq * 4 + r
            fls.append(p.dma("gq", S["flatrow"][0:1, ts(r, 128)],
                             S["flat16"][i:i + 1, :], deps=[cp, last], note="dma_flat"))
        mm = p.op("t", lambda e: e.matmul(S["pE"][:, :], S["ones"][0:1, :],
                                          S["flatrow"][0:1, :], start=True, stop=True),
                  fls + [last], "rb_mm_" + tag)
        last = p.op("v", lambda e, tq=tq: e.tensor_copy(dst[:, ts(tq, 512)], S["pE"][:, :]),
                    [mm], "rb_ev_" + tag)
        allfls += fls
    p.rb_guard = allfls
    return last


def global_max_ar(p, st, cols_ap, slot, s_idx, rs_idx, tag, deps):
    S = p.sb
    dr = p.dr
    cl = p.op("v", lambda e: e.reduce_max(S["mxc"][:, 62:63], cols_ap, AX),
              list(deps) + [st["ar_last"]], "cl_" + tag)
    tp = p.op("t", lambda e: e.transpose(S["pT2"][0:1, 0:128], S["mxc"][:, 62:63],
                                         S["identF"][:, :]), [cl], "tp_" + tag)
    rd = p.op("v", lambda e: e.reduce_max(S["mxc"][0:1, 63:64], S["pT2"][0:1, 0:128], AX),
              [tp], "rd_" + tag)
    d1 = p.dma("gq", dr["ar_in"].ap()[slot:slot + 1], S["mxc"][0:1, 63:64], deps=[rd],
               note="dma_" + tag)
    ar = p.op("c", lambda e: e.collective_compute(
        "AllReduce", ALU.max, replica_groups=[list(range(NCORES))],
        ins=[dr["ar_in"].ap()[slot:slot + 1].opt()],
        outs=[dr["ar_out"].ap()[slot:slot + 1].opt()]), [d1, st["ar_last"]], "AR_" + tag)
    st["ar_last"] = ar
    lb = p.dma("gq", S["mxc"][0:1, 62:63], dr["ar_out"].ap()[slot:slot + 1],
               deps=[ar, tp], note="dma_b" + tag)
    bc = bcast_p0(p, S["mxc"][0:1, 62:63], S["scal"][:, s_idx:s_idx + 1], 1, [lb], tag)
    sc1 = p.op("v", lambda e: e.tensor_scalar(S["scal"][:, s_idx:s_idx + 1],
                                              S["scal"][:, s_idx:s_idx + 1],
                                              1.0 / 32767.0, 1e-12,
                                              ALU.mult, ALU.max), [bc], "s_" + tag)
    return p.op("v", lambda e: e.reciprocal(S["scal"][:, rs_idx:rs_idx + 1],
                                            S["scal"][:, s_idx:s_idx + 1]), [sc1],
                "rs_" + tag)


def wts_sel(S, i):
    return S["wt0"] if i % 2 == 0 else S["wt1"]


def ph0(p, st):
    S = p.sb
    dr = p.dr
    x = dr["x"].ap()

    p.alloc("SA", [128, DC, T], F32)
    p.alloc("XT", [128, 16, 1024], F32)
    p.alloc("xn_t0", [128, 1024], F32)
    p.alloc("xn_t1", [128, 1024], F32)
    p.alloc("sqd", [128, 1024], F32)

    prm = {}
    for srcn, dstn in [("bvq_c", "sqk_c"), ("bvv_c", "sv_c"), ("cw0_c", "cw0"),
                       ("cw1_c", "cw1"), ("cw2_c", "cw2"), ("cb_c", "cb"),
                       ("td_c", "w_c"), ("tf_c", "nu_c"), ("tmk_c", "tmk"),
                       ("tmv_c", "tmv"), ("tmr_c", "tmr")]:
        prm[dstn] = p.dma("gq", S[dstn][:, :], dr[srcn].ap(), note="dma_prm")
    bvk = None

    cz1 = p.op("g", lambda e: e.memset(S["identF"][:, :], 0.0), (), "czF")
    ca1 = p.op("g", lambda e: e.affine_select(
        S["identF"][:, :], S["identF"][:, :], [[-1, 128]], ALU.not_equal, 1.0,
        base=0, channel_multiplier=1), [cz1], "caF")
    cz2 = p.op("g", lambda e: e.memset(S["identB"][:, :], 0.0), (), "czB")
    ca2 = p.op("g", lambda e: e.affine_select(
        S["identB"][:, :], S["identB"][:, :], [[-1, 128]], ALU.not_equal, 1.0,
        base=0, channel_multiplier=1), [cz2], "caB")
    cz3 = p.op("g", lambda e: e.memset(S["ones"][:, :], 1.0), (), "czO")
    cz4 = p.op("g", lambda e: e.memset(S["epsc"][:, :], EPS_LN), (), "czE")
    czr = p.op("g", lambda e: e.memset(S["rndc"][:, 0:1], RND), (), "czR1")
    czr2 = p.op("g", lambda e: e.memset(S["rndc"][:, 1:2], -RND), (), "czR2")
    cz5 = p.op("g", lambda e: e.memset(S["scal"][:, :], 0.0), (), "czS")
    cz6 = p.op("g", lambda e: e.memset(S["mxc"][:, :], 0.0), (), "czM")
    cz7 = p.op("g", lambda e: e.memset(S["stat"][:, :], 0.0), (), "czT")
    cz8 = p.op("g", lambda e: e.memset(S["tiny"][:, :], 0.0), (), "czU")
    cz9 = p.op("g", lambda e: e.memset(S["tiny2"][:, :], 0.0), (), "czV")
    consts = p.op("g", lambda e: e.engine_nop(),
                  [ca1, ca2, cz3, cz4, cz5, cz6, cz7, cz8, cz9, czr, czr2], "consts")
    st["consts"] = consts
    bvk = p.dma("gq", S["tiny2"][:, 0:8], dr["bvk_c"].ap(), deps=[cz9], note="dma_prm")

    s1 = p.op("s", lambda e: e.sign(S["sqk_c"][:, :], S["sqk_c"][:, :]), [prm["sqk_c"]], "sgq")
    s2 = p.op("s", lambda e: e.sign(S["tiny2"][:, 0:8], S["tiny2"][:, 0:8]), [bvk], "sgk")
    s3 = p.op("s", lambda e: e.sign(S["sv_c"][:, :], S["sv_c"][:, :]), [prm["sv_c"]], "sgv")
    st["sqk"] = p.op("v", lambda e: e.tensor_mul(S["sqk_c"][:, :], S["sqk_c"][:, :],
                                                 S["tiny2"][:, 0:8]), [s1, s2], "sqkm")
    st["sv"] = s3
    w1e = p.op("s", lambda e: e.activation(S["w_c"][:, :], S["w_c"][:, :], AF.Exp),
               [prm["w_c"]], "wexp")
    st["wc"] = p.op("v", lambda e: e.tensor_scalar(S["w_c"][:, :], S["w_c"][:, :], -1.0,
                                                   None, ALU.mult), [w1e], "wneg")
    st["nu"] = p.op("v", lambda e: e.tensor_scalar(S["nu_c"][:, :], S["nu_c"][:, :], -1.0,
                                                   None, ALU.mult), [prm["nu_c"]], "nuneg")

    loads, sq_ops, sm_ops = [], [], []
    for i in range(16):
        ld = p.dma("q", S["XT"][:, i, :], x[ts(i, 128), :], note="dma_x")
        loads.append(ld)
        sq1 = p.op("v", lambda e, i=i: e.tensor_mul(
            S["sqd"][:, :], S["XT"][:, i, :], S["XT"][:, i, :]), [ld, consts], "sq1")
        sq_ops.append(p.op("v", lambda e, i=i: e.reduce_sum(
            S["stat"][:, 16 + i:17 + i], S["sqd"][:, :], AX), [sq1], "sq2"))
        sm_ops.append(p.op("v", lambda e, i=i: e.reduce_sum(
            S["stat"][:, i:i + 1], S["XT"][:, i, :], AX), [ld, consts], "sm"))
    f1_ = p.op("v", lambda e: e.tensor_scalar(S["stat"][:, 0:16], S["stat"][:, 0:16],
                                              1.0 / D, None, ALU.mult), sm_ops, "mean")
    f2_ = p.op("v", lambda e: e.tensor_scalar(S["stat"][:, 16:32], S["stat"][:, 16:32],
                                              1.0 / D, None, ALU.mult), sq_ops, "ex2")
    f3_ = p.op("v", lambda e: e.tensor_mul(S["tiny"][:, 0:16], S["stat"][:, 0:16],
                                           S["stat"][:, 0:16]), [f1_], "m2")
    f4_ = p.op("v", lambda e: e.tensor_sub(S["stat"][:, 16:32], S["stat"][:, 16:32],
                                           S["tiny"][:, 0:16]), [f2_, f3_], "var")
    f5_ = p.op("s", lambda e: e.activation(S["stat"][:, 16:32], S["stat"][:, 16:32],
                                           AF.Sqrt, bias=S["epsc"][:, 0:1]),
               [f4_, st["consts"]], "sd")
    f6_ = p.op("v", lambda e: e.reciprocal(S["stat"][:, 16:32], S["stat"][:, 16:32]),
               [f5_], "rstd")
    nmean = p.op("v", lambda e: e.tensor_scalar(S["tiny"][:, 0:16], S["stat"][:, 0:16],
                                                -1.0, None, ALU.mult), [f6_], "nmean")

    prev_cp = {}
    last_mx = None
    for i in range(16):
        buf = S["xn_t0"] if i % 2 == 0 else S["xn_t1"]
        nrm = p.op("v", lambda e, i=i, buf=buf: e.tensor_scalar(
            buf[:, :], S["XT"][:, i, :], S["tiny"][:, i:i + 1],
            S["stat"][:, 16 + i:17 + i], ALU.add, ALU.mult),
            [nmean, prev_cp.get((i - 2, DC - 1))], "nrm")
        last_mx = p.op("v", lambda e, i=i, buf=buf: e.reduce_max(
            S["stat"][:, 32 + i:33 + i], buf[:, :], AX, apply_absolute_value=True),
            [nrm], "sxmax")
        for dc in range(DC):
            bank = "pT" if (i * DC + dc) % 2 == 0 else "pT2"
            tp = p.op("t", lambda e, buf=buf, dc=dc, bank=bank: e.transpose(
                S[bank][:, :], buf[:, ts(dc, 128)], S["identF"][:, :]),
                [nrm, consts, prev_cp.get((i, dc - 2) if dc >= 2 else (i - 1, DC - 2 + dc))],
                "tpxn")
            cp = p.op("s", lambda e, i=i, dc=dc, bank=bank: e.copy(
                S["SA"][:, dc, ts(i, 128)], S[bank][:, :]), [tp], "cpxn")
            prev_cp[(i, dc)] = cp
    st["xn_done"] = [prev_cp[(15, DC - 1)], prev_cp[(15, DC - 2)], last_mx]
    st["sx_done"] = last_mx
    st["xt_free"] = [nrm, last_mx, sq_ops[-1], sm_ops[-1]]
    p.free("sqd", "xn_t1", "xn_t0", "XT")


def ph1(p, st):
    S = p.sb
    dr = p.dr
    seg = p.seg
    AGN = p.AGN

    p.alloc("SXB", [128, T], BF16)
    p.alloc("RSXW1B", [128, T], F32)
    p.alloc("ws0", [128, 4096], F32)
    p.alloc("ws2", [128, 4096], F32)
    p.alloc("wq0", [128, 4096], BF16)
    p.alloc("wqt", [128, 32, 128], BF16)
    p.alloc("lt", [128, 96], F32)
    p.alloc("lt2", [128, 96], F32)

    ag_parts = []
    ld_w1 = [p.dma("q", S["ws0"][:, ts(j, 1024)], dr["w1_sh"].ap()[ts(j, 128), :],
                   deps=st["xt_free"], note="dma_w1s") for j in range(4)]
    ld_w2 = p.dma("q", S["ws2"][:, :], dr["w2_sh"].ap()[:, :], deps=st["xt_free"],
                  note="dma_w2s")
    a1 = p.op("v", lambda e: e.reduce_sum(S["tiny"][:, 0:1], S["ws0"][:, :], AX,
                                          apply_absolute_value=True), ld_w1, "absw1")
    a2 = p.op("v", lambda e: e.reduce_sum(S["tiny"][:, 1:2], S["ws2"][:, :], AX,
                                          apply_absolute_value=True), [ld_w2], "absw2")
    mm = p.op("t", lambda e: e.matmul(S["pT2"][:, 0:2], S["ones"][:, :], S["tiny"][:, 0:2],
                                      start=True, stop=True),
              [a1, a2, st["consts"]] + st["xn_done"], "summ")
    cps = p.op("v", lambda e: e.tensor_copy(S["mxc"][:, 56:58], S["pT2"][:, 0:2]), [mm], "sumc")
    d1 = p.dma("gq", dr["ar_in"].ap()[0:2], S["mxc"][0:1, 56:58], deps=[cps], note="dma_ar")
    ar = p.op("c", lambda e: e.collective_compute(
        "AllReduce", ALU.add, replica_groups=[list(range(NCORES))],
        ins=[dr["ar_in"].ap()[0:2].opt()], outs=[dr["ar_out"].ap()[0:2].opt()]),
        [d1], "AR_means")
    st["ar_last"] = ar
    lb = p.dma("gq", S["mxc"][0:1, 58:60], dr["ar_out"].ap()[0:2], deps=[ar], note="dma_arb")
    bc = bcast_p0(p, S["mxc"][0:1, 58:60], S["scal"][:, S_RW1:S_RW1 + 2], 2, [lb], "wm")
    c1 = p.op("v", lambda e: e.tensor_scalar(S["scal"][:, S_RW1:S_RW1 + 2],
                                             S["scal"][:, S_RW1:S_RW1 + 2],
                                             1.0 / (4096.0 * 1024.0),
                                             1e-5, ALU.mult, ALU.max), [bc], "wmc")
    sw = p.op("v", lambda e: e.reciprocal(S["scal"][:, S_SW1:S_SW1 + 2],
                                          S["scal"][:, S_RW1:S_RW1 + 2]), [c1], "wmr")
    st["sw"] = sw

    prev_ag = None
    for j in range(4):
        qd = quant3_col(p, S["wq0"][:, ts(j, 1024)], S["ws0"][:, ts(j, 1024)],
                        S["scal"][:, S_SW1:S_SW1 + 1], -1.0, 1.0, [sw, ld_w1[j]],
                        S["ws0"][:, ts(j, 1024)])
        tps = [p.dma("q", S["wqt"][:, dc, :],
                     S["wq0"][:, j * 1024 + dc * 128:j * 1024 + (dc + 1) * 128],
                     deps=[qd, prev_ag], note="dma_tp", transpose=True) for dc in range(8)]
        prev_ag = p.dma("gq", dap(dr["ag_in"], seg["w1T"] + j * 128,
                                  [[512, 128], [128 * 512, 8], [1, 128]]),
                        S["wqt"][:, 0:8, :], deps=tps, note="dma_agw")
        ag_parts.append(prev_ag)

    qd2 = quant3_col(p, S["wq0"][:, :], S["ws2"][:, :], S["scal"][:, S_SW2:S_SW2 + 1],
                     -1.0, 1.0, [sw, ld_w2, prev_ag], S["ws2"][:, :])
    tps = [p.dma("q", S["wqt"][:, hc, :], S["wq0"][:, ts(hc, 128)],
                 deps=[qd2, prev_ag], note="dma_tp", transpose=True) for hc in range(32)]
    prev_ag = p.dma("gq", dap(dr["ag_in"], seg["w2T"],
                              [[128, 128], [128 * 128, 32], [1, 128]]),
                    S["wqt"][:, 0:32, :], deps=tps, note="dma_agw2")
    ag_parts.append(prev_ag)

    def lpbq(src, rows, width, segnm, prev_ag, prev_m4):
        ntile = rows // 128
        nblk = width // 32
        nw = width // 128
        outs = []
        for j in range(ntile):
            ld = p.dma("q", S["ws0"][:, 0:width], src.ap()[ts(j, 128), :],
                       deps=[prev_m4], note="dma_lw")
            w3 = S["ws0"][:, 0:width].rearrange("p (b i) -> p b i", i=32)
            bm = p.op("v", lambda e, w3=w3, nblk=nblk: e.reduce_max(
                S["lt"][:, 0:nblk], w3, AX, apply_absolute_value=True), [ld], "lpbm")
            sc = p.op("v", lambda e, nblk=nblk: e.tensor_scalar(
                S["lt"][:, 0:nblk], S["lt"][:, 0:nblk], 1.0 / 7.0, 1e-12,
                ALU.mult, ALU.max), [bm], "lps")
            rc = p.op("v", lambda e, nblk=nblk: e.reciprocal(
                S["lt2"][:, 0:nblk], S["lt"][:, 0:nblk]), [sc], "lprs")
            rs_exp = dap(S["lt2"], 0, [[96, 128], [1, nblk], [0, 32]])
            s_exp = dap(S["lt"], 0, [[96, 128], [1, nblk], [0, 32]])
            m1 = p.op("v", lambda e, w3=w3, rs_exp=rs_exp: e.tensor_tensor(
                w3, w3, rs_exp, ALU.mult), [rc], "lpm1")
            m2 = p.op("v", lambda e, width=width: e.tensor_scalar(
                S["ws0"][:, 0:width], S["ws0"][:, 0:width], RND, -RND, ALU.add, ALU.add),
                [m1], "lprnd")
            m3 = p.op("v", lambda e, width=width: e.tensor_scalar(
                S["ws0"][:, 0:width], S["ws0"][:, 0:width], 7.0, -8.0, ALU.min, ALU.max),
                [m2], "lpclip")
            wq3 = S["wq0"][:, 0:width].rearrange("p (b i) -> p b i", i=32)
            m4 = p.op("v", lambda e, wq3=wq3, w3=w3, s_exp=s_exp: e.tensor_tensor(
                wq3, w3, s_exp, ALU.mult), [m3, prev_ag], "lpdeq")
            tps = [p.dma("q", S["wqt"][:, dc, :], S["wq0"][:, ts(dc, 128)],
                         deps=[m4, prev_ag], note="dma_tp", transpose=True)
                   for dc in range(nw)]
            prev_ag = p.dma("gq", dap(dr["ag_in"], seg[segnm] + j * 128,
                                      [[rows, 128], [128 * rows, nw], [1, 128]]),
                            S["wqt"][:, 0:nw, :], deps=tps, note="dma_aglp")
            outs.append(prev_ag)
            prev_m4 = m4
        return outs, prev_ag, prev_m4

    o1, prev_ag, pm = lpbq(dr["wup_sh"], 384, 1024, "upT", prev_ag, qd2)
    ag_parts += o1
    o2, prev_ag, pm = lpbq(dr["wgate_sh"], 384, 1024, "gateT", prev_ag, pm)
    ag_parts += o2
    o3, prev_ag, pm = lpbq(dr["wdown_sh"], 128, 3072, "downT", prev_ag, pm)
    ag_parts += o3

    prev_ldw = pm
    for nm, sg in [("wk_sh", "kT"), ("wv_sh", "vT"), ("wr_sh", "rT"), ("wo_sh", "oT")]:
        ld = p.dma("gq", S["wq0"][:, 0:1024], dr[nm].ap()[:, :], deps=[prev_ag, prev_ldw],
                   note="dma_wc")
        tps = [p.dma("q", S["wqt"][:, dc, :], S["wq0"][:, ts(dc, 128)],
                     deps=[ld, prev_ag], note="dma_tp", transpose=True) for dc in range(8)]
        prev_ag = p.dma("gq", dap(dr["ag_in"], seg[sg],
                                  [[128, 128], [128 * 128, 8], [1, 128]]),
                        S["wqt"][:, 0:8, :], deps=tps, note="dma_agk")
        ag_parts.append(prev_ag)
        prev_ldw = ld

    ag = p.op("c", lambda e: e.collective_compute(
        "AllGather", ALU.bypass, replica_groups=[list(range(NCORES))],
        ins=[dr["ag_in"].ap().opt()], outs=[dr["ag_out"].ap().opt()]),
        [ar] + ag_parts, "AG_w")
    st["ag"] = ag
    st["ar_last"] = ag

    sx1 = p.op("v", lambda e: e.tensor_scalar(S["stat"][:, 48:64], S["stat"][:, 32:48],
                                              1e-5, None, ALU.max), [st["sx_done"]], "sxc")
    sx2 = p.op("v", lambda e: e.reciprocal(S["tiny"][:, 0:16], S["stat"][:, 48:64]),
               [sx1], "sxr")
    sx3 = p.op("v", lambda e: e.tensor_scalar(S["tinyb"][:, 0:16], S["tiny"][:, 0:16],
                                              127.0, None, ALU.mult), [sx2], "sx127")
    sx3b = p.op("v", lambda e: e.tensor_copy(S["tiny"][:, 0:16], S["tinyb"][:, 0:16]),
                [sx3], "sxbk")
    sx4 = p.op("v", lambda e: e.reciprocal(S["tiny2"][:, 0:16], S["tiny"][:, 0:16]),
               [sx3b], "rsx")
    sx5 = p.op("v", lambda e: e.tensor_scalar(S["tiny2"][:, 0:16], S["tiny2"][:, 0:16],
                                              S["scal"][:, S_RW1:S_RW1 + 1], None,
                                              ALU.mult), [sx4, sw], "rsxw")
    st["sxb"] = row_bcast(p, st, S["tiny"][:, 0:16], S["SXB"],
                          [sx3b, sw] + st["xn_done"], "sxb")
    st["rsxw1b"] = row_bcast(p, st, S["tiny2"][:, 0:16], S["RSXW1B"],
                             [sx5, sw] + st["xn_done"], "rsxw")
    p.free("lt2", "lt", "wqt", "wq0", "ws2", "ws0")


def ph2(p, st):
    S = p.sb
    p.alloc("s_bf", [128, DC, T], BF16, side="right")
    p.alloc("cmask", [128, 4, 512], BF16, side="right")
    p.alloc("Qsb", [128, T], BF16)
    p.alloc("Xb", [128, T], BF16)
    p.alloc("Vbf", [128, T], BF16)
    p.alloc("Vtm", [128, 16, 128], BF16)
    p.alloc("attA0", [128, 512], BF16)
    p.alloc("attA1", [128, 512], BF16)
    p.alloc("attB0", [128, 512], BF16)
    p.alloc("attB1", [128, 512], BF16)

    cmk = []
    for rr in range(4):
        mz = p.op("g", lambda e, rr=rr: e.memset(S["cmask"][:, rr, :], 1.0),
                  [st["consts"]], "cmz")
        cmk.append(p.op("g", lambda e, rr=rr: e.affine_select(
            S["cmask"][:, rr, :], S["cmask"][:, rr, :], [[1, 512]], ALU.is_ge, 0.0,
            base=-128 * rr, channel_multiplier=-1), [mz], "cma"))
    evacs = []
    hp_fence = (list(st["xn_done"]) + [st["sqk"], st["sv"], st["consts"],
                st["sxb"], st["rsxw1b"]] + cmk)
    for hp in range(DC):
        qs = p.op("v", lambda e, hp=hp: e.tensor_scalar(
            S["Qsb"][:, :], S["SA"][:, hp, :], S["sqk_c"][:, hp:hp + 1], None, ALU.mult),
            hp_fence, "qs")
        xb = p.op("v", lambda e, hp=hp: e.tensor_copy(S["Xb"][:, :], S["SA"][:, hp, :]),
                  hp_fence, "xb")
        vb = p.op("v", lambda e, hp=hp: e.tensor_scalar(
            S["Vbf"][:, :], S["SA"][:, hp, :], S["sv_c"][:, hp:hp + 1], None, ALU.mult),
            hp_fence, "vb")
        vtm_ev = {}
        prev_ev = None
        for kc in range(16):
            tp = p.op("t", lambda e, kc=kc: e.transpose(
                S["pTb"][:, :], S["Vbf"][:, ts(kc, 128)], S["identB"][:, :]),
                [vb, prev_ev] + (hp_fence if kc == 0 else []), "vtp")
            ev = p.op("s", lambda e, kc=kc: e.copy(S["Vtm"][:, kc, :], S["pTb"][:, :]),
                      [tp], "vev")
            vtm_ev[kc] = ev
            prev_ev = ev
        evacs = []
        prev_sg = {}
        prev_av = {}
        for qt in range(NTQ):
            avbank = "pC"
            nkc = 4 * (qt + 1)
            last_avB = None
            for kc in range(nkc):
                attA = S["attA0"] if kc % 2 == 0 else S["attA1"]
                attB = S["attB0"] if kc % 2 == 0 else S["attB1"]
                bkA = "pA" if kc % 2 == 0 else "pB"
                bkB = "pD" if kc % 2 == 0 else "pE"
                qkA = p.op("t", lambda e, kc=kc, qt=qt, bkA=bkA: e.matmul(
                    S[bkA][:, :], S["Qsb"][0:64, ts(kc, 128)],
                    S["Xb"][0:64, ts(qt, 512)], start=True, stop=True,
                    tile_position=(0, 0), skip_group_check=True),
                    [qs, xb, prev_sg.get(("A", qt, kc - 2)) if kc >= 2
                     else prev_sg.get(("A", qt - 1, 4 * qt - 2 + kc))], "qkA")
                qkB = p.op("t", lambda e, kc=kc, qt=qt, bkB=bkB: e.matmul(
                    S[bkB][:, :], S["Qsb"][64:128, ts(kc, 128)],
                    S["Xb"][64:128, ts(qt, 512)], start=True, stop=True,
                    tile_position=(64, 0), skip_group_check=True),
                    [qs, xb, prev_sg.get(("B", qt, kc - 2)) if kc >= 2
                     else prev_sg.get(("B", qt - 1, 4 * qt - 2 + kc))], "qkB")
                sgA = p.op("s", lambda e, attA=attA, bkA=bkA: e.activation(
                    attA[:, :], S[bkA][:, :], AF.Sigmoid, scale=0.5),
                    [qkA, prev_av.get(("A", qt, kc - 2)) if kc >= 2
                     else prev_av.get(("A", qt - 1, 4 * qt - 1))], "sgA")
                sgB = p.op("s", lambda e, attB=attB, bkB=bkB: e.activation(
                    attB[:, :], S[bkB][:, :], AF.Sigmoid, scale=0.5),
                    [qkB, prev_av.get(("B", qt, kc - 2)) if kc >= 2
                     else prev_av.get(("B", qt - 1, 4 * qt - 1))], "sgB")
                prev_sg[("A", qt, kc)] = sgA
                prev_sg[("B", qt, kc)] = sgB
                depA, depB = sgA, sgB
                if kc >= 4 * qt:
                    rr = kc - 4 * qt
                    depA = p.op("v", lambda e, attA=attA, rr=rr: e.tensor_mul(
                        attA[:, :], attA[:, :], S["cmask"][:, rr, :]), [sgA], "mkA")
                    depB = p.op("v", lambda e, attB=attB, rr=rr: e.tensor_mul(
                        attB[:, :], attB[:, :], S["cmask"][:, rr, :]), [sgB], "mkB")
                avA = p.op("t", lambda e, kc=kc, avbank=avbank, attA=attA, nkc=nkc: e.matmul(
                    S[avbank][0:64, :], S["Vtm"][:, kc, 0:64], attA[:, :],
                    start=(kc == 0), stop=(kc == nkc - 1),
                    tile_position=(0, 0), skip_group_check=True),
                    [depA, vtm_ev[kc], evacs[-1] if evacs and kc == 0 else None],
                    "avA")
                avB = p.op("t", lambda e, kc=kc, avbank=avbank, attB=attB, nkc=nkc: e.matmul(
                    S[avbank][64:128, :], S["Vtm"][:, kc, 64:128], attB[:, :],
                    start=(kc == 0), stop=(kc == nkc - 1),
                    tile_position=(0, 64), skip_group_check=True),
                    [depB, vtm_ev[kc], avA], "avB")
                prev_av[("A", qt, kc)] = avA
                prev_av[("B", qt, kc)] = avB
                last_avB = avB
            ev = p.op("s", lambda e, hp=hp, qt=qt, avbank=avbank: e.copy(
                S["s_bf"][:, hp, ts(qt, 512)], S[avbank][:, :]), [last_avB], "sev")
            evacs.append(ev)
        hp_fence = list(evacs) + [st["sqk"], st["sv"]]
    st["attn_done"] = evacs
    p.free("attB1", "attB0", "attA1", "attA0", "Vtm", "Vbf", "Xb", "Qsb")
    p.free("cmask")


def ph3(p, st):
    S = p.sb
    dr = p.dr
    seg = p.seg
    AGN = p.AGN
    fence = st["attn_done"]

    p.alloc("HMAX", [128, T], F32)
    p.alloc("XQ", [128, DC, T], BF16)
    xq_ops = []
    for dc in range(DC):
        xq_ops.append(quant3_row(p, S["XQ"][:, dc, :], S["SA"][:, dc, :], S["SXB"][:, :],
                                 -128.0, 127.0,
                                 [st["sxb"], st["xn_done"][0], xq_ops[-1] if xq_ops
                                  else None] + fence, S["f1"][:, :]))
    xn_sp_dmas = []
    for dc in range(DC):
        xn_sp_dmas.append(p.dma("q", dap(dr["xn_sp"], dc * 128 * T, [[T, 128], [1, T]]),
                                S["SA"][:, dc, :], deps=[st["xn_done"][0]] + fence,
                                note="dma_xnsp"))
    st["xn_sp"] = xn_sp_dmas

    spill_ops = []
    hmax_ops = []
    sil_prev = {}
    mm_last_of_hc = {}
    prev_ev_ops = None
    for hc in range(HC0):
        r, jc = hc // 4, hc % 4
        wb = wts_sel(S, hc)
        wl = p.dma("q", wb[:, :, :],
                   dap(dr["ag_out"], r * AGN + seg["w1T"] + jc * 128,
                       [[512, 128], [128 * 512, 8], [1, 128]]),
                   deps=[st["ag"], mm_last_of_hc.get(hc - 2)], note="dma_w1t")
        mms = []
        new_ev_ops = []
        for tq in range(NTQ):
            bank = "pA" if tq % 2 == 0 else "pB"
            for dc in range(DC):
                mdep = [wl, xq_ops[dc]]
                if dc == 0:
                    if tq >= 2:
                        mdep.append(new_ev_ops[tq - 2])
                    elif prev_ev_ops is not None:
                        mdep.append(prev_ev_ops[tq + 2])
                m = p.op("t", lambda e, bank=bank, dc=dc, tq=tq, wb=wb: e.matmul(
                    S[bank][:, :], wb[:, dc, :], S["XQ"][:, dc, ts(tq, 512)],
                    start=(dc == 0), stop=(dc == DC - 1)), mdep, "mm1")
                mms.append(m)
            ev = p.op("v", lambda e, bank=bank, tq=tq: e.tensor_mul(
                S["f1"][:, ts(tq, 512)], S[bank][:, :], S["RSXW1B"][:, ts(tq, 512)]),
                [mms[-1], st["rsxw1b"]] + ([sil_prev.get(hc - 1)] if tq == 0 else []),
                "ev1")
            new_ev_ops.append(ev)
        prev_ev_ops = new_ev_ops
        mm_last_of_hc[hc] = mms[-1]
        c1 = p.op("s", lambda e, hc=hc: e.activation(
            S["f2"][:, :], S["f1"][:, :], AF.Copy, scale=S["cw1"][:, hc:hc + 1]),
            new_ev_ops + [sil_prev.get(hc - 1)], "cv1")
        c2 = p.op("v", lambda e, hc=hc: e.scalar_tensor_tensor(
            S["f2"][:, 1:T], S["f1"][:, 0:T - 1], S["cw0"][:, hc:hc + 1],
            S["f2"][:, 1:T], ALU.mult, ALU.add), [c1] + new_ev_ops, "cv0")
        c3 = p.op("v", lambda e, hc=hc: e.scalar_tensor_tensor(
            S["f2"][:, 0:T - 1], S["f1"][:, 1:T], S["cw2"][:, hc:hc + 1],
            S["f2"][:, 0:T - 1], ALU.mult, ALU.add), [c2], "cv2")
        stage = S["g1s"] if hc % 2 == 0 else S["g2s"]
        sil = p.op("s", lambda e, hc=hc, stage=stage: e.activation(
            stage[:, :], S["f2"][:, :], AF.Silu, bias=S["cb"][:, hc:hc + 1]),
            [c3, spill_ops[-2] if hc >= 2 else None,
             hmax_ops[-2] if hc >= 2 else None], "sil")
        sil_prev[hc] = sil
        if hc == 0:
            hm = p.op("g", lambda e, stage=stage: e.tensor_copy(
                S["HMAX"][:, :], stage[:, :]), [sil], "hm0")
        else:
            hm = p.op("v", lambda e, stage=stage: e.tensor_max(
                S["HMAX"][:, :], S["HMAX"][:, :], stage[:, :]), [sil, hmax_ops[-1]], "hm")
        hmax_ops.append(hm)
        spill_ops.append(p.dma("q", dap(dr["hsil_sp"], hc * 128 * T, [[T, 128], [1, T]]),
                               stage[:, :], deps=[sil], note="dma_hsp"))
    p.free("XQ")

    prev_rd = None
    for i in range(16):
        bank = "pT" if i % 2 == 0 else "pT2"
        tp = p.op("t", lambda e, i=i, bank=bank: e.transpose(
            S[bank][:, :], S["HMAX"][:, ts(i, 128)], S["identF"][:, :]),
            [hmax_ops[-1], prev_rd], "h2tp")
        rd = p.op("v", lambda e, i=i, bank=bank: e.reduce_max(
            S["stat"][:, 32 + i:33 + i], S[bank][:, :], AX), [tp], "h2rd")
        prev_rd = rd
    p.free("HMAX")
    p.free("RSXW1B", "SXB", "SA")

    p.alloc("MEAN1B", [128, T], F32)
    p.alloc("RSTD1B", [128, T], F32)
    p.alloc("S2B", [128, T], BF16)
    p.alloc("RS2W2B", [128, T], F32)
    p.alloc("MEAN0B", [128, T], F32)
    p.alloc("STD0B", [128, T], F32)
    p.alloc("XQ2", [128, HC0, TH], BF16)
    p.alloc("xnt", [128, 512], F32)
    sx1 = p.op("v", lambda e: e.tensor_scalar(S["stat"][:, 48:64], S["stat"][:, 32:48],
                                              1e-5, None, ALU.max), [prev_rd], "s2c")
    sx2 = p.op("v", lambda e: e.reciprocal(S["tiny"][:, 0:16], S["stat"][:, 48:64]),
               [sx1], "s2r")
    sx3 = p.op("v", lambda e: e.tensor_scalar(S["tinyb"][:, 0:16], S["tiny"][:, 0:16],
                                              127.0, None, ALU.mult), [sx2], "s2127")
    sx3b = p.op("v", lambda e: e.tensor_copy(S["tiny"][:, 0:16], S["tinyb"][:, 0:16]),
                [sx3], "s2bk")
    sx4 = p.op("v", lambda e: e.reciprocal(S["tiny2"][:, 0:16], S["tiny"][:, 0:16]),
               [sx3b], "rs2")
    sx5 = p.op("v", lambda e: e.tensor_scalar(S["tiny2"][:, 0:16], S["tiny2"][:, 0:16],
                                              S["scal"][:, S_RW2:S_RW2 + 1], None,
                                              ALU.mult), [sx4, st["sw"]], "rs2w")
    s2b = row_bcast(p, st, S["tiny"][:, 0:16], S["S2B"], [sx3b, prev_rd], "s2b")
    rs2b = row_bcast(p, st, S["tiny2"][:, 0:16], S["RS2W2B"], [sx5, prev_rd], "rs2b")
    sd1 = p.op("v", lambda e: e.reciprocal(S["tiny"][:, 0:16], S["stat"][:, 16:32]),
               [s2b, rs2b], "std0")
    m0b = row_bcast(p, st, S["stat"][:, 0:16], S["MEAN0B"], [sd1], "m0b")
    st0b = row_bcast(p, st, S["tiny"][:, 0:16], S["STD0B"], [sd1], "st0b")

    x1_dmas = []
    ln1_var_ops = []
    banks = ["pA", "pB"]
    for th in range(2):
        qz_ops = []
        for hc in range(HC0):
            stage = S["g1s"] if hc % 2 == 0 else S["g2s"]
            ld = p.dma("q", stage[:, 0:TH],
                       dap(dr["hsil_sp"], hc * 128 * T + th * TH, [[T, 128], [1, TH]]),
                       deps=[spill_ops[hc], s2b, qz_ops[-2] if hc >= 2 else None],
                       note="dma_hld")
            qz = quant3_row(p, S["XQ2"][:, hc, :], stage[:, 0:TH],
                            S["S2B"][:, ts(th, TH)], -128.0, 127.0, [ld],
                            S["f1"][:, (hc % 2) * TH:(hc % 2) * TH + TH],
                            chans=("v", "g", "g"))
            qz_ops.append(qz)
        wl_prev = {}
        for dc in range(DC):
            mms = {0: [], 1: []}
            wb = S["wt2"] if dc % 2 == 0 else S["wt3"]
            wl = p.dma("q", wb[:, :, :],
                       dap(dr["ag_out"], dc * AGN + seg["w2T"],
                           [[128, 128], [128 * 128, HC0], [1, 128]]),
                       deps=[st["ag"], qz_ops[-1], wl_prev.get(dc - 2)],
                       note="dma_w2t")
            wl_prev[dc] = None
            for hc in range(HC0):
                for tq2 in range(2):
                    m = p.op("t", lambda e, wb=wb, tq2=tq2, hc=hc: e.matmul(
                        S[banks[tq2]][:, :], wb[:, hc, :], S["XQ2"][:, hc, ts(tq2, 512)],
                        start=(hc == 0), stop=(hc == HC0 - 1)),
                        [wl] + ([x1_dmas[-1]] if hc == 0 and x1_dmas else []), "mm2")
                    mms[tq2].append(m)
            wl_prev[dc] = mms[1][-1]
            for tq2 in range(2):
                gt = th * 2 + tq2
                ev = p.op("v", lambda e, tq2=tq2, gt=gt: e.tensor_mul(
                    S["f1"][:, 0:512], S[banks[tq2]][:, :], S["RS2W2B"][:, ts(gt, 512)]),
                    [mms[tq2][-1], rs2b], "ev2")
                sil = p.op("s", lambda e: e.activation(
                    S["f2"][:, 0:512], S["f1"][:, 0:512], AF.Silu), [ev], "sil2")
                sg = p.op("v", lambda e, dc=dc, gt=gt: e.tensor_mul(
                    S["f2"][:, 0:512], S["f2"][:, 0:512], S["s_bf"][:, dc, ts(gt, 512)]),
                    [sil], "sg")
                xn_ld = p.dma("q", S["xnt"][:, :],
                              dap(dr["xn_sp"], dc * 128 * T + gt * 512,
                                  [[T, 128], [1, 512]]),
                              deps=[xn_sp_dmas[dc], m0b, st0b,
                                    x1_dmas[-1] if x1_dmas else None], note="dma_xnld")
                xr = p.op("v", lambda e, gt=gt: e.tensor_mul(
                    S["f3"][:, 0:512], S["xnt"][:, :], S["STD0B"][:, ts(gt, 512)]),
                    [xn_ld], "xstd")
                xr2 = p.op("v", lambda e, gt=gt: e.tensor_add(
                    S["f3"][:, 0:512], S["f3"][:, 0:512], S["MEAN0B"][:, ts(gt, 512)]),
                    [xr], "xmean")
                x1t = p.op("v", lambda e: e.tensor_add(
                    S["f3"][:, 0:512], S["f3"][:, 0:512], S["f2"][:, 0:512]),
                    [xr2, sg], "x1t")
                x1_dmas.append(p.dma("q", dap(dr["x1_sp"], dc * 128 * T + gt * 512,
                                              [[T, 128], [1, 512]]),
                                     S["f3"][:, 0:512], deps=[x1t], note="dma_x1sp"))
    st["x1_dmas"] = x1_dmas
    st["x1_idx"] = lambda dc, gt: (gt // 2) * 16 + dc * 2 + (gt % 2)
    p.free("xnt", "XQ2", "STD0B", "MEAN0B", "RS2W2B", "S2B")
    p.free("s_bf")


def ph4(p, st):
    S = p.sb
    dr = p.dr
    seg = p.seg
    AGN = p.AGN
    fence = st["x1_dmas"][-2:]

    p.alloc("xnt4", [128, 512], F32)
    ln1_var_ops = []
    prev = None
    for gt in range(NTQ):
        for dc in range(DC):
            ld = p.dma("q", S["xnt4"][:, :],
                       dap(dr["x1_sp"], dc * 128 * T + gt * 512, [[T, 128], [1, 512]]),
                       deps=[st["x1_dmas"][st["x1_idx"](dc, gt)], prev] +
                            (fence if dc == 0 and gt == 0 else []), note="dma_x1st")
            sqz = p.op("v", lambda e: e.tensor_mul(
                S["f2"][:, 0:512], S["xnt4"][:, :], S["xnt4"][:, :]), [ld], "sq1")
            mS1 = p.op("t", lambda e, dc=dc: e.matmul(
                S["pC"][:, :], S["ones"][:, :], S["xnt4"][:, :],
                start=(dc == 0), stop=(dc == DC - 1)),
                [ld, ln1_var_ops[-1] if dc == 0 and ln1_var_ops else None], "ms1")
            mS2 = p.op("t", lambda e, dc=dc: e.matmul(
                S["pD"][:, :], S["ones"][:, :], S["f2"][:, 0:512],
                start=(dc == 0), stop=(dc == DC - 1)), [sqz], "ms2")
            prev = sqz
        e1 = p.op("v", lambda e, gt=gt: e.tensor_scalar(
            S["MEAN1B"][:, ts(gt, 512)], S["pC"][:, :], 1.0 / D, None,
            ALU.mult), [mS1], "ln1m")
        e2 = p.op("v", lambda e: e.tensor_scalar(
            S["f1"][:, 0:512], S["pD"][:, :], 1.0 / D, None, ALU.mult),
            [mS2], "ln1e2")
        e3 = p.op("v", lambda e, gt=gt: e.tensor_mul(
            S["f2"][:, 0:512], S["MEAN1B"][:, ts(gt, 512)],
            S["MEAN1B"][:, ts(gt, 512)]), [e1, e2], "ln1m2")
        e4 = p.op("v", lambda e: e.tensor_sub(
            S["f1"][:, 0:512], S["f1"][:, 0:512], S["f2"][:, 0:512]), [e3], "ln1v")
        e5 = p.op("s", lambda e: e.activation(
            S["f1"][:, 0:512], S["f1"][:, 0:512], AF.Sqrt,
            bias=S["epsc"][:, 0:1]), [e4], "ln1sd")
        e6 = p.op("v", lambda e, gt=gt: e.reciprocal(
            S["RSTD1B"][:, ts(gt, 512)], S["f1"][:, 0:512]), [e5], "ln1rs")
        ln1_var_ops.append(e6)
        prev = e6
    st["ln1"] = ln1_var_ops

    mx_cols = []
    xn1_dmas = []
    prev = st["ln1"][-1]
    for dc in range(DC):
        for gt in range(NTQ):
            ld = p.dma("q", S["xnt4"][:, :],
                       dap(dr["x1_sp"], dc * 128 * T + gt * 512, [[T, 128], [1, 512]]),
                       deps=[st["x1_dmas"][st["x1_idx"](dc, gt)], prev], note="dma_x1ld")
            n1 = p.op("v", lambda e, gt=gt: e.tensor_sub(
                S["f1"][:, ts(gt, 512)], S["xnt4"][:, :], S["MEAN1B"][:, ts(gt, 512)]),
                [ld] + st["ln1"], "n1")
            prev = p.op("v", lambda e, gt=gt: e.tensor_mul(
                S["f1"][:, ts(gt, 512)], S["f1"][:, ts(gt, 512)],
                S["RSTD1B"][:, ts(gt, 512)]), [n1], "n2")
        mx_cols.append(p.op("v", lambda e, dc=dc: e.reduce_max(
            S["mxc"][:, dc:dc + 1], S["f1"][:, :], AX, apply_absolute_value=True),
            [prev], "m0mx"))
        d = p.dma("q", dap(dr["xn1_sp"], dc * 128 * T, [[T, 128], [1, T]]),
                  S["f1"][:, :], deps=[prev], note="dma_xn1")
        xn1_dmas.append(d)
        prev = d
    rs0 = global_max_ar(p, st, S["mxc"][:, 0:8], 2, S_S0, S_RS0, "m0", mx_cols)
    p.free("xnt4")
    p.free("RSTD1B", "MEAN1B")

    p.alloc("KT", [128, DC, T], BF16, side="right")
    p.alloc("VT", [128, DC, T], BF16, side="right")
    p.alloc("SRT", [128, DC, T], BF16, side="right")
    p.alloc("MIX", [128, DC, T], BF16, side="right")
    banks = ["pA", "pB", "pC", "pD"]
    st["proj_done"] = {}
    prev_q = None
    for mixn, wseg, dstn, sig in [("tmk", "kT", "KT", False), ("tmv", "vT", "VT", False),
                                  ("tmr", "rT", "SRT", True)]:
        mix_ops = []
        prev = prev_q
        for dc in range(DC):
            ld = p.dma("q", S["f1"][:, :],
                       dap(dr["xn1_sp"], dc * 128 * T, [[T, 128], [1, T]]),
                       deps=[xn1_dmas[dc], prev], note="dma_xm")
            dsub = p.op("v", lambda e: e.tensor_sub(S["f2"][:, 1:T], S["f1"][:, 1:T],
                                                    S["f1"][:, 0:T - 1]), [ld], "dmx")
            dcol = p.op("v", lambda e: e.tensor_copy(S["f2"][:, 0:1], S["f1"][:, 0:1]),
                        [ld], "dc0")
            o1 = p.op("v", lambda e, mixn=mixn, dc=dc: e.scalar_tensor_tensor(
                S["MIX"][:, dc, 1:T], S["f2"][:, 1:T], S[mixn][:, dc:dc + 1],
                S["f1"][:, 0:T - 1], ALU.mult, ALU.add), [dsub, dcol], "mx1")
            o2 = p.op("v", lambda e, mixn=mixn, dc=dc: e.tensor_scalar(
                S["MIX"][:, dc, 0:1], S["f2"][:, 0:1], S[mixn][:, dc:dc + 1], None,
                ALU.mult), [dcol], "mx0")
            mix_ops += [o1, o2]
            prev = o2
        ev_prev = {}
        evl = []
        last_mm_cc = {}
        for cc in range(DC):
            mms = {tq: None for tq in range(NTQ)}
            wb = wts_sel(S, cc)
            wl = p.dma("q", wb[:, :, :],
                       dap(dr["ag_out"], cc * AGN + seg[wseg],
                           [[128, 128], [128 * 128, DC], [1, 128]]),
                       deps=[st["ag"], last_mm_cc.get(cc - 2)] +
                            (mix_ops if cc == 0 else []), note="dma_wp")
            for dc in range(DC):
                for tq in range(NTQ):
                    m = p.op("t", lambda e, wb=wb, dc=dc, tq=tq: e.matmul(
                        S[banks[tq]][:, :], wb[:, dc, :], S["MIX"][:, dc, ts(tq, 512)],
                        start=(dc == 0), stop=(dc == DC - 1)),
                        [wl, ev_prev.get((cc - 1, tq))], "mmp")
                    mms[tq] = m
            last_mm_cc[cc] = mms[NTQ - 1]
            for tq in range(NTQ):
                if sig:
                    ev = p.op("s", lambda e, cc=cc, tq=tq, dstn=dstn: e.activation(
                        S[dstn][:, cc, ts(tq, 512)], S[banks[tq]][:, :], AF.Sigmoid),
                        [mms[tq]], "evsig")
                else:
                    ev = p.op("s", lambda e, cc=cc, tq=tq, dstn=dstn: e.copy(
                        S[dstn][:, cc, ts(tq, 512)], S[banks[tq]][:, :]), [mms[tq]], "evp")
                ev_prev[(cc, tq)] = ev
                evl.append(ev)
        st["proj_done"][dstn] = evl
        prev_q = evl[-1]
    p.free("MIX")

    p.alloc("XQ1", [128, DC, T], BF16, side="right")
    xq1_ops = []
    proj_fence = (st["proj_done"]["KT"][-1:] + st["proj_done"]["VT"][-1:]
                  + st["proj_done"]["SRT"][-1:])
    prev = None
    for dc in range(DC):
        ld = p.dma("q", S["f1"][:, :], dap(dr["xn1_sp"], dc * 128 * T, [[T, 128], [1, T]]),
                   deps=[xn1_dmas[dc], prev] + (proj_fence if dc == 0 else []),
                   note="dma_xq1")
        q = quant3_col(p, S["XQ1"][:, dc, :], S["f1"][:, :],
                       S["scal"][:, S_RS0:S_RS0 + 1], -32768.0, 32767.0,
                       [ld, rs0], S["f2"][:, :])
        xq1_ops.append(q)
        prev = q
    st["xq1"] = xq1_ops


def ph5(p, st):
    S = p.sb
    dr = p.dr
    seg = p.seg
    AGN = p.AGN

    TQS = T // 4
    p.alloc("f4", [128, T], F32)
    p.alloc("PP", [128, TQS + 1], F32)
    p.alloc("AAs", [128, TQS + 1], F32)
    p.alloc("BBs", [128, TQS + 1], F32)
    p.alloc("X4b", [128, TQS], F32)
    X1b = S["f1"][:, 0:TQS]
    X2b = S["f2"][:, 0:TQS]
    X3b = S["f3"][:, 0:TQS]

    upgate_spills = {"up": [], "gate": []}
    mx_ops = {"up": [], "gate": []}
    banksUG = {"up": ["pA", "pB"], "gate": ["pC", "pD"]}
    stages = {"up": "f4", "gate": "f4"}
    stages2 = {"up": "f4", "gate": "f4"}
    last_mm = {"up": None, "gate": None}
    for Ic in range(IC1):
        r, jc = Ic // 3, Ic % 3
        for nm, wsg, spnm in [("up", "upT", "up_sp"), ("gate", "gateT", "gate_sp")]:
            wb = wts_sel(S, 2 * Ic + (0 if nm == "up" else 1))
            wl = p.dma("q", wb[:, :, :],
                       dap(dr["ag_out"], r * AGN + seg[wsg] + jc * 128,
                           [[384, 128], [128 * 384, 8], [1, 128]]),
                       deps=[st["ag"], last_mm[nm]] + (st["xq1"] if Ic == 0 else []),
                       note="dma_wug")
            stage = S[stages[nm]] if Ic % 2 == 0 else S[stages2[nm]]
            evs = []
            mprev = None
            for tq in range(NTQ):
                bank = banksUG[nm][tq % 2]
                for dc in range(DC):
                    m = p.op("t", lambda e, wb=wb, bank=bank, dc=dc, tq=tq: e.matmul(
                        S[bank][:, :], wb[:, dc, :], S["XQ1"][:, dc, ts(tq, 512)],
                        start=(dc == 0), stop=(dc == DC - 1)),
                        [wl, evs[tq - 2] if tq >= 2 and dc == 0 else None], "mmug")
                    mprev = m
                lastsp = (upgate_spills["gate"] if nm == "up"
                          else upgate_spills["up"])
                ev = p.op("s", lambda e, bank=bank, tq=tq, stage=stage: e.activation(
                    stage[:, ts(tq, 512)], S[bank][:, :], AF.Copy,
                    scale=S["scal"][:, S_S0:S_S0 + 1]),
                    [mprev, st["xq1"][-1],
                     upgate_spills[nm][-1] if upgate_spills[nm] and tq == 0 else None,
                     lastsp[-1] if lastsp and tq == 0 else None],
                    "evug")
                evs.append(ev)
            last_mm[nm] = mprev
            mx_ops[nm].append(p.op("v", lambda e, stage=stage, nm=nm, Ic=Ic: e.reduce_max(
                S["mxc"][:, (8 if nm == "up" else 32) + Ic:
                         (9 if nm == "up" else 33) + Ic],
                stage[:, :], AX, apply_absolute_value=True), evs, "mxug"))
            upgate_spills[nm].append(p.dma("q", dap(dr[spnm], Ic * 128 * T,
                                                    [[T, 128], [1, T]]),
                                           stage[:, :], deps=evs, note="dma_ugsp"))
    st["upg_spills"] = upgate_spills

    c1 = p.op("v", lambda e: e.reduce_max(S["mxc"][:, 56:57], S["mxc"][:, 8:32], AX),
              mx_ops["up"] + [st["ar_last"]], "m1cl")
    c2 = p.op("v", lambda e: e.reduce_max(S["mxc"][:, 57:58], S["mxc"][:, 32:56], AX),
              mx_ops["gate"], "m2cl")
    tp = p.op("t", lambda e: e.transpose(S["pT2"][0:2, 0:128], S["mxc"][:, 56:58],
                                         S["identF"][:, :]), [c1, c2], "m12tp")
    rd1 = p.op("v", lambda e: e.reduce_max(S["mxc"][0:2, 58:59], S["pT2"][0:2, 0:128], AX),
               [tp], "m12rd")
    dcp = p.dma("gq", dr["ar_in"].ap()[3:5], dap(S["mxc"], 58, [[64, 2], [1, 1]]),
                deps=[rd1], note="dma_m12")
    ar12 = p.op("c", lambda e: e.collective_compute(
        "AllReduce", ALU.max, replica_groups=[list(range(NCORES))],
        ins=[dr["ar_in"].ap()[3:5].opt()], outs=[dr["ar_out"].ap()[3:5].opt()]),
        [dcp, st["ar_last"]], "AR_m12")
    st["ar_last"] = ar12
    lb = p.dma("gq", S["mxc"][0:1, 58:60], dr["ar_out"].ap()[3:5], deps=[ar12, dcp],
               note="dma_m12b")
    bc = bcast_p0(p, S["mxc"][0:1, 58:60], S["mxc"][:, 60:62], 2, [lb], "m12")
    q1 = p.op("v", lambda e: e.tensor_scalar(S["scal"][:, S_S1:S_S1 + 1],
                                             S["mxc"][:, 60:61], 1.0 / 32767.0, 1e-12,
                                             ALU.mult, ALU.max), [bc], "s1s")
    q2 = p.op("v", lambda e: e.reciprocal(S["scal"][:, S_RS1:S_RS1 + 1],
                                          S["scal"][:, S_S1:S_S1 + 1]), [q1], "rs1")
    q3 = p.op("v", lambda e: e.tensor_scalar(S["scal"][:, S_S2:S_S2 + 1],
                                             S["mxc"][:, 61:62], 1.0 / 32767.0, 1e-12,
                                             ALU.mult, ALU.max), [bc], "s2s")
    q4 = p.op("v", lambda e: e.reciprocal(S["scal"][:, S_RS2:S_RS2 + 1],
                                          S["scal"][:, S_S2:S_S2 + 1]), [q3], "rs2s")
    q5 = p.op("v", lambda e: e.tensor_scalar(S["scal"][:, S_S265536:S_S265536 + 1],
                                             S["scal"][:, S_S2:S_S2 + 1], 1.0 / 65536.0,
                                             None, ALU.mult), [q3], "s265")
    st["scales12"] = [q2, q4, q5]

    rwkv_done = []
    rwkv_pair = {}
    prev_half = None
    for cc in range(DC):
        for h in range(4):
            off = h * TQS
            kap = S["KT"][:, cc, off:off + TQS]
            vap = S["VT"][:, cc, off:off + TQS]
            deps0 = [st["proj_done"]["KT"][cc * NTQ + h],
                     st["proj_done"]["VT"][cc * NTQ + h],
                     st["proj_done"]["SRT"][-1],
                     st["wc"], st["nu"], prev_half]
            if h == 0:
                i1 = p.op("v", lambda e: e.memset(S["PP"][:, 0:1], -1e38), deps0, "ppi")
                i2 = p.op("v", lambda e: e.memset(S["AAs"][:, 0:1], 0.0), deps0, "aai")
                i3 = p.op("v", lambda e: e.memset(S["BBs"][:, 0:1], 0.0), deps0, "bbi")
            else:
                i1 = p.op("v", lambda e: e.tensor_copy(S["PP"][:, 0:1],
                                                       S["PP"][:, TQS:TQS + 1]), deps0, "ppc")
                i2 = p.op("v", lambda e: e.tensor_copy(S["AAs"][:, 0:1],
                                                       S["AAs"][:, TQS:TQS + 1]), deps0, "aac")
                i3 = p.op("v", lambda e: e.tensor_copy(S["BBs"][:, 0:1],
                                                       S["BBs"][:, TQS:TQS + 1]), deps0, "bbc")
            wbc = p.op("v", lambda e, cc=cc: e.tensor_copy(
                S["X4b"][:, :], dap(S["w_c"], cc, [[8, 128], [0, TQS]])), deps0, "wbc")
            scn = p.op("v", lambda e, kap=kap: e.tensor_tensor_scan(
                S["PP"][:, 1:], S["X4b"][:, :], kap, S["PP"][:, 0:1], ALU.add, ALU.max),
                [i1, i2, i3, wbc], "ppscan")
            t1 = p.op("v", lambda e, kap=kap: e.tensor_sub(X1b, kap, S["PP"][:, 1:]),
                      [scn], "t1")
            t2 = p.op("v", lambda e: e.tensor_sub(X2b, S["PP"][:, 0:TQS], S["PP"][:, 1:]),
                      [scn], "t2")
            e2 = p.op("s", lambda e: e.activation(X3b, X1b, AF.Exp), [t1], "e2")
            e1 = p.op("s", lambda e, cc=cc: e.activation(
                X2b, X2b, AF.Exp, bias=S["w_c"][:, cc:cc + 1]), [t2], "e1")
            za = p.op("v", lambda e, vap=vap: e.tensor_mul(S["X4b"][:, :], X3b, vap),
                      [e2, e1, scn], "za")
            sa = p.op("v", lambda e: e.tensor_tensor_scan(
                S["AAs"][:, 1:], X2b, S["X4b"][:, :], S["AAs"][:, 0:1],
                ALU.mult, ALU.add), [za], "aascan")
            sb_ = p.op("v", lambda e: e.tensor_tensor_scan(
                S["BBs"][:, 1:], X2b, X3b, S["BBs"][:, 0:1], ALU.mult, ALU.add),
                [sa], "bbscan")
            ra = p.op("v", lambda e, kap=kap: e.tensor_sub(X1b, S["PP"][:, 0:TQS], kap),
                      [sb_, t1], "ra")
            rb = p.op("v", lambda e, cc=cc: e.tensor_scalar(
                X1b, X1b, S["nu_c"][:, cc:cc + 1], CAP, ALU.add, ALU.min), [ra], "rb")
            rr = p.op("s", lambda e: e.activation(X1b, X1b, AF.Exp), [rb], "rr")
            nm1 = p.op("v", lambda e: e.tensor_mul(X2b, X1b, S["AAs"][:, 0:TQS]),
                       [rr], "nm1")
            nm2 = p.op("v", lambda e, vap=vap: e.tensor_add(X2b, X2b, vap), [nm1], "nm2")
            dn1 = p.op("v", lambda e: e.tensor_mul(X3b, X1b, S["BBs"][:, 0:TQS]),
                       [nm1], "dn1")
            dn2 = p.op("v", lambda e: e.tensor_scalar(X3b, X3b, 1.0, None, ALU.add),
                       [dn1], "dn2")
            rcp = p.op("v", lambda e: e.reciprocal(X3b, X3b), [dn2], "rcp")
            yy = p.op("v", lambda e: e.tensor_mul(X2b, X2b, X3b), [rcp, nm2], "yy")
            rw = p.op("v", lambda e, cc=cc, off=off: e.tensor_mul(
                S["KT"][:, cc, off:off + TQS], X2b, S["SRT"][:, cc, off:off + TQS]),
                [yy], "rwkv")
            rwkv_done.append(rw)
            rwkv_pair.setdefault(cc, []).append(rw)
            prev_half = rw
    st["rwkv"] = rwkv_done

    s1_dmas = []
    ev_prev = {}
    banks = ["pA", "pB", "pC", "pD"]
    last_mm_dcg = {}
    for dcg in range(DC):
        mms = {tq: None for tq in range(NTQ)}
        wb = wts_sel(S, dcg)
        wl = p.dma("q", wb[:, :, :],
                   dap(dr["ag_out"], dcg * AGN + seg["oT"],
                       [[128, 128], [128 * 128, DC], [1, 128]]),
                   deps=[st["ag"], last_mm_dcg.get(dcg - 2)] +
                        ([last_mm["up"], last_mm["gate"]] if dcg == 0 else []),
                   note="dma_wo")
        for cc in range(DC):
            for tq in range(NTQ):
                m = p.op("t", lambda e, wb=wb, cc=cc, tq=tq: e.matmul(
                    S[banks[tq]][:, :], wb[:, cc, :], S["KT"][:, cc, ts(tq, 512)],
                    start=(cc == 0), stop=(cc == DC - 1)),
                    [wl, ev_prev.get((dcg - 1, tq)), rwkv_pair[cc][-1]], "mmo")
                mms[tq] = m
        last_mm_dcg[dcg] = mms[NTQ - 1]
        for tq in range(NTQ):
            stage = S["f4"]
            ev = p.op("s", lambda e, tq=tq, stage=stage: e.copy(
                stage[:, ts(tq, 512)], S[banks[tq]][:, :]),
                [mms[tq], s1_dmas[-2] if len(s1_dmas) >= 2 else None,
                 last_mm["up"] if dcg == 0 and tq == 0 else None,
                 last_mm["gate"] if dcg == 0 and tq == 0 else None], "evo")
            ev_prev[(dcg, tq)] = ev
            s1_dmas.append(p.dma("q", dap(dr["s1_sp"], dcg * 128 * T + tq * 512,
                                          [[T, 128], [1, 512]]),
                                 stage[:, ts(tq, 512)], deps=[ev], note="dma_s1sp"))
    st["s1_dmas"] = s1_dmas
    p.free("X4b", "BBs", "AAs", "PP")
    p.free("XQ1", "SRT", "VT", "KT")
    p.alloc("f5", [128, T], F32)


def ph6(p, st):
    S = p.sb
    dr = p.dr
    seg = p.seg
    AGN = p.AGN

    g2_spills = []
    mx3 = []
    gq_l = []
    g2_l = []
    for Ic in range(IC1):
        stage = S["f4"] if Ic % 2 == 0 else S["f5"]
        ld = p.dma("q", stage[:, :], dap(dr["gate_sp"], Ic * 128 * T, [[T, 128], [1, T]]),
                   deps=[st["upg_spills"]["gate"][Ic],
                         gq_l[-2] if len(gq_l) >= 2 else None] + st["scales12"] +
                        (st["s1_dmas"][-1:] if Ic == 0 else []), note="dma_gld")
        gq = quant3_col(p, S["f1"][:, :], stage[:, :], S["scal"][:, S_RS2:S_RS2 + 1],
                        -32768.0, 32767.0, [ld, g2_l[-1] if g2_l else None],
                        S["f1"][:, :], chan3="g")
        gq_l.append(gq)
        sg = p.op("s", lambda e: e.activation(S["f2"][:, :], S["f1"][:, :], AF.Sigmoid,
                                              scale=S["scal"][:, S_S2:S_S2 + 1]),
                  [gq], "sigg")
        u1 = p.op("s", lambda e: e.activation(S["f2"][:, :], S["f2"][:, :], AF.Copy,
                                              scale=65536.0, bias=RND), [sg], "u16a")
        u1b = p.op("s", lambda e: e.activation(S["f2"][:, :], S["f2"][:, :], AF.Copy,
                                               bias=-RND), [u1], "u16ab")
        u2 = p.op("v", lambda e: e.tensor_scalar(S["f2"][:, :], S["f2"][:, :], 65535.0,
                                                 None, ALU.min), [u1b], "u16b")
        g2 = p.op("v", lambda e: e.scalar_tensor_tensor(
            S["f3"][:, :], S["f1"][:, :], S["scal"][:, S_S265536:S_S265536 + 1],
            S["f2"][:, :], ALU.mult, ALU.mult),
            [u2, gq, g2_spills[-1] if g2_spills else None], "g2")
        g2_l.append(g2)
        mx3.append(p.op("v", lambda e, Ic=Ic: e.reduce_max(
            S["mxc"][:, 8 + Ic:9 + Ic], S["f3"][:, :], AX,
            apply_absolute_value=True), [g2], "mx3"))
        sp = p.dma("q", dap(dr["gate2_sp"], Ic * 128 * T, [[T, 128], [1, T]]),
                   S["f3"][:, :], deps=[g2], note="dma_g2sp")
        g2_spills.append(sp)
    ar3 = global_max_ar(p, st, S["mxc"][:, 8:32], 5, S_S3, S_RS3, "m3", mx3)
    ss = p.op("v", lambda e: e.tensor_mul(S["scal"][:, S_S3S1:S_S3S1 + 1],
                                          S["scal"][:, S_S3:S_S3 + 1],
                                          S["scal"][:, S_S1:S_S1 + 1]),
              [ar3] + st["scales12"], "s3s1")

    o_spills = []
    mx4 = []
    q1_l = []
    q2_l = []
    oo_l = []
    for Ic in range(IC1):
        st1 = S["f4"]
        st2 = S["f5"]
        ld1 = p.dma("q", st1[:, :], dap(dr["gate2_sp"], Ic * 128 * T, [[T, 128], [1, T]]),
                    deps=[g2_spills[Ic], ss, q1_l[-1] if q1_l else None], note="dma_g2ld")
        ld2 = p.dma("q", st2[:, :], dap(dr["up_sp"], Ic * 128 * T, [[T, 128], [1, T]]),
                    deps=[st["upg_spills"]["up"][Ic], ss, q2_l[-1] if q2_l else None],
                    note="dma_upld")
        q1 = quant3_col(p, S["f1"][:, :], st1[:, :], S["scal"][:, S_RS3:S_RS3 + 1],
                        -32768.0, 32767.0, [ld1, oo_l[-1] if oo_l else None],
                        S["f1"][:, :])
        q2 = quant3_col(p, S["f2"][:, :], st2[:, :], S["scal"][:, S_RS1:S_RS1 + 1],
                        -32768.0, 32767.0, [ld2, oo_l[-1] if oo_l else None],
                        S["f2"][:, :])
        q1_l.append(q1)
        q2_l.append(q2)
        oo = p.op("v", lambda e: e.scalar_tensor_tensor(
            S["f3"][:, :], S["f1"][:, :], S["scal"][:, S_S3S1:S_S3S1 + 1],
            S["f2"][:, :], ALU.mult, ALU.mult),
            [q1, q2, o_spills[-1] if o_spills else None], "oprod")
        oo_l.append(oo)
        mx4.append(p.op("v", lambda e, Ic=Ic: e.reduce_max(
            S["mxc"][:, 8 + Ic:9 + Ic], S["f3"][:, :], AX,
            apply_absolute_value=True), [oo], "mx4"))
        sp = p.dma("q", dap(dr["o_sp"], Ic * 128 * T, [[T, 128], [1, T]]),
                   S["f3"][:, :], deps=[oo], note="dma_osp")
        o_spills.append(sp)
    ar4 = global_max_ar(p, st, S["mxc"][:, 8:32], 6, S_S4, S_RS4, "m4", mx4)

    p.alloc("OQ", [128, IC1, TH], BF16)
    out_dmas = []
    last_mmd = None
    last_x2 = None
    dn_last = {0: None, 1: None}
    banks = ["pA", "pB"]
    for th in range(2):
        oq_ops = []
        for Ic in range(IC1):
            stg = S["f4"][:, 0:TH] if Ic % 2 == 0 else S["f4"][:, TH:T]
            xdeps = [o_spills[Ic], ar4,
                     oq_ops[-2] if len(oq_ops) >= 2 else None]
            if Ic == 0:
                xdeps += st["s1_dmas"][-1:]
            if Ic < 2 and th == 1:
                xdeps.append(last_mmd)
            ld = p.dma("q", stg,
                       dap(dr["o_sp"], Ic * 128 * T + th * TH, [[T, 128], [1, TH]]),
                       deps=xdeps, note="dma_old")
            q = quant3_col(p, S["OQ"][:, Ic, :], stg,
                           S["scal"][:, S_RS4:S_RS4 + 1], -32768.0, 32767.0,
                           [ld, oq_ops[-1] if oq_ops else last_x2],
                           S["f1"][:, 0:TH])
            oq_ops.append(q)
        for dc in range(DC):
            mms = {0: [], 1: []}
            b = dc % 2
            wb = S["wt2"] if b == 0 else S["wt3"]
            wl = p.dma("q", wb[:, 0:IC1, :],
                       dap(dr["ag_out"], dc * AGN + seg["downT"],
                           [[128, 128], [128 * 128, IC1], [1, 128]]),
                       deps=[st["ag"], dn_last[b] if dn_last[b] is not None
                             else st["x1_dmas"][-1]], note="dma_wd")
            for Ic in range(IC1):
                for tq2 in range(2):
                    m = p.op("t", lambda e, wb=wb, tq2=tq2, Ic=Ic: e.matmul(
                        S[banks[tq2]][:, :], wb[:, Ic, :], S["OQ"][:, Ic, ts(tq2, 512)],
                        start=(Ic == 0), stop=(Ic == IC1 - 1)),
                        [wl, oq_ops[Ic]] + ([out_dmas[-1]] if Ic == 0 and out_dmas
                                            else []), "mmd")
                    mms[tq2].append(m)
                    last_mmd = m
            dn_last[b] = last_mmd
            for tq2 in range(2):
                gt = th * 2 + tq2
                sil = p.op("s", lambda e, tq2=tq2: e.activation(
                    S["f1"][:, 0:512], S[banks[tq2]][:, :], AF.Silu,
                    scale=S["scal"][:, S_S4:S_S4 + 1]), [mms[tq2][-1]], "sild")
                s1ld = p.dma("q", S["f3"][:, 0:512],
                             dap(dr["s1_sp"], dc * 128 * T + gt * 512,
                                 [[T, 128], [1, 512]]),
                             deps=[st["s1_dmas"][dc * NTQ + gt],
                                   out_dmas[-4] if len(out_dmas) >= 4 else None],
                             note="dma_s1ld")
                sgd = p.op("v", lambda e: e.tensor_mul(
                    S["f1"][:, 0:512], S["f1"][:, 0:512], S["f3"][:, 0:512]),
                    [sil, s1ld], "sgd")
                x1ld = p.dma("q", S["f2"][:, 0:512],
                             dap(dr["x1_sp"], dc * 128 * T + gt * 512,
                                 [[T, 128], [1, 512]]),
                             deps=[st["x1_dmas"][st["x1_idx"](dc, gt)], sgd], note="dma_x1f")
                x2 = p.op("v", lambda e: e.tensor_add(S["f2"][:, 0:512], S["f2"][:, 0:512],
                                                      S["f1"][:, 0:512]), [x1ld, sgd], "x2")
                last_x2 = x2
                for j in range(4):
                    bank = "pT" if j % 2 == 0 else "pT2"
                    tp = p.op("t", lambda e, j=j, bank=bank: e.transpose(
                        S[bank][:, :], S["f2"][:, ts(j, 128)], S["identF"][:, :]),
                        [x2, out_dmas[-2] if len(out_dmas) >= 2 else None], "tpo")
                    cp = p.op("v", lambda e, j=j, bank=bank: e.tensor_copy(
                        S["f3"][:, 512 + j * 128:512 + (j + 1) * 128], S[bank][:, :]),
                        [tp], "cpo")
                    out_dmas.append(p.dma("q",
                                          dr["out"].ap()[gt * 512 + j * 128:
                                                         gt * 512 + (j + 1) * 128,
                                                         ts(dc, 128)],
                                          S["f3"][:, 512 + j * 128:512 + (j + 1) * 128],
                                          deps=[cp], note="dma_out"))
    st["out_dmas"] = out_dmas
    p.free("OQ")


def host_inputs(inputs, core):
    def cols(a):
        return np.ascontiguousarray(np.asarray(a, dtype=np.float32).reshape(8, 128).T)
    cw = np.asarray(inputs["L0_ffn_conv_w"], dtype=np.float32)[:, 0, :]
    cwp = cw.reshape(32, 128, 3).transpose(1, 0, 2)
    def sl(a):
        a = np.asarray(a, dtype=np.float32)
        n = a.shape[0] // NCORES
        return np.ascontiguousarray(a[core * n:(core + 1) * n])
    return {
        "x": np.ascontiguousarray(np.asarray(inputs["L0_attn_x"], dtype=np.float32)[core]),
        "bvq_c": cols(np.asarray(inputs["L0_attn_bv_q"]).reshape(-1)),
        "bvk_c": cols(np.asarray(inputs["L0_attn_bv_k"]).reshape(-1)),
        "bvv_c": cols(np.asarray(inputs["L0_attn_bv_v"]).reshape(-1)),
        "cw0_c": np.ascontiguousarray(cwp[:, :, 0]),
        "cw1_c": np.ascontiguousarray(cwp[:, :, 1]),
        "cw2_c": np.ascontiguousarray(cwp[:, :, 2]),
        "cb_c": np.ascontiguousarray(
            np.asarray(inputs["L0_ffn_conv_b"], dtype=np.float32).reshape(32, 128).T),
        "td_c": cols(inputs["L1_attn_time_decay"]),
        "tf_c": cols(inputs["L1_attn_time_first"]),
        "tmk_c": cols(np.asarray(inputs["L1_attn_time_mix_k"]).reshape(-1)),
        "tmv_c": cols(np.asarray(inputs["L1_attn_time_mix_v"]).reshape(-1)),
        "tmr_c": cols(np.asarray(inputs["L1_attn_time_mix_r"]).reshape(-1)),
        "w1_sh": sl(inputs["L0_ffn_w1"]),
        "w2_sh": sl(inputs["L0_ffn_w2"]),
        "wup_sh": sl(inputs["L1_ffn_w_up"]),
        "wgate_sh": sl(inputs["L1_ffn_w_gate"]),
        "wdown_sh": sl(inputs["L1_ffn_w_down"]),
        "wk_sh": sl(inputs["L1_attn_Wk"]),
        "wv_sh": sl(inputs["L1_attn_Wv"]),
        "wr_sh": sl(inputs["L1_attn_Wr"]),
        "wo_sh": sl(inputs["L1_attn_Wo"]),
    }


def kernel(**inputs):
    nc, p = build(debug_stop=DEBUG_STOP, debug_dumps=DEBUG_DUMPS)
    in_maps = [host_inputs(inputs, c) for c in range(NCORES)]
    res = run_bass_kernel_spmd(nc, in_maps, core_ids=list(range(NCORES)))
    return np.stack([np.asarray(res.results[c]["out"]) for c in range(NCORES)], axis=0)


# version 18
# speedup vs baseline: 1.0007x; 1.0007x over previous
import contextlib
import numpy as np
import concourse.bass as bass
import concourse.mybir as mybir
from concourse.bass_utils import run_bass_kernel_spmd

F32 = mybir.dt.float32
BF16 = mybir.dt.bfloat16
ALU = mybir.AluOpType
AF = mybir.ActivationFunctionType
AX = mybir.AxisListType.X

T = 2048
TH = 1024
D = 1024
DC = 8
H0 = 4096
HC0 = 32
I1 = 3072
IC1 = 24
NTQ = 4
RND = 12582912.0
CAP = 60.0
NCORES = 8
EPS_LN = 1e-5

DEBUG_STOP = None
DEBUG_DUMPS = ()

CHANS = ("t", "v", "s", "g", "q", "gq", "aq", "c")
ENG_OF = {"t": "t", "v": "v", "s": "s", "g": "g", "q": "q", "gq": "g", "aq": "s", "c": "g"}
ASYNC = ("q", "gq", "aq", "c")
NSUB = {"q": 16, "gq": 8, "aq": 4}


class Op:
    __slots__ = ("chan", "fn", "deps", "seq", "sig", "note", "sub")

    def __init__(self, chan, fn, deps, note):
        self.chan = chan
        self.fn = fn
        self.deps = deps
        self.seq = None
        self.sig = False
        self.note = note
        self.sub = 0


def is_dma(chan):
    return chan in ("q", "gq", "aq")


class P:
    def __init__(self, nc):
        self.nc = nc
        self.ops = []
        self.open_cm = {}
        self.sb = {}
        self.dr = {}
        self.seg = {}

    def alloc(self, name, shape, dtype, psum=False, side="left"):
        if psum:
            cm = self.nc.psum_tensor(name, shape, dtype)
        else:
            cm = self.nc.sbuf_tensor(name, shape, dtype, side=side)
        h = cm.__enter__()
        self.open_cm[name] = cm
        self.sb[name] = h
        return h

    def free(self, *names):
        for name in names:
            self.open_cm.pop(name).__exit__(None, None, None)

    def op(self, chan, fn, deps=(), note=""):
        o = Op(chan, fn, [d for d in deps if d is not None], note)
        self.ops.append(o)
        return o

    def dma(self, chan, out, in_, deps=(), note="dma", transpose=False):
        assert is_dma(chan)
        def fn(e):
            return e.dma_start(out=out, in_=in_, transpose=transpose)
        return self.op(chan, fn, deps, note)

    def barrier_deps(self):
        last = {}
        for o in self.ops:
            last[o.chan] = o
        return list(last.values())

    def emit(self, sems):
        for o in self.ops:
            if is_dma(o.chan) or o.chan in ("c", "v", "s", "g"):
                o.sig = True
            for d in o.deps:
                d.sig = True
        counters = {c: 0 for c in CHANS}
        subcnt = {}
        for o in self.ops:
            if not o.sig:
                continue
            if is_dma(o.chan):
                idx = counters[o.chan]
                counters[o.chan] += 1
                ns = NSUB[o.chan]
                o.sub = idx % ns
                k = subcnt.get((o.chan, o.sub), 0) + 1
                subcnt[(o.chan, o.sub)] = k
                o.seq = 16 * k
            else:
                counters[o.chan] += 1
                o.seq = counters[o.chan]
        eng_ops = {e: [] for e in ("t", "v", "s", "g", "q")}
        for o in self.ops:
            eng_ops[ENG_OF[o.chan]].append(o)

        def semof(chan, sub):
            return sems[chan][sub] if is_dma(chan) else sems[chan]

        def emit_engine(e, handle):
            waited = {}
            for o in eng_ops[e]:
                if o.chan in ("v", "s", "g") and o.seq is not None and o.seq > 1:
                    key = (o.chan, 0)
                    if o.seq - 1 > waited.get(key, 0):
                        handle.wait_ge(semof(o.chan, 0), o.seq - 1)
                        waited[key] = o.seq - 1
                for d in o.deps:
                    key = (d.chan, d.sub)
                    if d.seq > waited.get(key, 0):
                        handle.wait_ge(semof(d.chan, d.sub), d.seq)
                        waited[key] = d.seq
                if o.sig and is_dma(o.chan) and o.seq > 16:
                    key = (o.chan, o.sub)
                    if o.seq - 16 > waited.get(key, 0):
                        handle.wait_ge(semof(o.chan, o.sub), o.seq - 16)
                        waited[key] = o.seq - 16
                ins = o.fn(handle)
                if o.sig:
                    assert ins is not None, o.note
                    ins.then_inc(semof(o.chan, o.sub), 16 if is_dma(o.chan) else 1)

        with self.nc.Block() as b:
            @b.tensor
            def _(t):
                emit_engine("t", t)

            @b.vector
            def _(v):
                emit_engine("v", v)

            @b.scalar
            def _(s):
                emit_engine("s", s)

            @b.gpsimd
            def _(g):
                emit_engine("g", g)

            @b.sync
            def _(q):
                emit_engine("q", q)


def dap(dr_tensor, offset, pattern):
    return bass.AP(dr_tensor, offset, pattern)


def ts(i, w):
    return slice(i * w, (i + 1) * w)


S_SW1, S_SW2, S_RW1, S_RW2 = 0, 1, 2, 3
S_RS0, S_S0, S_RS1, S_S1 = 4, 5, 6, 7
S_RS2, S_S2, S_RS3, S_S3 = 8, 9, 10, 11
S_RS4, S_S4, S_S265536, S_S3S1 = 12, 13, 14, 15


def build(debug_stop=None, debug_dumps=(), debug_dram=()):
    nc = bass.Bass("TRN2", target_bir_lowering=False)
    p = P(nc)
    dr = p.dr
    dr["x"] = nc.dram_tensor("x", [T, D], F32, kind="ExternalInput")
    dr["out"] = nc.dram_tensor("out", [T, D], F32, kind="ExternalOutput")
    for nm in ["bvq_c", "bvk_c", "bvv_c", "td_c", "tf_c", "tmk_c", "tmv_c", "tmr_c"]:
        dr[nm] = nc.dram_tensor(nm, [128, 8], F32, kind="ExternalInput")
    for nm in ["cw0_c", "cw1_c", "cw2_c", "cb_c"]:
        dr[nm] = nc.dram_tensor(nm, [128, 32], F32, kind="ExternalInput")
    dr["w1_sh"] = nc.dram_tensor("w1_sh", [512, 1024], F32, kind="ExternalInput")
    dr["w2_sh"] = nc.dram_tensor("w2_sh", [128, 4096], F32, kind="ExternalInput")
    dr["wup_sh"] = nc.dram_tensor("wup_sh", [384, 1024], F32, kind="ExternalInput")
    dr["wgate_sh"] = nc.dram_tensor("wgate_sh", [384, 1024], F32, kind="ExternalInput")
    dr["wdown_sh"] = nc.dram_tensor("wdown_sh", [128, 3072], F32, kind="ExternalInput")
    for nm in ["wk_sh", "wv_sh", "wr_sh", "wo_sh"]:
        dr[nm] = nc.dram_tensor(nm, [128, 1024], F32, kind="ExternalInput")

    seg = p.seg
    off = 0
    for nm, n in [("w1T", 1024 * 512), ("w2T", 4096 * 128), ("upT", 1024 * 384),
                  ("gateT", 1024 * 384), ("downT", 3072 * 128), ("kT", 1024 * 128),
                  ("vT", 1024 * 128), ("rT", 1024 * 128), ("oT", 1024 * 128)]:
        seg[nm] = off
        off += n
    p.AGN = off
    dr["ag_in"] = nc.dram_tensor("ag_in", [p.AGN], BF16)
    dr["ag_out"] = nc.dram_tensor("ag_out", [NCORES * p.AGN], BF16)
    dr["ar_in"] = nc.dram_tensor("ar_in", [8], F32)
    dr["ar_out"] = nc.dram_tensor("ar_out", [8], F32)
    dr["hsil_sp"] = nc.dram_tensor("hsil_sp", [H0 * T], BF16)
    dr["up_sp"] = nc.dram_tensor("up_sp", [I1 * T], F32)
    dr["gate_sp"] = nc.dram_tensor("gate_sp", [I1 * T], F32)
    dr["gate2_sp"] = nc.dram_tensor("gate2_sp", [I1 * T], F32)
    dr["o_sp"] = nc.dram_tensor("o_sp", [I1 * T], F32)
    dr["x1_sp"] = nc.dram_tensor("x1_sp", [D * T], F32)
    dr["xn_sp"] = nc.dram_tensor("xn_sp", [D * T], F32)
    dr["xn1_sp"] = nc.dram_tensor("xn1_sp", [D * T], F32)
    dr["s1_sp"] = nc.dram_tensor("s1_sp", [D * T], F32)

    p.alloc("identF", [128, 128], F32)
    p.alloc("identB", [128, 128], BF16)
    p.alloc("ones", [128, 128], F32)
    for nm in ["sqk_c", "sv_c", "w_c", "nu_c", "tmk", "tmv", "tmr"]:
        p.alloc(nm, [128, 8], F32)
    for nm in ["cw0", "cw1", "cw2", "cb"]:
        p.alloc(nm, [128, 32], F32)
    p.alloc("scal", [128, 16], F32)
    p.alloc("epsc", [128, 1], F32)
    p.alloc("rndc", [128, 2], F32)
    p.alloc("stat", [128, 64], F32)
    p.alloc("tiny", [128, 16], F32)
    p.alloc("tiny2", [128, 16], F32)
    p.alloc("tinyb", [128, 16], BF16)
    p.alloc("mxc", [128, 64], F32)
    p.alloc("flatrow", [1, 512], F32)
    p.alloc("flat16", [16, 128], F32)
    p.alloc("f1", [128, T], F32)
    p.alloc("f2", [128, T], F32)
    p.alloc("f3", [128, T], F32)
    p.alloc("g1s", [128, T], BF16)
    p.alloc("g2s", [128, T], BF16)
    p.alloc("wt0", [128, DC, 128], BF16)
    p.alloc("wt1", [128, DC, 128], BF16)
    p.alloc("wt2", [128, HC0, 128], BF16, side="right")
    p.alloc("wt3", [128, HC0, 128], BF16, side="right")
    p.alloc("pA", [128, 512], F32, psum=True)
    p.alloc("pB", [128, 512], F32, psum=True)
    p.alloc("pC", [128, 512], F32, psum=True)
    p.alloc("pD", [128, 512], F32, psum=True)
    p.alloc("pE", [128, 512], F32, psum=True)
    p.alloc("pT", [128, 128], F32, psum=True)
    p.alloc("pT2", [128, 128], F32, psum=True)
    p.alloc("pTb", [128, 128], BF16, psum=True)

    st = {}
    ph0(p, st)
    ph1(p, st)
    if debug_stop != "p1":
        ph2(p, st)
        if debug_stop != "p2":
            ph3(p, st)
            if debug_stop != "p3":
                ph4(p, st)
                if debug_stop != "p4":
                    ph5(p, st)
                    if debug_stop != "p5":
                        ph6(p, st)

    if debug_dram:
        bd = p.barrier_deps()
        dmp = []
        for nm in debug_dram:
            t = dr[nm]
            n = int(np.prod(t.shape))
            dt = nc.dram_tensor("ddump_" + nm, [n], t.dtype, kind="ExternalOutput")
            dmp.append(p.dma("q", dt.ap(), t.ap(), deps=bd, note="dma_ddump"))
        p.op("g", lambda e: e.nop(), deps=dmp, note="ddump_wait")
    if debug_dumps:
        bd = p.barrier_deps()
        dmp = []
        for nm in debug_dumps:
            t = p.sb[nm]
            shape = list(t.shape)
            flat = [shape[0], int(np.prod(shape[1:]))]
            dt = nc.dram_tensor("dump_" + nm, flat, t.dtype, kind="ExternalOutput")
            ap = t[:, :] if len(shape) == 2 else t.ap().rearrange("p a b -> p (a b)")
            dmp.append(p.dma("q", dt.ap(), ap, deps=bd, note="dma_dump"))
        p.op("g", lambda e: e.nop(), deps=dmp, note="dump_wait")

    if "out_dmas" in st:
        p.op("g", lambda e: e.nop(), deps=st["out_dmas"][-64:], note="final_wait")

    stk = contextlib.ExitStack()
    sems = {}
    for c in CHANS:
        if c in NSUB:
            sems[c] = [stk.enter_context(nc.semaphore(f"sem_{c}{i}"))
                       for i in range(NSUB[c])]
        else:
            sems[c] = stk.enter_context(nc.semaphore("sem_" + c))
    p.emit(sems)
    for name in reversed(list(p.open_cm)):
        p.open_cm[name].__exit__(None, None, None)
    p.open_cm.clear()
    stk.close()
    return nc, p


def quant3_col_g(p, out_ap, in_ap, scale_col, lo, hi, deps, tmp):
    return quant3_col(p, out_ap, in_ap, scale_col, lo, hi, deps, tmp)


def quant3_col(p, out_ap, in_ap, scale_col, lo, hi, deps, tmp, chan3="v"):
    S = p.sb
    o1 = p.op("s", lambda e: e.activation(tmp, in_ap, AF.Copy, scale=scale_col,
                                          bias=RND), deps, "q3a")
    o2 = p.op("s", lambda e: e.activation(tmp, tmp, AF.Copy, bias=-RND), [o1], "q3b")
    return p.op("v", lambda e: e.tensor_scalar(out_ap, tmp, hi, lo, ALU.min, ALU.max),
                [o2], "q3c")


def quant3_row(p, out_ap, in_ap, row_ap, lo, hi, deps, tmp, chans=None):
    S = p.sb
    o1 = p.op("v", lambda e: e.tensor_mul(tmp, in_ap, row_ap), deps, "qra")
    o2 = p.op("s", lambda e: e.activation(tmp, tmp, AF.Copy, bias=RND), [o1], "qrb1")
    o3 = p.op("s", lambda e: e.activation(tmp, tmp, AF.Copy, bias=-RND), [o2], "qrb2")
    return p.op("v", lambda e: e.tensor_scalar(out_ap, tmp, hi, lo, ALU.min, ALU.max),
                [o3], "qrc")


def bcast_p0(p, src_row, dst, n, deps, tag):
    S = p.sb
    mm = p.op("t", lambda e: e.matmul(S["pT2"][:, 0:n], S["ones"][0:1, :], src_row,
                                      start=True, stop=True), deps, "bc_mm_" + tag)
    return p.op("v", lambda e: e.tensor_copy(dst, S["pT2"][:, 0:n]), [mm], "bc_cp_" + tag)


def row_bcast(p, st, cols16, dst, deps, tag):
    S = p.sb
    guard = getattr(p, "rb_guard", [])
    tp = p.op("t", lambda e: e.transpose(S["pT2"][0:16, 0:128], cols16, S["identF"][:, :]),
              list(deps) + [st["consts"]] + guard, "rb_tp_" + tag)
    cp = p.op("v", lambda e: e.tensor_copy(S["flat16"][0:16, :], S["pT2"][0:16, 0:128]),
              [tp], "rb_cp_" + tag)
    last = None
    allfls = []
    for tq in range(NTQ):
        fls = []
        for r in range(4):
            i = tq * 4 + r
            fls.append(p.dma("gq", S["flatrow"][0:1, ts(r, 128)],
                             S["flat16"][i:i + 1, :], deps=[cp, last], note="dma_flat"))
        mm = p.op("t", lambda e: e.matmul(S["pE"][:, :], S["ones"][0:1, :],
                                          S["flatrow"][0:1, :], start=True, stop=True),
                  fls + [last], "rb_mm_" + tag)
        last = p.op("v", lambda e, tq=tq: e.tensor_copy(dst[:, ts(tq, 512)], S["pE"][:, :]),
                    [mm], "rb_ev_" + tag)
        allfls += fls
    p.rb_guard = allfls
    return last


def global_max_ar(p, st, cols_ap, slot, s_idx, rs_idx, tag, deps):
    S = p.sb
    dr = p.dr
    cl = p.op("v", lambda e: e.reduce_max(S["mxc"][:, 62:63], cols_ap, AX),
              list(deps) + [st["ar_last"]], "cl_" + tag)
    tp = p.op("t", lambda e: e.transpose(S["pT2"][0:1, 0:128], S["mxc"][:, 62:63],
                                         S["identF"][:, :]), [cl], "tp_" + tag)
    rd = p.op("v", lambda e: e.reduce_max(S["mxc"][0:1, 63:64], S["pT2"][0:1, 0:128], AX),
              [tp], "rd_" + tag)
    d1 = p.dma("gq", dr["ar_in"].ap()[slot:slot + 1], S["mxc"][0:1, 63:64], deps=[rd],
               note="dma_" + tag)
    ar = p.op("c", lambda e: e.collective_compute(
        "AllReduce", ALU.max, replica_groups=[list(range(NCORES))],
        ins=[dr["ar_in"].ap()[slot:slot + 1].opt()],
        outs=[dr["ar_out"].ap()[slot:slot + 1].opt()]), [d1, st["ar_last"]], "AR_" + tag)
    st["ar_last"] = ar
    lb = p.dma("gq", S["mxc"][0:1, 62:63], dr["ar_out"].ap()[slot:slot + 1],
               deps=[ar, tp], note="dma_b" + tag)
    bc = bcast_p0(p, S["mxc"][0:1, 62:63], S["scal"][:, s_idx:s_idx + 1], 1, [lb], tag)
    sc1 = p.op("v", lambda e: e.tensor_scalar(S["scal"][:, s_idx:s_idx + 1],
                                              S["scal"][:, s_idx:s_idx + 1],
                                              1.0 / 32767.0, 1e-12,
                                              ALU.mult, ALU.max), [bc], "s_" + tag)
    return p.op("v", lambda e: e.reciprocal(S["scal"][:, rs_idx:rs_idx + 1],
                                            S["scal"][:, s_idx:s_idx + 1]), [sc1],
                "rs_" + tag)


def wts_sel(S, i):
    return S["wt0"] if i % 2 == 0 else S["wt1"]


def ph0(p, st):
    S = p.sb
    dr = p.dr
    x = dr["x"].ap()

    p.alloc("SA", [128, DC, T], F32)
    p.alloc("XT", [128, 16, 1024], F32)
    p.alloc("xn_t0", [128, 1024], F32)
    p.alloc("xn_t1", [128, 1024], F32)
    p.alloc("sqd", [128, 1024], F32)

    prm = {}
    for srcn, dstn in [("bvq_c", "sqk_c"), ("bvv_c", "sv_c"), ("cw0_c", "cw0"),
                       ("cw1_c", "cw1"), ("cw2_c", "cw2"), ("cb_c", "cb"),
                       ("td_c", "w_c"), ("tf_c", "nu_c"), ("tmk_c", "tmk"),
                       ("tmv_c", "tmv"), ("tmr_c", "tmr")]:
        prm[dstn] = p.dma("gq", S[dstn][:, :], dr[srcn].ap(), note="dma_prm")
    bvk = None

    cz1 = p.op("g", lambda e: e.memset(S["identF"][:, :], 0.0), (), "czF")
    ca1 = p.op("g", lambda e: e.affine_select(
        S["identF"][:, :], S["identF"][:, :], [[-1, 128]], ALU.not_equal, 1.0,
        base=0, channel_multiplier=1), [cz1], "caF")
    cz2 = p.op("g", lambda e: e.memset(S["identB"][:, :], 0.0), (), "czB")
    ca2 = p.op("g", lambda e: e.affine_select(
        S["identB"][:, :], S["identB"][:, :], [[-1, 128]], ALU.not_equal, 1.0,
        base=0, channel_multiplier=1), [cz2], "caB")
    cz3 = p.op("g", lambda e: e.memset(S["ones"][:, :], 1.0), (), "czO")
    cz4 = p.op("g", lambda e: e.memset(S["epsc"][:, :], EPS_LN), (), "czE")
    czr = p.op("g", lambda e: e.memset(S["rndc"][:, 0:1], RND), (), "czR1")
    czr2 = p.op("g", lambda e: e.memset(S["rndc"][:, 1:2], -RND), (), "czR2")
    cz5 = p.op("g", lambda e: e.memset(S["scal"][:, :], 0.0), (), "czS")
    cz6 = p.op("g", lambda e: e.memset(S["mxc"][:, :], 0.0), (), "czM")
    cz7 = p.op("g", lambda e: e.memset(S["stat"][:, :], 0.0), (), "czT")
    cz8 = p.op("g", lambda e: e.memset(S["tiny"][:, :], 0.0), (), "czU")
    cz9 = p.op("g", lambda e: e.memset(S["tiny2"][:, :], 0.0), (), "czV")
    consts = p.op("g", lambda e: e.engine_nop(),
                  [ca1, ca2, cz3, cz4, cz5, cz6, cz7, cz8, cz9, czr, czr2], "consts")
    st["consts"] = consts
    bvk = p.dma("gq", S["tiny2"][:, 0:8], dr["bvk_c"].ap(), deps=[cz9], note="dma_prm")

    s1 = p.op("s", lambda e: e.sign(S["sqk_c"][:, :], S["sqk_c"][:, :]), [prm["sqk_c"]], "sgq")
    s2 = p.op("s", lambda e: e.sign(S["tiny2"][:, 0:8], S["tiny2"][:, 0:8]), [bvk], "sgk")
    s3 = p.op("s", lambda e: e.sign(S["sv_c"][:, :], S["sv_c"][:, :]), [prm["sv_c"]], "sgv")
    st["sqk"] = p.op("v", lambda e: e.tensor_mul(S["sqk_c"][:, :], S["sqk_c"][:, :],
                                                 S["tiny2"][:, 0:8]), [s1, s2], "sqkm")
    st["sv"] = s3
    w1e = p.op("s", lambda e: e.activation(S["w_c"][:, :], S["w_c"][:, :], AF.Exp),
               [prm["w_c"]], "wexp")
    st["wc"] = p.op("v", lambda e: e.tensor_scalar(S["w_c"][:, :], S["w_c"][:, :], -1.0,
                                                   None, ALU.mult), [w1e], "wneg")
    st["nu"] = p.op("v", lambda e: e.tensor_scalar(S["nu_c"][:, :], S["nu_c"][:, :], -1.0,
                                                   None, ALU.mult), [prm["nu_c"]], "nuneg")

    loads, sq_ops, sm_ops = [], [], []
    for i in range(16):
        ld = p.dma("q", S["XT"][:, i, :], x[ts(i, 128), :], note="dma_x")
        loads.append(ld)
        sq1 = p.op("v", lambda e, i=i: e.tensor_mul(
            S["sqd"][:, :], S["XT"][:, i, :], S["XT"][:, i, :]), [ld, consts], "sq1")
        sq_ops.append(p.op("v", lambda e, i=i: e.reduce_sum(
            S["stat"][:, 16 + i:17 + i], S["sqd"][:, :], AX), [sq1], "sq2"))
        sm_ops.append(p.op("v", lambda e, i=i: e.reduce_sum(
            S["stat"][:, i:i + 1], S["XT"][:, i, :], AX), [ld, consts], "sm"))
    f1_ = p.op("v", lambda e: e.tensor_scalar(S["stat"][:, 0:16], S["stat"][:, 0:16],
                                              1.0 / D, None, ALU.mult), sm_ops, "mean")
    f2_ = p.op("v", lambda e: e.tensor_scalar(S["stat"][:, 16:32], S["stat"][:, 16:32],
                                              1.0 / D, None, ALU.mult), sq_ops, "ex2")
    f3_ = p.op("v", lambda e: e.tensor_mul(S["tiny"][:, 0:16], S["stat"][:, 0:16],
                                           S["stat"][:, 0:16]), [f1_], "m2")
    f4_ = p.op("v", lambda e: e.tensor_sub(S["stat"][:, 16:32], S["stat"][:, 16:32],
                                           S["tiny"][:, 0:16]), [f2_, f3_], "var")
    f5_ = p.op("s", lambda e: e.activation(S["stat"][:, 16:32], S["stat"][:, 16:32],
                                           AF.Sqrt, bias=S["epsc"][:, 0:1]),
               [f4_, st["consts"]], "sd")
    f6_ = p.op("v", lambda e: e.reciprocal(S["stat"][:, 16:32], S["stat"][:, 16:32]),
               [f5_], "rstd")
    nmean = p.op("v", lambda e: e.tensor_scalar(S["tiny"][:, 0:16], S["stat"][:, 0:16],
                                                -1.0, None, ALU.mult), [f6_], "nmean")

    prev_cp = {}
    last_mx = None
    for i in range(16):
        buf = S["xn_t0"] if i % 2 == 0 else S["xn_t1"]
        nrm = p.op("v", lambda e, i=i, buf=buf: e.tensor_scalar(
            buf[:, :], S["XT"][:, i, :], S["tiny"][:, i:i + 1],
            S["stat"][:, 16 + i:17 + i], ALU.add, ALU.mult),
            [nmean, prev_cp.get((i - 2, DC - 1))], "nrm")
        last_mx = p.op("v", lambda e, i=i, buf=buf: e.reduce_max(
            S["stat"][:, 32 + i:33 + i], buf[:, :], AX, apply_absolute_value=True),
            [nrm], "sxmax")
        for dc in range(DC):
            bank = "pT" if (i * DC + dc) % 2 == 0 else "pT2"
            tp = p.op("t", lambda e, buf=buf, dc=dc, bank=bank: e.transpose(
                S[bank][:, :], buf[:, ts(dc, 128)], S["identF"][:, :]),
                [nrm, consts, prev_cp.get((i, dc - 2) if dc >= 2 else (i - 1, DC - 2 + dc))],
                "tpxn")
            cp = p.op("s", lambda e, i=i, dc=dc, bank=bank: e.copy(
                S["SA"][:, dc, ts(i, 128)], S[bank][:, :]), [tp], "cpxn")
            prev_cp[(i, dc)] = cp
    st["xn_done"] = [prev_cp[(15, DC - 1)], prev_cp[(15, DC - 2)], last_mx]
    st["sx_done"] = last_mx
    st["xt_free"] = [nrm, last_mx, sq_ops[-1], sm_ops[-1]]
    p.free("sqd", "xn_t1", "xn_t0", "XT")


def ph1(p, st):
    S = p.sb
    dr = p.dr
    seg = p.seg
    AGN = p.AGN

    p.alloc("SXB", [128, T], BF16)
    p.alloc("RSXW1B", [128, T], F32)
    p.alloc("ws0", [128, 4096], F32)
    p.alloc("ws2", [128, 4096], F32)
    p.alloc("wq0", [128, 4096], BF16)
    p.alloc("wqt", [128, 32, 128], BF16)
    p.alloc("lt", [128, 96], F32)
    p.alloc("lt2", [128, 96], F32)

    ag_parts = []
    ld_w1 = [p.dma("q", S["ws0"][:, ts(j, 1024)], dr["w1_sh"].ap()[ts(j, 128), :],
                   deps=st["xt_free"], note="dma_w1s") for j in range(4)]
    ld_w2 = p.dma("q", S["ws2"][:, :], dr["w2_sh"].ap()[:, :], deps=st["xt_free"],
                  note="dma_w2s")
    a1 = p.op("v", lambda e: e.reduce_sum(S["tiny"][:, 0:1], S["ws0"][:, :], AX,
                                          apply_absolute_value=True), ld_w1, "absw1")
    a2 = p.op("v", lambda e: e.reduce_sum(S["tiny"][:, 1:2], S["ws2"][:, :], AX,
                                          apply_absolute_value=True), [ld_w2], "absw2")
    mm = p.op("t", lambda e: e.matmul(S["pT2"][:, 0:2], S["ones"][:, :], S["tiny"][:, 0:2],
                                      start=True, stop=True),
              [a1, a2, st["consts"]] + st["xn_done"], "summ")
    cps = p.op("v", lambda e: e.tensor_copy(S["mxc"][:, 56:58], S["pT2"][:, 0:2]), [mm], "sumc")
    d1 = p.dma("gq", dr["ar_in"].ap()[0:2], S["mxc"][0:1, 56:58], deps=[cps], note="dma_ar")
    ar = p.op("c", lambda e: e.collective_compute(
        "AllReduce", ALU.add, replica_groups=[list(range(NCORES))],
        ins=[dr["ar_in"].ap()[0:2].opt()], outs=[dr["ar_out"].ap()[0:2].opt()]),
        [d1], "AR_means")
    st["ar_last"] = ar
    lb = p.dma("gq", S["mxc"][0:1, 58:60], dr["ar_out"].ap()[0:2], deps=[ar], note="dma_arb")
    bc = bcast_p0(p, S["mxc"][0:1, 58:60], S["scal"][:, S_RW1:S_RW1 + 2], 2, [lb], "wm")
    c1 = p.op("v", lambda e: e.tensor_scalar(S["scal"][:, S_RW1:S_RW1 + 2],
                                             S["scal"][:, S_RW1:S_RW1 + 2],
                                             1.0 / (4096.0 * 1024.0),
                                             1e-5, ALU.mult, ALU.max), [bc], "wmc")
    sw = p.op("v", lambda e: e.reciprocal(S["scal"][:, S_SW1:S_SW1 + 2],
                                          S["scal"][:, S_RW1:S_RW1 + 2]), [c1], "wmr")
    st["sw"] = sw

    prev_ag = None
    for j in range(4):
        qd = quant3_col(p, S["wq0"][:, ts(j, 1024)], S["ws0"][:, ts(j, 1024)],
                        S["scal"][:, S_SW1:S_SW1 + 1], -1.0, 1.0, [sw, ld_w1[j]],
                        S["ws0"][:, ts(j, 1024)])
        tps = [p.dma("q", S["wqt"][:, dc, :],
                     S["wq0"][:, j * 1024 + dc * 128:j * 1024 + (dc + 1) * 128],
                     deps=[qd, prev_ag], note="dma_tp", transpose=True) for dc in range(8)]
        prev_ag = p.dma("gq", dap(dr["ag_in"], seg["w1T"] + j * 128,
                                  [[512, 128], [128 * 512, 8], [1, 128]]),
                        S["wqt"][:, 0:8, :], deps=tps, note="dma_agw")
        ag_parts.append(prev_ag)

    qd2 = quant3_col(p, S["wq0"][:, :], S["ws2"][:, :], S["scal"][:, S_SW2:S_SW2 + 1],
                     -1.0, 1.0, [sw, ld_w2, prev_ag], S["ws2"][:, :])
    tps = [p.dma("q", S["wqt"][:, hc, :], S["wq0"][:, ts(hc, 128)],
                 deps=[qd2, prev_ag], note="dma_tp", transpose=True) for hc in range(32)]
    prev_ag = p.dma("gq", dap(dr["ag_in"], seg["w2T"],
                              [[128, 128], [128 * 128, 32], [1, 128]]),
                    S["wqt"][:, 0:32, :], deps=tps, note="dma_agw2")
    ag_parts.append(prev_ag)

    def lpbq(src, rows, width, segnm, prev_ag, prev_m4):
        ntile = rows // 128
        nblk = width // 32
        nw = width // 128
        outs = []
        for j in range(ntile):
            ld = p.dma("q", S["ws0"][:, 0:width], src.ap()[ts(j, 128), :],
                       deps=[prev_m4], note="dma_lw")
            w3 = S["ws0"][:, 0:width].rearrange("p (b i) -> p b i", i=32)
            bm = p.op("v", lambda e, w3=w3, nblk=nblk: e.reduce_max(
                S["lt"][:, 0:nblk], w3, AX, apply_absolute_value=True), [ld], "lpbm")
            sc = p.op("v", lambda e, nblk=nblk: e.tensor_scalar(
                S["lt"][:, 0:nblk], S["lt"][:, 0:nblk], 1.0 / 7.0, 1e-12,
                ALU.mult, ALU.max), [bm], "lps")
            rc = p.op("v", lambda e, nblk=nblk: e.reciprocal(
                S["lt2"][:, 0:nblk], S["lt"][:, 0:nblk]), [sc], "lprs")
            rs_exp = dap(S["lt2"], 0, [[96, 128], [1, nblk], [0, 32]])
            s_exp = dap(S["lt"], 0, [[96, 128], [1, nblk], [0, 32]])
            m1 = p.op("v", lambda e, w3=w3, rs_exp=rs_exp: e.tensor_tensor(
                w3, w3, rs_exp, ALU.mult), [rc], "lpm1")
            m2 = p.op("v", lambda e, width=width: e.tensor_scalar(
                S["ws0"][:, 0:width], S["ws0"][:, 0:width], RND, -RND, ALU.add, ALU.add),
                [m1], "lprnd")
            m3 = p.op("v", lambda e, width=width: e.tensor_scalar(
                S["ws0"][:, 0:width], S["ws0"][:, 0:width], 7.0, -8.0, ALU.min, ALU.max),
                [m2], "lpclip")
            wq3 = S["wq0"][:, 0:width].rearrange("p (b i) -> p b i", i=32)
            m4 = p.op("v", lambda e, wq3=wq3, w3=w3, s_exp=s_exp: e.tensor_tensor(
                wq3, w3, s_exp, ALU.mult), [m3, prev_ag], "lpdeq")
            tps = [p.dma("q", S["wqt"][:, dc, :], S["wq0"][:, ts(dc, 128)],
                         deps=[m4, prev_ag], note="dma_tp", transpose=True)
                   for dc in range(nw)]
            prev_ag = p.dma("gq", dap(dr["ag_in"], seg[segnm] + j * 128,
                                      [[rows, 128], [128 * rows, nw], [1, 128]]),
                            S["wqt"][:, 0:nw, :], deps=tps, note="dma_aglp")
            outs.append(prev_ag)
            prev_m4 = m4
        return outs, prev_ag, prev_m4

    o1, prev_ag, pm = lpbq(dr["wup_sh"], 384, 1024, "upT", prev_ag, qd2)
    ag_parts += o1
    o2, prev_ag, pm = lpbq(dr["wgate_sh"], 384, 1024, "gateT", prev_ag, pm)
    ag_parts += o2
    o3, prev_ag, pm = lpbq(dr["wdown_sh"], 128, 3072, "downT", prev_ag, pm)
    ag_parts += o3

    prev_ldw = pm
    for nm, sg in [("wk_sh", "kT"), ("wv_sh", "vT"), ("wr_sh", "rT"), ("wo_sh", "oT")]:
        ld = p.dma("gq", S["wq0"][:, 0:1024], dr[nm].ap()[:, :], deps=[prev_ag, prev_ldw],
                   note="dma_wc")
        tps = [p.dma("q", S["wqt"][:, dc, :], S["wq0"][:, ts(dc, 128)],
                     deps=[ld, prev_ag], note="dma_tp", transpose=True) for dc in range(8)]
        prev_ag = p.dma("gq", dap(dr["ag_in"], seg[sg],
                                  [[128, 128], [128 * 128, 8], [1, 128]]),
                        S["wqt"][:, 0:8, :], deps=tps, note="dma_agk")
        ag_parts.append(prev_ag)
        prev_ldw = ld

    ag = p.op("c", lambda e: e.collective_compute(
        "AllGather", ALU.bypass, replica_groups=[list(range(NCORES))],
        ins=[dr["ag_in"].ap().opt()], outs=[dr["ag_out"].ap().opt()]),
        [ar] + ag_parts, "AG_w")
    st["ag"] = ag
    st["ar_last"] = ag

    sx1 = p.op("v", lambda e: e.tensor_scalar(S["stat"][:, 48:64], S["stat"][:, 32:48],
                                              1e-5, None, ALU.max), [st["sx_done"]], "sxc")
    sx2 = p.op("v", lambda e: e.reciprocal(S["tiny"][:, 0:16], S["stat"][:, 48:64]),
               [sx1], "sxr")
    sx3 = p.op("v", lambda e: e.tensor_scalar(S["tinyb"][:, 0:16], S["tiny"][:, 0:16],
                                              127.0, None, ALU.mult), [sx2], "sx127")
    sx3b = p.op("v", lambda e: e.tensor_copy(S["tiny"][:, 0:16], S["tinyb"][:, 0:16]),
                [sx3], "sxbk")
    sx4 = p.op("v", lambda e: e.reciprocal(S["tiny2"][:, 0:16], S["tiny"][:, 0:16]),
               [sx3b], "rsx")
    sx5 = p.op("v", lambda e: e.tensor_scalar(S["tiny2"][:, 0:16], S["tiny2"][:, 0:16],
                                              S["scal"][:, S_RW1:S_RW1 + 1], None,
                                              ALU.mult), [sx4, sw], "rsxw")
    st["sxb"] = row_bcast(p, st, S["tiny"][:, 0:16], S["SXB"],
                          [sx3b, sw] + st["xn_done"], "sxb")
    st["rsxw1b"] = row_bcast(p, st, S["tiny2"][:, 0:16], S["RSXW1B"],
                             [sx5, sw] + st["xn_done"], "rsxw")
    p.free("lt2", "lt", "wqt", "wq0", "ws2", "ws0")


def ph2(p, st):
    S = p.sb
    p.alloc("s_bf", [128, DC, T], BF16, side="right")
    p.alloc("cmask", [128, 4, 512], BF16, side="right")
    p.alloc("Qsb", [128, T], BF16)
    p.alloc("Xb", [128, T], BF16)
    p.alloc("Vbf", [128, T], BF16)
    p.alloc("Vtm", [128, 16, 128], BF16)
    p.alloc("attA0", [128, 512], BF16)
    p.alloc("attA1", [128, 512], BF16)
    p.alloc("attB0", [128, 512], BF16)
    p.alloc("attB1", [128, 512], BF16)

    cmk = []
    for rr in range(4):
        mz = p.op("g", lambda e, rr=rr: e.memset(S["cmask"][:, rr, :], 1.0),
                  [st["consts"]], "cmz")
        cmk.append(p.op("g", lambda e, rr=rr: e.affine_select(
            S["cmask"][:, rr, :], S["cmask"][:, rr, :], [[1, 512]], ALU.is_ge, 0.0,
            base=-128 * rr, channel_multiplier=-1), [mz], "cma"))
    evacs = []
    hp_fence = (list(st["xn_done"]) + [st["sqk"], st["sv"], st["consts"],
                st["sxb"], st["rsxw1b"]] + cmk)
    for hp in range(DC):
        qs = p.op("v", lambda e, hp=hp: e.tensor_scalar(
            S["Qsb"][:, :], S["SA"][:, hp, :], S["sqk_c"][:, hp:hp + 1], None, ALU.mult),
            hp_fence, "qs")
        xb = p.op("v", lambda e, hp=hp: e.tensor_copy(S["Xb"][:, :], S["SA"][:, hp, :]),
                  hp_fence, "xb")
        vb = p.op("v", lambda e, hp=hp: e.tensor_scalar(
            S["Vbf"][:, :], S["SA"][:, hp, :], S["sv_c"][:, hp:hp + 1], None, ALU.mult),
            hp_fence, "vb")
        vtm_ev = {}
        prev_ev = None
        for kc in range(16):
            tp = p.op("t", lambda e, kc=kc: e.transpose(
                S["pTb"][:, :], S["Vbf"][:, ts(kc, 128)], S["identB"][:, :]),
                [vb, prev_ev] + (hp_fence if kc == 0 else []), "vtp")
            ev = p.op("s", lambda e, kc=kc: e.copy(S["Vtm"][:, kc, :], S["pTb"][:, :]),
                      [tp], "vev")
            vtm_ev[kc] = ev
            prev_ev = ev
        evacs = []
        prev_sg = {}
        prev_av = {}
        for qt in range(NTQ):
            avbank = "pC"
            nkc = 4 * (qt + 1)
            last_avB = None
            for kc in range(nkc):
                attA = S["attA0"] if kc % 2 == 0 else S["attA1"]
                attB = S["attB0"] if kc % 2 == 0 else S["attB1"]
                bkA = "pA" if kc % 2 == 0 else "pB"
                bkB = "pD" if kc % 2 == 0 else "pE"
                qkA = p.op("t", lambda e, kc=kc, qt=qt, bkA=bkA: e.matmul(
                    S[bkA][:, :], S["Qsb"][0:64, ts(kc, 128)],
                    S["Xb"][0:64, ts(qt, 512)], start=True, stop=True,
                    tile_position=(0, 0), skip_group_check=True),
                    [qs, xb, prev_sg.get(("A", qt, kc - 2)) if kc >= 2
                     else prev_sg.get(("A", qt - 1, 4 * qt - 2 + kc))], "qkA")
                qkB = p.op("t", lambda e, kc=kc, qt=qt, bkB=bkB: e.matmul(
                    S[bkB][:, :], S["Qsb"][64:128, ts(kc, 128)],
                    S["Xb"][64:128, ts(qt, 512)], start=True, stop=True,
                    tile_position=(64, 0), skip_group_check=True),
                    [qs, xb, prev_sg.get(("B", qt, kc - 2)) if kc >= 2
                     else prev_sg.get(("B", qt - 1, 4 * qt - 2 + kc))], "qkB")
                sgA = p.op("s", lambda e, attA=attA, bkA=bkA: e.activation(
                    attA[:, :], S[bkA][:, :], AF.Sigmoid, scale=0.5),
                    [qkA, prev_av.get(("A", qt, kc - 2)) if kc >= 2
                     else prev_av.get(("A", qt - 1, 4 * qt - 1))], "sgA")
                sgB = p.op("s", lambda e, attB=attB, bkB=bkB: e.activation(
                    attB[:, :], S[bkB][:, :], AF.Sigmoid, scale=0.5),
                    [qkB, prev_av.get(("B", qt, kc - 2)) if kc >= 2
                     else prev_av.get(("B", qt - 1, 4 * qt - 1))], "sgB")
                prev_sg[("A", qt, kc)] = sgA
                prev_sg[("B", qt, kc)] = sgB
                depA, depB = sgA, sgB
                if kc >= 4 * qt:
                    rr = kc - 4 * qt
                    depA = p.op("v", lambda e, attA=attA, rr=rr: e.tensor_mul(
                        attA[:, :], attA[:, :], S["cmask"][:, rr, :]), [sgA], "mkA")
                    depB = p.op("v", lambda e, attB=attB, rr=rr: e.tensor_mul(
                        attB[:, :], attB[:, :], S["cmask"][:, rr, :]), [sgB], "mkB")
                avA = p.op("t", lambda e, kc=kc, avbank=avbank, attA=attA, nkc=nkc: e.matmul(
                    S[avbank][0:64, :], S["Vtm"][:, kc, 0:64], attA[:, :],
                    start=(kc == 0), stop=(kc == nkc - 1),
                    tile_position=(0, 0), skip_group_check=True),
                    [depA, vtm_ev[kc], evacs[-1] if evacs and kc == 0 else None],
                    "avA")
                avB = p.op("t", lambda e, kc=kc, avbank=avbank, attB=attB, nkc=nkc: e.matmul(
                    S[avbank][64:128, :], S["Vtm"][:, kc, 64:128], attB[:, :],
                    start=(kc == 0), stop=(kc == nkc - 1),
                    tile_position=(0, 64), skip_group_check=True),
                    [depB, vtm_ev[kc], avA], "avB")
                prev_av[("A", qt, kc)] = avA
                prev_av[("B", qt, kc)] = avB
                last_avB = avB
            ev = p.op("s", lambda e, hp=hp, qt=qt, avbank=avbank: e.copy(
                S["s_bf"][:, hp, ts(qt, 512)], S[avbank][:, :]), [last_avB], "sev")
            evacs.append(ev)
        hp_fence = list(evacs) + [st["sqk"], st["sv"]]
    st["attn_done"] = evacs
    p.free("attB1", "attB0", "attA1", "attA0", "Vtm", "Vbf", "Xb", "Qsb")
    p.free("cmask")


def ph3(p, st):
    S = p.sb
    dr = p.dr
    seg = p.seg
    AGN = p.AGN
    fence = st["attn_done"]

    p.alloc("HMAX", [128, T], F32)
    p.alloc("XQ", [128, DC, T], BF16)
    xq_ops = []
    for dc in range(DC):
        xq_ops.append(quant3_row(p, S["XQ"][:, dc, :], S["SA"][:, dc, :], S["SXB"][:, :],
                                 -128.0, 127.0,
                                 [st["sxb"], st["xn_done"][0], xq_ops[-1] if xq_ops
                                  else None] + fence, S["f1"][:, :]))
    xn_sp_dmas = []
    for dc in range(DC):
        xn_sp_dmas.append(p.dma("q", dap(dr["xn_sp"], dc * 128 * T, [[T, 128], [1, T]]),
                                S["SA"][:, dc, :], deps=[st["xn_done"][0]] + fence,
                                note="dma_xnsp"))
    st["xn_sp"] = xn_sp_dmas

    spill_ops = []
    hmax_ops = []
    sil_prev = {}
    mm_last_of_hc = {}
    prev_ev_ops = None
    for hc in range(HC0):
        r, jc = hc // 4, hc % 4
        wb = wts_sel(S, hc)
        wl = p.dma("q", wb[:, :, :],
                   dap(dr["ag_out"], r * AGN + seg["w1T"] + jc * 128,
                       [[512, 128], [128 * 512, 8], [1, 128]]),
                   deps=[st["ag"], mm_last_of_hc.get(hc - 2)], note="dma_w1t")
        mms = []
        new_ev_ops = []
        for tq in range(NTQ):
            bank = "pA" if tq % 2 == 0 else "pB"
            for dc in range(DC):
                mdep = [wl, xq_ops[dc]]
                if dc == 0:
                    if tq >= 2:
                        mdep.append(new_ev_ops[tq - 2])
                    elif prev_ev_ops is not None:
                        mdep.append(prev_ev_ops[tq + 2])
                m = p.op("t", lambda e, bank=bank, dc=dc, tq=tq, wb=wb: e.matmul(
                    S[bank][:, :], wb[:, dc, :], S["XQ"][:, dc, ts(tq, 512)],
                    start=(dc == 0), stop=(dc == DC - 1)), mdep, "mm1")
                mms.append(m)
            ev = p.op("v", lambda e, bank=bank, tq=tq: e.tensor_mul(
                S["f1"][:, ts(tq, 512)], S[bank][:, :], S["RSXW1B"][:, ts(tq, 512)]),
                [mms[-1], st["rsxw1b"]] + ([sil_prev.get(hc - 1)] if tq == 0 else []),
                "ev1")
            new_ev_ops.append(ev)
        prev_ev_ops = new_ev_ops
        mm_last_of_hc[hc] = mms[-1]
        c1 = p.op("s", lambda e, hc=hc: e.activation(
            S["f2"][:, :], S["f1"][:, :], AF.Copy, scale=S["cw1"][:, hc:hc + 1]),
            new_ev_ops + [sil_prev.get(hc - 1)], "cv1")
        c2 = p.op("v", lambda e, hc=hc: e.scalar_tensor_tensor(
            S["f2"][:, 1:T], S["f1"][:, 0:T - 1], S["cw0"][:, hc:hc + 1],
            S["f2"][:, 1:T], ALU.mult, ALU.add), [c1] + new_ev_ops, "cv0")
        c3 = p.op("v", lambda e, hc=hc: e.scalar_tensor_tensor(
            S["f2"][:, 0:T - 1], S["f1"][:, 1:T], S["cw2"][:, hc:hc + 1],
            S["f2"][:, 0:T - 1], ALU.mult, ALU.add), [c2], "cv2")
        stage = S["g1s"] if hc % 2 == 0 else S["g2s"]
        sil = p.op("s", lambda e, hc=hc, stage=stage: e.activation(
            stage[:, :], S["f2"][:, :], AF.Silu, bias=S["cb"][:, hc:hc + 1]),
            [c3, spill_ops[-2] if hc >= 2 else None,
             hmax_ops[-2] if hc >= 2 else None], "sil")
        sil_prev[hc] = sil
        if hc == 0:
            hm = p.op("g", lambda e, stage=stage: e.tensor_copy(
                S["HMAX"][:, :], stage[:, :]), [sil], "hm0")
        else:
            hm = p.op("v", lambda e, stage=stage: e.tensor_max(
                S["HMAX"][:, :], S["HMAX"][:, :], stage[:, :]), [sil, hmax_ops[-1]], "hm")
        hmax_ops.append(hm)
        spill_ops.append(p.dma("q", dap(dr["hsil_sp"], hc * 128 * T, [[T, 128], [1, T]]),
                               stage[:, :], deps=[sil], note="dma_hsp"))
    p.free("XQ")

    prev_rd = None
    for i in range(16):
        bank = "pT" if i % 2 == 0 else "pT2"
        tp = p.op("t", lambda e, i=i, bank=bank: e.transpose(
            S[bank][:, :], S["HMAX"][:, ts(i, 128)], S["identF"][:, :]),
            [hmax_ops[-1], prev_rd], "h2tp")
        rd = p.op("v", lambda e, i=i, bank=bank: e.reduce_max(
            S["stat"][:, 32 + i:33 + i], S[bank][:, :], AX), [tp], "h2rd")
        prev_rd = rd
    p.free("HMAX")
    p.free("RSXW1B", "SXB", "SA")

    p.alloc("MEAN1B", [128, T], F32)
    p.alloc("RSTD1B", [128, T], F32)
    p.alloc("S2B", [128, T], BF16)
    p.alloc("RS2W2B", [128, T], F32)
    p.alloc("MEAN0B", [128, T], F32)
    p.alloc("STD0B", [128, T], F32)
    p.alloc("XQ2", [128, HC0, TH], BF16)
    p.alloc("xnt", [128, 512], F32)
    sx1 = p.op("v", lambda e: e.tensor_scalar(S["stat"][:, 48:64], S["stat"][:, 32:48],
                                              1e-5, None, ALU.max), [prev_rd], "s2c")
    sx2 = p.op("v", lambda e: e.reciprocal(S["tiny"][:, 0:16], S["stat"][:, 48:64]),
               [sx1], "s2r")
    sx3 = p.op("v", lambda e: e.tensor_scalar(S["tinyb"][:, 0:16], S["tiny"][:, 0:16],
                                              127.0, None, ALU.mult), [sx2], "s2127")
    sx3b = p.op("v", lambda e: e.tensor_copy(S["tiny"][:, 0:16], S["tinyb"][:, 0:16]),
                [sx3], "s2bk")
    sx4 = p.op("v", lambda e: e.reciprocal(S["tiny2"][:, 0:16], S["tiny"][:, 0:16]),
               [sx3b], "rs2")
    sx5 = p.op("v", lambda e: e.tensor_scalar(S["tiny2"][:, 0:16], S["tiny2"][:, 0:16],
                                              S["scal"][:, S_RW2:S_RW2 + 1], None,
                                              ALU.mult), [sx4, st["sw"]], "rs2w")
    s2b = row_bcast(p, st, S["tiny"][:, 0:16], S["S2B"], [sx3b, prev_rd], "s2b")
    rs2b = row_bcast(p, st, S["tiny2"][:, 0:16], S["RS2W2B"], [sx5, prev_rd], "rs2b")
    sd1 = p.op("v", lambda e: e.reciprocal(S["tiny"][:, 0:16], S["stat"][:, 16:32]),
               [s2b, rs2b], "std0")
    m0b = row_bcast(p, st, S["stat"][:, 0:16], S["MEAN0B"], [sd1], "m0b")
    st0b = row_bcast(p, st, S["tiny"][:, 0:16], S["STD0B"], [sd1], "st0b")

    x1_dmas = []
    ln1_var_ops = []
    banks = ["pA", "pB"]
    prev_qz = None
    for th in range(2):
        qz_ops = []
        for hc in range(HC0):
            stage = S["g1s"] if hc % 2 == 0 else S["g2s"]
            if hc >= 2:
                war = qz_ops[-2]
            elif th == 0:
                war = spill_ops[HC0 - 2 + hc]
            else:
                war = prev_qz[HC0 - 2 + hc]
            ld = p.dma("q", stage[:, 0:TH],
                       dap(dr["hsil_sp"], hc * 128 * T + th * TH, [[T, 128], [1, TH]]),
                       deps=[spill_ops[hc], s2b, war],
                       note="dma_hld")
            qz = quant3_row(p, S["XQ2"][:, hc, :], stage[:, 0:TH],
                            S["S2B"][:, ts(th, TH)], -128.0, 127.0, [ld],
                            S["f1"][:, (hc % 2) * TH:(hc % 2) * TH + TH],
                            chans=("v", "g", "g"))
            qz_ops.append(qz)
        wl_prev = {}
        for dc in range(DC):
            mms = {0: [], 1: []}
            wb = S["wt2"] if dc % 2 == 0 else S["wt3"]
            wl = p.dma("q", wb[:, :, :],
                       dap(dr["ag_out"], dc * AGN + seg["w2T"],
                           [[128, 128], [128 * 128, HC0], [1, 128]]),
                       deps=[st["ag"], qz_ops[-1], wl_prev.get(dc - 2)],
                       note="dma_w2t")
            wl_prev[dc] = None
            for hc in range(HC0):
                for tq2 in range(2):
                    m = p.op("t", lambda e, wb=wb, tq2=tq2, hc=hc: e.matmul(
                        S[banks[tq2]][:, :], wb[:, hc, :], S["XQ2"][:, hc, ts(tq2, 512)],
                        start=(hc == 0), stop=(hc == HC0 - 1)),
                        [wl] + ([x1_dmas[-1]] if hc == 0 and x1_dmas else []), "mm2")
                    mms[tq2].append(m)
            wl_prev[dc] = mms[1][-1]
            for tq2 in range(2):
                gt = th * 2 + tq2
                ev = p.op("v", lambda e, tq2=tq2, gt=gt: e.tensor_mul(
                    S["f1"][:, 0:512], S[banks[tq2]][:, :], S["RS2W2B"][:, ts(gt, 512)]),
                    [mms[tq2][-1], rs2b], "ev2")
                sil = p.op("s", lambda e: e.activation(
                    S["f2"][:, 0:512], S["f1"][:, 0:512], AF.Silu), [ev], "sil2")
                sg = p.op("v", lambda e, dc=dc, gt=gt: e.tensor_mul(
                    S["f2"][:, 0:512], S["f2"][:, 0:512], S["s_bf"][:, dc, ts(gt, 512)]),
                    [sil], "sg")
                xn_ld = p.dma("q", S["xnt"][:, :],
                              dap(dr["xn_sp"], dc * 128 * T + gt * 512,
                                  [[T, 128], [1, 512]]),
                              deps=[xn_sp_dmas[dc], m0b, st0b,
                                    x1_dmas[-1] if x1_dmas else None], note="dma_xnld")
                xr = p.op("v", lambda e, gt=gt: e.tensor_mul(
                    S["f3"][:, 0:512], S["xnt"][:, :], S["STD0B"][:, ts(gt, 512)]),
                    [xn_ld], "xstd")
                xr2 = p.op("v", lambda e, gt=gt: e.tensor_add(
                    S["f3"][:, 0:512], S["f3"][:, 0:512], S["MEAN0B"][:, ts(gt, 512)]),
                    [xr], "xmean")
                x1t = p.op("v", lambda e: e.tensor_add(
                    S["f3"][:, 0:512], S["f3"][:, 0:512], S["f2"][:, 0:512]),
                    [xr2, sg], "x1t")
                x1_dmas.append(p.dma("q", dap(dr["x1_sp"], dc * 128 * T + gt * 512,
                                              [[T, 128], [1, 512]]),
                                     S["f3"][:, 0:512], deps=[x1t], note="dma_x1sp"))
        prev_qz = qz_ops
    st["x1_dmas"] = x1_dmas
    st["x1_idx"] = lambda dc, gt: (gt // 2) * 16 + dc * 2 + (gt % 2)
    p.free("xnt", "XQ2", "STD0B", "MEAN0B", "RS2W2B", "S2B")
    p.free("s_bf")


def ph4(p, st):
    S = p.sb
    dr = p.dr
    seg = p.seg
    AGN = p.AGN
    fence = st["x1_dmas"][-2:]

    p.alloc("xnt4", [128, 512], F32)
    ln1_var_ops = []
    prev = None
    mS1 = mS2 = None
    for gt in range(NTQ):
        for dc in range(DC):
            ld = p.dma("q", S["xnt4"][:, :],
                       dap(dr["x1_sp"], dc * 128 * T + gt * 512, [[T, 128], [1, 512]]),
                       deps=[st["x1_dmas"][st["x1_idx"](dc, gt)], prev, mS1] +
                            (fence if dc == 0 and gt == 0 else []), note="dma_x1st")
            sqz = p.op("v", lambda e: e.tensor_mul(
                S["f2"][:, 0:512], S["xnt4"][:, :], S["xnt4"][:, :]), [ld, mS2], "sq1")
            mS1 = p.op("t", lambda e, dc=dc: e.matmul(
                S["pC"][:, :], S["ones"][:, :], S["xnt4"][:, :],
                start=(dc == 0), stop=(dc == DC - 1)),
                [ld, ln1_var_ops[-1] if dc == 0 and ln1_var_ops else None], "ms1")
            mS2 = p.op("t", lambda e, dc=dc: e.matmul(
                S["pD"][:, :], S["ones"][:, :], S["f2"][:, 0:512],
                start=(dc == 0), stop=(dc == DC - 1)), [sqz], "ms2")
            prev = sqz
        e1 = p.op("v", lambda e, gt=gt: e.tensor_scalar(
            S["MEAN1B"][:, ts(gt, 512)], S["pC"][:, :], 1.0 / D, None,
            ALU.mult), [mS1], "ln1m")
        e2 = p.op("v", lambda e: e.tensor_scalar(
            S["f1"][:, 0:512], S["pD"][:, :], 1.0 / D, None, ALU.mult),
            [mS2], "ln1e2")
        e3 = p.op("v", lambda e, gt=gt: e.tensor_mul(
            S["f2"][:, 0:512], S["MEAN1B"][:, ts(gt, 512)],
            S["MEAN1B"][:, ts(gt, 512)]), [e1, e2], "ln1m2")
        e4 = p.op("v", lambda e: e.tensor_sub(
            S["f1"][:, 0:512], S["f1"][:, 0:512], S["f2"][:, 0:512]), [e3], "ln1v")
        e5 = p.op("s", lambda e: e.activation(
            S["f1"][:, 0:512], S["f1"][:, 0:512], AF.Sqrt,
            bias=S["epsc"][:, 0:1]), [e4], "ln1sd")
        e6 = p.op("v", lambda e, gt=gt: e.reciprocal(
            S["RSTD1B"][:, ts(gt, 512)], S["f1"][:, 0:512]), [e5], "ln1rs")
        ln1_var_ops.append(e6)
        prev = e6
    st["ln1"] = ln1_var_ops

    mx_cols = []
    xn1_dmas = []
    prev = st["ln1"][-1]
    for dc in range(DC):
        for gt in range(NTQ):
            ld = p.dma("q", S["xnt4"][:, :],
                       dap(dr["x1_sp"], dc * 128 * T + gt * 512, [[T, 128], [1, 512]]),
                       deps=[st["x1_dmas"][st["x1_idx"](dc, gt)], prev], note="dma_x1ld")
            n1 = p.op("v", lambda e, gt=gt: e.tensor_sub(
                S["f1"][:, ts(gt, 512)], S["xnt4"][:, :], S["MEAN1B"][:, ts(gt, 512)]),
                [ld] + st["ln1"], "n1")
            prev = p.op("v", lambda e, gt=gt: e.tensor_mul(
                S["f1"][:, ts(gt, 512)], S["f1"][:, ts(gt, 512)],
                S["RSTD1B"][:, ts(gt, 512)]), [n1], "n2")
        mx_cols.append(p.op("v", lambda e, dc=dc: e.reduce_max(
            S["mxc"][:, dc:dc + 1], S["f1"][:, :], AX, apply_absolute_value=True),
            [prev], "m0mx"))
        d = p.dma("q", dap(dr["xn1_sp"], dc * 128 * T, [[T, 128], [1, T]]),
                  S["f1"][:, :], deps=[prev], note="dma_xn1")
        xn1_dmas.append(d)
        prev = d
    rs0 = global_max_ar(p, st, S["mxc"][:, 0:8], 2, S_S0, S_RS0, "m0", mx_cols)
    p.free("xnt4")
    p.free("RSTD1B", "MEAN1B")

    p.alloc("KT", [128, DC, T], BF16, side="right")
    p.alloc("VT", [128, DC, T], BF16, side="right")
    p.alloc("SRT", [128, DC, T], BF16, side="right")
    p.alloc("MIX", [128, DC, T], BF16, side="right")
    banks = ["pA", "pB", "pC", "pD"]
    st["proj_done"] = {}
    prev_q = None
    for mixn, wseg, dstn, sig in [("tmk", "kT", "KT", False), ("tmv", "vT", "VT", False),
                                  ("tmr", "rT", "SRT", True)]:
        mix_ops = []
        prev = prev_q
        for dc in range(DC):
            ld = p.dma("q", S["f1"][:, :],
                       dap(dr["xn1_sp"], dc * 128 * T, [[T, 128], [1, T]]),
                       deps=[xn1_dmas[dc], xn1_dmas[-1] if dc == 0 else None,
                             mx_cols[-1] if dc == 0 else None, prev],
                       note="dma_xm")
            dsub = p.op("v", lambda e: e.tensor_sub(S["f2"][:, 1:T], S["f1"][:, 1:T],
                                                    S["f1"][:, 0:T - 1]), [ld], "dmx")
            dcol = p.op("v", lambda e: e.tensor_copy(S["f2"][:, 0:1], S["f1"][:, 0:1]),
                        [ld], "dc0")
            o1 = p.op("v", lambda e, mixn=mixn, dc=dc: e.scalar_tensor_tensor(
                S["MIX"][:, dc, 1:T], S["f2"][:, 1:T], S[mixn][:, dc:dc + 1],
                S["f1"][:, 0:T - 1], ALU.mult, ALU.add), [dsub, dcol], "mx1")
            o2 = p.op("v", lambda e, mixn=mixn, dc=dc: e.tensor_scalar(
                S["MIX"][:, dc, 0:1], S["f2"][:, 0:1], S[mixn][:, dc:dc + 1], None,
                ALU.mult), [dcol], "mx0")
            mix_ops += [o1, o2]
            prev = o2
        ev_prev = {}
        evl = []
        last_mm_cc = {}
        for cc in range(DC):
            mms = {tq: None for tq in range(NTQ)}
            wb = wts_sel(S, cc)
            wl = p.dma("q", wb[:, :, :],
                       dap(dr["ag_out"], cc * AGN + seg[wseg],
                           [[128, 128], [128 * 128, DC], [1, 128]]),
                       deps=[st["ag"], last_mm_cc.get(cc - 2)] +
                            (mix_ops if cc == 0 else []), note="dma_wp")
            for dc in range(DC):
                for tq in range(NTQ):
                    m = p.op("t", lambda e, wb=wb, dc=dc, tq=tq: e.matmul(
                        S[banks[tq]][:, :], wb[:, dc, :], S["MIX"][:, dc, ts(tq, 512)],
                        start=(dc == 0), stop=(dc == DC - 1)),
                        [wl, ev_prev.get((cc - 1, tq))], "mmp")
                    mms[tq] = m
            last_mm_cc[cc] = mms[NTQ - 1]
            for tq in range(NTQ):
                if sig:
                    ev = p.op("s", lambda e, cc=cc, tq=tq, dstn=dstn: e.activation(
                        S[dstn][:, cc, ts(tq, 512)], S[banks[tq]][:, :], AF.Sigmoid),
                        [mms[tq]], "evsig")
                else:
                    ev = p.op("s", lambda e, cc=cc, tq=tq, dstn=dstn: e.copy(
                        S[dstn][:, cc, ts(tq, 512)], S[banks[tq]][:, :]), [mms[tq]], "evp")
                ev_prev[(cc, tq)] = ev
                evl.append(ev)
        st["proj_done"][dstn] = evl
        prev_q = evl[-1]
    p.free("MIX")

    p.alloc("XQ1", [128, DC, T], BF16, side="right")
    xq1_ops = []
    proj_fence = (st["proj_done"]["KT"][-1:] + st["proj_done"]["VT"][-1:]
                  + st["proj_done"]["SRT"][-1:])
    prev = None
    for dc in range(DC):
        ld = p.dma("q", S["f1"][:, :], dap(dr["xn1_sp"], dc * 128 * T, [[T, 128], [1, T]]),
                   deps=[xn1_dmas[dc], prev] + (proj_fence if dc == 0 else []),
                   note="dma_xq1")
        q = quant3_col(p, S["XQ1"][:, dc, :], S["f1"][:, :],
                       S["scal"][:, S_RS0:S_RS0 + 1], -32768.0, 32767.0,
                       [ld, rs0], S["f2"][:, :])
        xq1_ops.append(q)
        prev = q
    st["xq1"] = xq1_ops


def ph5(p, st):
    S = p.sb
    dr = p.dr
    seg = p.seg
    AGN = p.AGN

    TQS = T // 4
    p.alloc("f4", [128, T], F32)
    p.alloc("PP", [128, TQS + 1], F32)
    p.alloc("AAs", [128, TQS + 1], F32)
    p.alloc("BBs", [128, TQS + 1], F32)
    p.alloc("X4b", [128, TQS], F32)
    X1b = S["f1"][:, 0:TQS]
    X2b = S["f2"][:, 0:TQS]
    X3b = S["f3"][:, 0:TQS]

    upgate_spills = {"up": [], "gate": []}
    mx_ops = {"up": [], "gate": []}
    banksUG = {"up": ["pA", "pB"], "gate": ["pC", "pD"]}
    stages = {"up": "f4", "gate": "f4"}
    stages2 = {"up": "f4", "gate": "f4"}
    last_mm = {"up": None, "gate": None}
    last_mx = None
    for Ic in range(IC1):
        r, jc = Ic // 3, Ic % 3
        for nm, wsg, spnm in [("up", "upT", "up_sp"), ("gate", "gateT", "gate_sp")]:
            wb = wts_sel(S, 2 * Ic + (0 if nm == "up" else 1))
            wl = p.dma("q", wb[:, :, :],
                       dap(dr["ag_out"], r * AGN + seg[wsg] + jc * 128,
                           [[384, 128], [128 * 384, 8], [1, 128]]),
                       deps=[st["ag"], last_mm[nm]] + (st["xq1"] if Ic == 0 else []),
                       note="dma_wug")
            stage = S[stages[nm]] if Ic % 2 == 0 else S[stages2[nm]]
            evs = []
            mprev = None
            for tq in range(NTQ):
                bank = banksUG[nm][tq % 2]
                for dc in range(DC):
                    m = p.op("t", lambda e, wb=wb, bank=bank, dc=dc, tq=tq: e.matmul(
                        S[bank][:, :], wb[:, dc, :], S["XQ1"][:, dc, ts(tq, 512)],
                        start=(dc == 0), stop=(dc == DC - 1)),
                        [wl, evs[tq - 2] if tq >= 2 and dc == 0 else None], "mmug")
                    mprev = m
                lastsp = (upgate_spills["gate"] if nm == "up"
                          else upgate_spills["up"])
                ev = p.op("s", lambda e, bank=bank, tq=tq, stage=stage: e.activation(
                    stage[:, ts(tq, 512)], S[bank][:, :], AF.Copy,
                    scale=S["scal"][:, S_S0:S_S0 + 1]),
                    [mprev, st["xq1"][-1],
                     upgate_spills[nm][-1] if upgate_spills[nm] and tq == 0 else None,
                     lastsp[-1] if lastsp and tq == 0 else None,
                     last_mx if tq == 0 else None],
                    "evug")
                evs.append(ev)
            last_mm[nm] = mprev
            last_mx = p.op("v", lambda e, stage=stage, nm=nm, Ic=Ic: e.reduce_max(
                S["mxc"][:, (8 if nm == "up" else 32) + Ic:
                         (9 if nm == "up" else 33) + Ic],
                stage[:, :], AX, apply_absolute_value=True), evs, "mxug")
            mx_ops[nm].append(last_mx)
            upgate_spills[nm].append(p.dma("q", dap(dr[spnm], Ic * 128 * T,
                                                    [[T, 128], [1, T]]),
                                           stage[:, :], deps=evs, note="dma_ugsp"))
    st["upg_spills"] = upgate_spills

    c1 = p.op("v", lambda e: e.reduce_max(S["mxc"][:, 56:57], S["mxc"][:, 8:32], AX),
              mx_ops["up"] + [st["ar_last"]], "m1cl")
    c2 = p.op("v", lambda e: e.reduce_max(S["mxc"][:, 57:58], S["mxc"][:, 32:56], AX),
              mx_ops["gate"], "m2cl")
    tp = p.op("t", lambda e: e.transpose(S["pT2"][0:2, 0:128], S["mxc"][:, 56:58],
                                         S["identF"][:, :]), [c1, c2], "m12tp")
    rd1 = p.op("v", lambda e: e.reduce_max(S["mxc"][0:2, 58:59], S["pT2"][0:2, 0:128], AX),
               [tp], "m12rd")
    dcp = p.dma("gq", dr["ar_in"].ap()[3:5], dap(S["mxc"], 58, [[64, 2], [1, 1]]),
                deps=[rd1], note="dma_m12")
    ar12 = p.op("c", lambda e: e.collective_compute(
        "AllReduce", ALU.max, replica_groups=[list(range(NCORES))],
        ins=[dr["ar_in"].ap()[3:5].opt()], outs=[dr["ar_out"].ap()[3:5].opt()]),
        [dcp, st["ar_last"]], "AR_m12")
    st["ar_last"] = ar12
    lb = p.dma("gq", S["mxc"][0:1, 58:60], dr["ar_out"].ap()[3:5], deps=[ar12, dcp],
               note="dma_m12b")
    bc = bcast_p0(p, S["mxc"][0:1, 58:60], S["mxc"][:, 60:62], 2, [lb], "m12")
    q1 = p.op("v", lambda e: e.tensor_scalar(S["scal"][:, S_S1:S_S1 + 1],
                                             S["mxc"][:, 60:61], 1.0 / 32767.0, 1e-12,
                                             ALU.mult, ALU.max), [bc], "s1s")
    q2 = p.op("v", lambda e: e.reciprocal(S["scal"][:, S_RS1:S_RS1 + 1],
                                          S["scal"][:, S_S1:S_S1 + 1]), [q1], "rs1")
    q3 = p.op("v", lambda e: e.tensor_scalar(S["scal"][:, S_S2:S_S2 + 1],
                                             S["mxc"][:, 61:62], 1.0 / 32767.0, 1e-12,
                                             ALU.mult, ALU.max), [bc], "s2s")
    q4 = p.op("v", lambda e: e.reciprocal(S["scal"][:, S_RS2:S_RS2 + 1],
                                          S["scal"][:, S_S2:S_S2 + 1]), [q3], "rs2s")
    q5 = p.op("v", lambda e: e.tensor_scalar(S["scal"][:, S_S265536:S_S265536 + 1],
                                             S["scal"][:, S_S2:S_S2 + 1], 1.0 / 65536.0,
                                             None, ALU.mult), [q3], "s265")
    st["scales12"] = [q2, q4, q5]

    rwkv_done = []
    rwkv_pair = {}
    prev_half = None
    for cc in range(DC):
        for h in range(4):
            off = h * TQS
            kap = S["KT"][:, cc, off:off + TQS]
            vap = S["VT"][:, cc, off:off + TQS]
            deps0 = [st["proj_done"]["KT"][cc * NTQ + h],
                     st["proj_done"]["VT"][cc * NTQ + h],
                     st["proj_done"]["SRT"][-1],
                     st["wc"], st["nu"], prev_half]
            if h == 0:
                i1 = p.op("v", lambda e: e.memset(S["PP"][:, 0:1], -1e38), deps0, "ppi")
                i2 = p.op("v", lambda e: e.memset(S["AAs"][:, 0:1], 0.0), deps0, "aai")
                i3 = p.op("v", lambda e: e.memset(S["BBs"][:, 0:1], 0.0), deps0, "bbi")
            else:
                i1 = p.op("v", lambda e: e.tensor_copy(S["PP"][:, 0:1],
                                                       S["PP"][:, TQS:TQS + 1]), deps0, "ppc")
                i2 = p.op("v", lambda e: e.tensor_copy(S["AAs"][:, 0:1],
                                                       S["AAs"][:, TQS:TQS + 1]), deps0, "aac")
                i3 = p.op("v", lambda e: e.tensor_copy(S["BBs"][:, 0:1],
                                                       S["BBs"][:, TQS:TQS + 1]), deps0, "bbc")
            wbc = p.op("v", lambda e, cc=cc: e.tensor_copy(
                S["X4b"][:, :], dap(S["w_c"], cc, [[8, 128], [0, TQS]])), deps0, "wbc")
            scn = p.op("v", lambda e, kap=kap: e.tensor_tensor_scan(
                S["PP"][:, 1:], S["X4b"][:, :], kap, S["PP"][:, 0:1], ALU.add, ALU.max),
                [i1, i2, i3, wbc], "ppscan")
            t1 = p.op("v", lambda e, kap=kap: e.tensor_sub(X1b, kap, S["PP"][:, 1:]),
                      [scn], "t1")
            t2 = p.op("v", lambda e: e.tensor_sub(X2b, S["PP"][:, 0:TQS], S["PP"][:, 1:]),
                      [scn], "t2")
            e2 = p.op("s", lambda e: e.activation(X3b, X1b, AF.Exp), [t1], "e2")
            e1 = p.op("s", lambda e, cc=cc: e.activation(
                X2b, X2b, AF.Exp, bias=S["w_c"][:, cc:cc + 1]), [t2], "e1")
            za = p.op("v", lambda e, vap=vap: e.tensor_mul(S["X4b"][:, :], X3b, vap),
                      [e2, e1, scn], "za")
            sa = p.op("v", lambda e: e.tensor_tensor_scan(
                S["AAs"][:, 1:], X2b, S["X4b"][:, :], S["AAs"][:, 0:1],
                ALU.mult, ALU.add), [za], "aascan")
            sb_ = p.op("v", lambda e: e.tensor_tensor_scan(
                S["BBs"][:, 1:], X2b, X3b, S["BBs"][:, 0:1], ALU.mult, ALU.add),
                [sa], "bbscan")
            ra = p.op("v", lambda e, kap=kap: e.tensor_sub(X1b, S["PP"][:, 0:TQS], kap),
                      [sb_, t1], "ra")
            rb = p.op("v", lambda e, cc=cc: e.tensor_scalar(
                X1b, X1b, S["nu_c"][:, cc:cc + 1], CAP, ALU.add, ALU.min), [ra], "rb")
            rr = p.op("s", lambda e: e.activation(X1b, X1b, AF.Exp), [rb], "rr")
            nm1 = p.op("v", lambda e: e.tensor_mul(X2b, X1b, S["AAs"][:, 0:TQS]),
                       [rr], "nm1")
            nm2 = p.op("v", lambda e, vap=vap: e.tensor_add(X2b, X2b, vap), [nm1], "nm2")
            dn1 = p.op("v", lambda e: e.tensor_mul(X3b, X1b, S["BBs"][:, 0:TQS]),
                       [nm1], "dn1")
            dn2 = p.op("v", lambda e: e.tensor_scalar(X3b, X3b, 1.0, None, ALU.add),
                       [dn1], "dn2")
            rcp = p.op("v", lambda e: e.reciprocal(X3b, X3b), [dn2], "rcp")
            yy = p.op("v", lambda e: e.tensor_mul(X2b, X2b, X3b), [rcp, nm2], "yy")
            rw = p.op("v", lambda e, cc=cc, off=off: e.tensor_mul(
                S["KT"][:, cc, off:off + TQS], X2b, S["SRT"][:, cc, off:off + TQS]),
                [yy], "rwkv")
            rwkv_done.append(rw)
            rwkv_pair.setdefault(cc, []).append(rw)
            prev_half = rw
    st["rwkv"] = rwkv_done

    s1_dmas = []
    ev_prev = {}
    banks = ["pA", "pB", "pC", "pD"]
    last_mm_dcg = {}
    for dcg in range(DC):
        mms = {tq: None for tq in range(NTQ)}
        wb = wts_sel(S, dcg)
        wl = p.dma("q", wb[:, :, :],
                   dap(dr["ag_out"], dcg * AGN + seg["oT"],
                       [[128, 128], [128 * 128, DC], [1, 128]]),
                   deps=[st["ag"], last_mm_dcg.get(dcg - 2)] +
                        ([last_mm["up"], last_mm["gate"]] if dcg == 0 else []),
                   note="dma_wo")
        for cc in range(DC):
            for tq in range(NTQ):
                m = p.op("t", lambda e, wb=wb, cc=cc, tq=tq: e.matmul(
                    S[banks[tq]][:, :], wb[:, cc, :], S["KT"][:, cc, ts(tq, 512)],
                    start=(cc == 0), stop=(cc == DC - 1)),
                    [wl, ev_prev.get((dcg - 1, tq)), rwkv_pair[cc][-1]], "mmo")
                mms[tq] = m
        last_mm_dcg[dcg] = mms[NTQ - 1]
        for tq in range(NTQ):
            stage = S["f4"]
            ev = p.op("s", lambda e, tq=tq, stage=stage: e.copy(
                stage[:, ts(tq, 512)], S[banks[tq]][:, :]),
                [mms[tq], s1_dmas[-4] if len(s1_dmas) >= 4 else None,
                 last_mm["up"] if dcg == 0 and tq == 0 else None,
                 last_mm["gate"] if dcg == 0 and tq == 0 else None,
                 upgate_spills["up"][-1] if dcg == 0 and tq == 0 else None,
                 upgate_spills["gate"][-1] if dcg == 0 and tq == 0 else None], "evo")
            ev_prev[(dcg, tq)] = ev
            s1_dmas.append(p.dma("q", dap(dr["s1_sp"], dcg * 128 * T + tq * 512,
                                          [[T, 128], [1, 512]]),
                                 stage[:, ts(tq, 512)], deps=[ev], note="dma_s1sp"))
    st["s1_dmas"] = s1_dmas
    p.free("X4b", "BBs", "AAs", "PP")
    p.free("XQ1", "SRT", "VT", "KT")
    p.alloc("f5", [128, T], F32)


def ph6(p, st):
    S = p.sb
    dr = p.dr
    seg = p.seg
    AGN = p.AGN

    g2_spills = []
    mx3 = []
    gq_l = []
    g2_l = []
    for Ic in range(IC1):
        stage = S["f4"] if Ic % 2 == 0 else S["f5"]
        ld = p.dma("q", stage[:, :], dap(dr["gate_sp"], Ic * 128 * T, [[T, 128], [1, T]]),
                   deps=[st["upg_spills"]["gate"][Ic],
                         gq_l[-2] if len(gq_l) >= 2 else None] + st["scales12"] +
                        (st["s1_dmas"][-4:] if Ic < 2 else []), note="dma_gld")
        gq = quant3_col(p, S["f1"][:, :], stage[:, :], S["scal"][:, S_RS2:S_RS2 + 1],
                        -32768.0, 32767.0, [ld, g2_l[-1] if g2_l else None],
                        S["f1"][:, :], chan3="g")
        gq_l.append(gq)
        sg = p.op("s", lambda e: e.activation(S["f2"][:, :], S["f1"][:, :], AF.Sigmoid,
                                              scale=S["scal"][:, S_S2:S_S2 + 1]),
                  [gq], "sigg")
        u1 = p.op("s", lambda e: e.activation(S["f2"][:, :], S["f2"][:, :], AF.Copy,
                                              scale=65536.0, bias=RND), [sg], "u16a")
        u1b = p.op("s", lambda e: e.activation(S["f2"][:, :], S["f2"][:, :], AF.Copy,
                                               bias=-RND), [u1], "u16ab")
        u2 = p.op("v", lambda e: e.tensor_scalar(S["f2"][:, :], S["f2"][:, :], 65535.0,
                                                 None, ALU.min), [u1b], "u16b")
        g2 = p.op("v", lambda e: e.scalar_tensor_tensor(
            S["f3"][:, :], S["f1"][:, :], S["scal"][:, S_S265536:S_S265536 + 1],
            S["f2"][:, :], ALU.mult, ALU.mult),
            [u2, gq, g2_spills[-1] if g2_spills else None], "g2")
        g2_l.append(g2)
        mx3.append(p.op("v", lambda e, Ic=Ic: e.reduce_max(
            S["mxc"][:, 8 + Ic:9 + Ic], S["f3"][:, :], AX,
            apply_absolute_value=True), [g2], "mx3"))
        sp = p.dma("q", dap(dr["gate2_sp"], Ic * 128 * T, [[T, 128], [1, T]]),
                   S["f3"][:, :], deps=[g2], note="dma_g2sp")
        g2_spills.append(sp)
    ar3 = global_max_ar(p, st, S["mxc"][:, 8:32], 5, S_S3, S_RS3, "m3", mx3)
    ss = p.op("v", lambda e: e.tensor_mul(S["scal"][:, S_S3S1:S_S3S1 + 1],
                                          S["scal"][:, S_S3:S_S3 + 1],
                                          S["scal"][:, S_S1:S_S1 + 1]),
              [ar3] + st["scales12"], "s3s1")

    o_spills = []
    mx4 = []
    q1_l = []
    q2_l = []
    oo_l = []
    for Ic in range(IC1):
        st1 = S["f4"]
        st2 = S["f5"]
        ld1 = p.dma("q", st1[:, :], dap(dr["gate2_sp"], Ic * 128 * T, [[T, 128], [1, T]]),
                    deps=[g2_spills[Ic], ss, q1_l[-1] if q1_l else None], note="dma_g2ld")
        ld2 = p.dma("q", st2[:, :], dap(dr["up_sp"], Ic * 128 * T, [[T, 128], [1, T]]),
                    deps=[st["upg_spills"]["up"][Ic], ss, q2_l[-1] if q2_l else None],
                    note="dma_upld")
        q1 = quant3_col(p, S["f1"][:, :], st1[:, :], S["scal"][:, S_RS3:S_RS3 + 1],
                        -32768.0, 32767.0, [ld1, oo_l[-1] if oo_l else None],
                        S["f1"][:, :])
        q2 = quant3_col(p, S["f2"][:, :], st2[:, :], S["scal"][:, S_RS1:S_RS1 + 1],
                        -32768.0, 32767.0, [ld2, oo_l[-1] if oo_l else None],
                        S["f2"][:, :])
        q1_l.append(q1)
        q2_l.append(q2)
        oo = p.op("v", lambda e: e.scalar_tensor_tensor(
            S["f3"][:, :], S["f1"][:, :], S["scal"][:, S_S3S1:S_S3S1 + 1],
            S["f2"][:, :], ALU.mult, ALU.mult),
            [q1, q2, o_spills[-1] if o_spills else g2_spills[-1]], "oprod")
        oo_l.append(oo)
        mx4.append(p.op("v", lambda e, Ic=Ic: e.reduce_max(
            S["mxc"][:, 8 + Ic:9 + Ic], S["f3"][:, :], AX,
            apply_absolute_value=True), [oo], "mx4"))
        sp = p.dma("q", dap(dr["o_sp"], Ic * 128 * T, [[T, 128], [1, T]]),
                   S["f3"][:, :], deps=[oo], note="dma_osp")
        o_spills.append(sp)
    ar4 = global_max_ar(p, st, S["mxc"][:, 8:32], 6, S_S4, S_RS4, "m4", mx4)

    p.alloc("OQ", [128, IC1, TH], BF16)
    out_dmas = []
    last_mmd = None
    last_x2 = None
    last_sgd = None
    last_tpo = None
    dn_last = {0: None, 1: None}
    banks = ["pA", "pB"]
    for th in range(2):
        oq_ops = []
        for Ic in range(IC1):
            stg = S["f4"][:, 0:TH] if Ic % 2 == 0 else S["f4"][:, TH:T]
            xdeps = [o_spills[Ic], ar4,
                     oq_ops[-2] if len(oq_ops) >= 2 else None]
            if Ic == 0:
                xdeps += st["s1_dmas"][-1:]
            if Ic < 2 and th == 1:
                xdeps.append(last_mmd)
            ld = p.dma("q", stg,
                       dap(dr["o_sp"], Ic * 128 * T + th * TH, [[T, 128], [1, TH]]),
                       deps=xdeps, note="dma_old")
            q = quant3_col(p, S["OQ"][:, Ic, :], stg,
                           S["scal"][:, S_RS4:S_RS4 + 1], -32768.0, 32767.0,
                           [ld, oq_ops[-1] if oq_ops else last_x2],
                           S["f1"][:, 0:TH])
            oq_ops.append(q)
        for dc in range(DC):
            mms = {0: [], 1: []}
            b = dc % 2
            wb = S["wt2"] if b == 0 else S["wt3"]
            wl = p.dma("q", wb[:, 0:IC1, :],
                       dap(dr["ag_out"], dc * AGN + seg["downT"],
                           [[128, 128], [128 * 128, IC1], [1, 128]]),
                       deps=[st["ag"], dn_last[b] if dn_last[b] is not None
                             else st["x1_dmas"][-1]], note="dma_wd")
            for Ic in range(IC1):
                for tq2 in range(2):
                    m = p.op("t", lambda e, wb=wb, tq2=tq2, Ic=Ic: e.matmul(
                        S[banks[tq2]][:, :], wb[:, Ic, :], S["OQ"][:, Ic, ts(tq2, 512)],
                        start=(Ic == 0), stop=(Ic == IC1 - 1)),
                        [wl, oq_ops[Ic]] + ([out_dmas[-1]] if Ic == 0 and out_dmas
                                            else []), "mmd")
                    mms[tq2].append(m)
                    last_mmd = m
            dn_last[b] = last_mmd
            for tq2 in range(2):
                gt = th * 2 + tq2
                sil = p.op("s", lambda e, tq2=tq2: e.activation(
                    S["f1"][:, 0:512], S[banks[tq2]][:, :], AF.Silu,
                    scale=S["scal"][:, S_S4:S_S4 + 1]),
                    [mms[tq2][-1], last_x2], "sild")
                s1ld = p.dma("q", S["f3"][:, 0:512],
                             dap(dr["s1_sp"], dc * 128 * T + gt * 512,
                                 [[T, 128], [1, 512]]),
                             deps=[st["s1_dmas"][dc * NTQ + gt],
                                   out_dmas[-4] if len(out_dmas) >= 4 else None],
                             note="dma_s1ld")
                sgd = p.op("v", lambda e: e.tensor_mul(
                    S["f1"][:, 0:512], S["f1"][:, 0:512], S["f3"][:, 0:512]),
                    [sil, s1ld], "sgd")
                last_sgd = sgd
                x1ld = p.dma("q", S["f2"][:, 0:512],
                             dap(dr["x1_sp"], dc * 128 * T + gt * 512,
                                 [[T, 128], [1, 512]]),
                             deps=[st["x1_dmas"][st["x1_idx"](dc, gt)], sgd, last_tpo],
                             note="dma_x1f")
                x2 = p.op("v", lambda e: e.tensor_add(S["f2"][:, 0:512], S["f2"][:, 0:512],
                                                      S["f1"][:, 0:512]), [x1ld, sgd], "x2")
                last_x2 = x2
                for j in range(4):
                    bank = "pT" if j % 2 == 0 else "pT2"
                    tp = p.op("t", lambda e, j=j, bank=bank: e.transpose(
                        S[bank][:, :], S["f2"][:, ts(j, 128)], S["identF"][:, :]),
                        [x2, out_dmas[-2] if len(out_dmas) >= 2 else None], "tpo")
                    last_tpo = tp
                    cp = p.op("v", lambda e, j=j, bank=bank: e.tensor_copy(
                        S["f3"][:, 512 + j * 128:512 + (j + 1) * 128], S[bank][:, :]),
                        [tp, out_dmas[-4] if len(out_dmas) >= 4 else None], "cpo")
                    out_dmas.append(p.dma("q",
                                          dr["out"].ap()[gt * 512 + j * 128:
                                                         gt * 512 + (j + 1) * 128,
                                                         ts(dc, 128)],
                                          S["f3"][:, 512 + j * 128:512 + (j + 1) * 128],
                                          deps=[cp], note="dma_out"))
    st["out_dmas"] = out_dmas
    p.free("OQ")


def host_inputs(inputs, core):
    def cols(a):
        return np.ascontiguousarray(np.asarray(a, dtype=np.float32).reshape(8, 128).T)
    cw = np.asarray(inputs["L0_ffn_conv_w"], dtype=np.float32)[:, 0, :]
    cwp = cw.reshape(32, 128, 3).transpose(1, 0, 2)
    def sl(a):
        a = np.asarray(a, dtype=np.float32)
        n = a.shape[0] // NCORES
        return np.ascontiguousarray(a[core * n:(core + 1) * n])
    return {
        "x": np.ascontiguousarray(np.asarray(inputs["L0_attn_x"], dtype=np.float32)[core]),
        "bvq_c": cols(np.asarray(inputs["L0_attn_bv_q"]).reshape(-1)),
        "bvk_c": cols(np.asarray(inputs["L0_attn_bv_k"]).reshape(-1)),
        "bvv_c": cols(np.asarray(inputs["L0_attn_bv_v"]).reshape(-1)),
        "cw0_c": np.ascontiguousarray(cwp[:, :, 0]),
        "cw1_c": np.ascontiguousarray(cwp[:, :, 1]),
        "cw2_c": np.ascontiguousarray(cwp[:, :, 2]),
        "cb_c": np.ascontiguousarray(
            np.asarray(inputs["L0_ffn_conv_b"], dtype=np.float32).reshape(32, 128).T),
        "td_c": cols(inputs["L1_attn_time_decay"]),
        "tf_c": cols(inputs["L1_attn_time_first"]),
        "tmk_c": cols(np.asarray(inputs["L1_attn_time_mix_k"]).reshape(-1)),
        "tmv_c": cols(np.asarray(inputs["L1_attn_time_mix_v"]).reshape(-1)),
        "tmr_c": cols(np.asarray(inputs["L1_attn_time_mix_r"]).reshape(-1)),
        "w1_sh": sl(inputs["L0_ffn_w1"]),
        "w2_sh": sl(inputs["L0_ffn_w2"]),
        "wup_sh": sl(inputs["L1_ffn_w_up"]),
        "wgate_sh": sl(inputs["L1_ffn_w_gate"]),
        "wdown_sh": sl(inputs["L1_ffn_w_down"]),
        "wk_sh": sl(inputs["L1_attn_Wk"]),
        "wv_sh": sl(inputs["L1_attn_Wv"]),
        "wr_sh": sl(inputs["L1_attn_Wr"]),
        "wo_sh": sl(inputs["L1_attn_Wo"]),
    }


def kernel(**inputs):
    nc, p = build(debug_stop=DEBUG_STOP, debug_dumps=DEBUG_DUMPS)
    in_maps = [host_inputs(inputs, c) for c in range(NCORES)]
    res = run_bass_kernel_spmd(nc, in_maps, core_ids=list(range(NCORES)))
    return np.stack([np.asarray(res.results[c]["out"]) for c in range(NCORES)], axis=0)


# version 19
# speedup vs baseline: 1.0336x; 1.0329x over previous
import contextlib
import numpy as np
import concourse.bass as bass
import concourse.mybir as mybir
from concourse.bass_utils import run_bass_kernel_spmd

F32 = mybir.dt.float32
BF16 = mybir.dt.bfloat16
ALU = mybir.AluOpType
AF = mybir.ActivationFunctionType
AX = mybir.AxisListType.X

T = 2048
TH = 1024
D = 1024
DC = 8
H0 = 4096
HC0 = 32
I1 = 3072
IC1 = 24
NTQ = 4
RND = 12582912.0
CAP = 60.0
NCORES = 8
EPS_LN = 1e-5

DEBUG_STOP = None
DEBUG_DUMPS = ()

CHANS = ("t", "v", "s", "g", "q", "gq", "aq", "c")
ENG_OF = {"t": "t", "v": "v", "s": "s", "g": "g", "q": "q", "gq": "g", "aq": "s", "c": "g"}
ASYNC = ("q", "gq", "aq", "c")
NSUB = {"q": 16, "gq": 8, "aq": 4}


class Op:
    __slots__ = ("chan", "fn", "deps", "seq", "sig", "note", "sub")

    def __init__(self, chan, fn, deps, note):
        self.chan = chan
        self.fn = fn
        self.deps = deps
        self.seq = None
        self.sig = False
        self.note = note
        self.sub = 0


def is_dma(chan):
    return chan in ("q", "gq", "aq")


class P:
    def __init__(self, nc):
        self.nc = nc
        self.ops = []
        self.open_cm = {}
        self.sb = {}
        self.dr = {}
        self.seg = {}

    def alloc(self, name, shape, dtype, psum=False, side="left"):
        if psum:
            cm = self.nc.psum_tensor(name, shape, dtype)
        else:
            cm = self.nc.sbuf_tensor(name, shape, dtype, side=side)
        h = cm.__enter__()
        self.open_cm[name] = cm
        self.sb[name] = h
        return h

    def free(self, *names):
        for name in names:
            self.open_cm.pop(name).__exit__(None, None, None)

    def op(self, chan, fn, deps=(), note=""):
        o = Op(chan, fn, [d for d in deps if d is not None], note)
        self.ops.append(o)
        return o

    def dma(self, chan, out, in_, deps=(), note="dma", transpose=False):
        assert is_dma(chan)
        def fn(e):
            return e.dma_start(out=out, in_=in_, transpose=transpose)
        return self.op(chan, fn, deps, note)

    def barrier_deps(self):
        last = {}
        for o in self.ops:
            last[o.chan] = o
        return list(last.values())

    def emit(self, sems):
        for o in self.ops:
            if is_dma(o.chan) or o.chan in ("c", "v", "s", "g"):
                o.sig = True
            for d in o.deps:
                d.sig = True
        counters = {c: 0 for c in CHANS}
        subcnt = {}
        for o in self.ops:
            if not o.sig:
                continue
            if is_dma(o.chan):
                idx = counters[o.chan]
                counters[o.chan] += 1
                ns = NSUB[o.chan]
                o.sub = idx % ns
                k = subcnt.get((o.chan, o.sub), 0) + 1
                subcnt[(o.chan, o.sub)] = k
                o.seq = 16 * k
            else:
                counters[o.chan] += 1
                o.seq = counters[o.chan]
        eng_ops = {e: [] for e in ("t", "v", "s", "g", "q")}
        for o in self.ops:
            eng_ops[ENG_OF[o.chan]].append(o)

        def semof(chan, sub):
            return sems[chan][sub] if is_dma(chan) else sems[chan]

        def emit_engine(e, handle):
            waited = {}
            for o in eng_ops[e]:
                if o.chan in ("v", "s", "g") and o.seq is not None and o.seq > 1:
                    key = (o.chan, 0)
                    if o.seq - 1 > waited.get(key, 0):
                        handle.wait_ge(semof(o.chan, 0), o.seq - 1)
                        waited[key] = o.seq - 1
                for d in o.deps:
                    key = (d.chan, d.sub)
                    if d.seq > waited.get(key, 0):
                        handle.wait_ge(semof(d.chan, d.sub), d.seq)
                        waited[key] = d.seq
                if o.sig and is_dma(o.chan) and o.seq > 16:
                    key = (o.chan, o.sub)
                    if o.seq - 16 > waited.get(key, 0):
                        handle.wait_ge(semof(o.chan, o.sub), o.seq - 16)
                        waited[key] = o.seq - 16
                ins = o.fn(handle)
                if o.sig:
                    assert ins is not None, o.note
                    ins.then_inc(semof(o.chan, o.sub), 16 if is_dma(o.chan) else 1)

        with self.nc.Block() as b:
            @b.tensor
            def _(t):
                emit_engine("t", t)

            @b.vector
            def _(v):
                emit_engine("v", v)

            @b.scalar
            def _(s):
                emit_engine("s", s)

            @b.gpsimd
            def _(g):
                emit_engine("g", g)

            @b.sync
            def _(q):
                emit_engine("q", q)


def dap(dr_tensor, offset, pattern):
    return bass.AP(dr_tensor, offset, pattern)


def ts(i, w):
    return slice(i * w, (i + 1) * w)


S_SW1, S_SW2, S_RW1, S_RW2 = 0, 1, 2, 3
S_RS0, S_S0, S_RS1, S_S1 = 4, 5, 6, 7
S_RS2, S_S2, S_RS3, S_S3 = 8, 9, 10, 11
S_RS4, S_S4, S_S265536, S_S3S1 = 12, 13, 14, 15


def build(debug_stop=None, debug_dumps=(), debug_dram=()):
    nc = bass.Bass("TRN2", target_bir_lowering=False)
    p = P(nc)
    dr = p.dr
    dr["x"] = nc.dram_tensor("x", [T, D], F32, kind="ExternalInput")
    dr["out"] = nc.dram_tensor("out", [T, D], F32, kind="ExternalOutput")
    for nm in ["bvq_c", "bvk_c", "bvv_c", "td_c", "tf_c", "tmk_c", "tmv_c", "tmr_c"]:
        dr[nm] = nc.dram_tensor(nm, [128, 8], F32, kind="ExternalInput")
    for nm in ["cw0_c", "cw1_c", "cw2_c", "cb_c"]:
        dr[nm] = nc.dram_tensor(nm, [128, 32], F32, kind="ExternalInput")
    dr["w1_sh"] = nc.dram_tensor("w1_sh", [512, 1024], F32, kind="ExternalInput")
    dr["w2_sh"] = nc.dram_tensor("w2_sh", [128, 4096], F32, kind="ExternalInput")
    dr["wup_sh"] = nc.dram_tensor("wup_sh", [384, 1024], F32, kind="ExternalInput")
    dr["wgate_sh"] = nc.dram_tensor("wgate_sh", [384, 1024], F32, kind="ExternalInput")
    dr["wdown_sh"] = nc.dram_tensor("wdown_sh", [128, 3072], F32, kind="ExternalInput")
    for nm in ["wk_sh", "wv_sh", "wr_sh", "wo_sh"]:
        dr[nm] = nc.dram_tensor(nm, [128, 1024], F32, kind="ExternalInput")

    seg = p.seg
    off = 0
    for nm, n in [("w1T", 1024 * 512), ("w2T", 4096 * 128), ("upT", 1024 * 384),
                  ("gateT", 1024 * 384), ("downT", 3072 * 128), ("kT", 1024 * 128),
                  ("vT", 1024 * 128), ("rT", 1024 * 128), ("oT", 1024 * 128)]:
        seg[nm] = off
        off += n
    p.AGN = off
    dr["ag_in"] = nc.dram_tensor("ag_in", [p.AGN], BF16)
    dr["ag_out"] = nc.dram_tensor("ag_out", [NCORES * p.AGN], BF16)
    dr["ar_in"] = nc.dram_tensor("ar_in", [8], F32)
    dr["ar_out"] = nc.dram_tensor("ar_out", [8], F32)
    dr["hsil_sp"] = nc.dram_tensor("hsil_sp", [H0 * T], BF16)
    dr["up_sp"] = nc.dram_tensor("up_sp", [I1 * T], F32)
    dr["gate_sp"] = nc.dram_tensor("gate_sp", [I1 * T], F32)
    dr["gate2_sp"] = nc.dram_tensor("gate2_sp", [I1 * T], F32)
    dr["o_sp"] = nc.dram_tensor("o_sp", [I1 * T], F32)
    dr["x1_sp"] = nc.dram_tensor("x1_sp", [D * T], F32)
    dr["xn_sp"] = nc.dram_tensor("xn_sp", [D * T], F32)
    dr["xn1_sp"] = nc.dram_tensor("xn1_sp", [D * T], F32)
    dr["s1_sp"] = nc.dram_tensor("s1_sp", [D * T], F32)

    p.alloc("identF", [128, 128], F32)
    p.alloc("identB", [128, 128], BF16)
    p.alloc("ones", [128, 128], F32)
    for nm in ["sqk_c", "sv_c", "w_c", "nu_c", "tmk", "tmv", "tmr"]:
        p.alloc(nm, [128, 8], F32)
    for nm in ["cw0", "cw1", "cw2", "cb"]:
        p.alloc(nm, [128, 32], F32)
    p.alloc("scal", [128, 16], F32)
    p.alloc("epsc", [128, 1], F32)
    p.alloc("rndc", [128, 2], F32)
    p.alloc("stat", [128, 64], F32)
    p.alloc("tiny", [128, 16], F32)
    p.alloc("tiny2", [128, 16], F32)
    p.alloc("tinyb", [128, 16], BF16)
    p.alloc("mxc", [128, 64], F32)
    p.alloc("flatrow", [1, 512], F32)
    p.alloc("flat16", [16, 128], F32)
    p.alloc("f1", [128, T], F32)
    p.alloc("f2", [128, T], F32)
    p.alloc("f3", [128, T], F32)
    p.alloc("g1s", [128, T], BF16)
    p.alloc("g2s", [128, T], BF16)
    p.alloc("wt0", [128, DC, 128], BF16)
    p.alloc("wt1", [128, DC, 128], BF16)
    p.alloc("wt2", [128, HC0, 128], BF16, side="right")
    p.alloc("wt3", [128, HC0, 128], BF16, side="right")
    p.alloc("pA", [128, 512], F32, psum=True)
    p.alloc("pB", [128, 512], F32, psum=True)
    p.alloc("pC", [128, 512], F32, psum=True)
    p.alloc("pD", [128, 512], F32, psum=True)
    p.alloc("pE", [128, 512], F32, psum=True)
    p.alloc("pT", [128, 128], F32, psum=True)
    p.alloc("pT2", [128, 128], F32, psum=True)
    p.alloc("pTb", [128, 128], BF16, psum=True)

    st = {}
    ph0(p, st)
    ph1(p, st)
    if debug_stop != "p1":
        ph2(p, st)
        if debug_stop != "p2":
            ph3(p, st)
            if debug_stop != "p3":
                ph4(p, st)
                if debug_stop != "p4":
                    ph5(p, st)
                    if debug_stop != "p5":
                        ph6(p, st)

    if debug_dram:
        bd = p.barrier_deps()
        dmp = []
        for nm in debug_dram:
            t = dr[nm]
            n = int(np.prod(t.shape))
            dt = nc.dram_tensor("ddump_" + nm, [n], t.dtype, kind="ExternalOutput")
            dmp.append(p.dma("q", dt.ap(), t.ap(), deps=bd, note="dma_ddump"))
        p.op("g", lambda e: e.nop(), deps=dmp, note="ddump_wait")
    if debug_dumps:
        bd = p.barrier_deps()
        dmp = []
        for nm in debug_dumps:
            t = p.sb[nm]
            shape = list(t.shape)
            flat = [shape[0], int(np.prod(shape[1:]))]
            dt = nc.dram_tensor("dump_" + nm, flat, t.dtype, kind="ExternalOutput")
            ap = t[:, :] if len(shape) == 2 else t.ap().rearrange("p a b -> p (a b)")
            dmp.append(p.dma("q", dt.ap(), ap, deps=bd, note="dma_dump"))
        p.op("g", lambda e: e.nop(), deps=dmp, note="dump_wait")

    if "out_dmas" in st:
        p.op("g", lambda e: e.nop(), deps=st["out_dmas"][-64:], note="final_wait")

    stk = contextlib.ExitStack()
    sems = {}
    for c in CHANS:
        if c in NSUB:
            sems[c] = [stk.enter_context(nc.semaphore(f"sem_{c}{i}"))
                       for i in range(NSUB[c])]
        else:
            sems[c] = stk.enter_context(nc.semaphore("sem_" + c))
    p.emit(sems)
    for name in reversed(list(p.open_cm)):
        p.open_cm[name].__exit__(None, None, None)
    p.open_cm.clear()
    stk.close()
    return nc, p


def quant3_col_g(p, out_ap, in_ap, scale_col, lo, hi, deps, tmp):
    return quant3_col(p, out_ap, in_ap, scale_col, lo, hi, deps, tmp)


def quant3_col(p, out_ap, in_ap, scale_col, lo, hi, deps, tmp, chan3="v"):
    S = p.sb
    o1 = p.op("s", lambda e: e.activation(tmp, in_ap, AF.Copy, scale=scale_col,
                                          bias=RND), deps, "q3a")
    o2 = p.op("s", lambda e: e.activation(tmp, tmp, AF.Copy, bias=-RND), [o1], "q3b")
    return p.op("v", lambda e: e.tensor_scalar(out_ap, tmp, hi, lo, ALU.min, ALU.max),
                [o2], "q3c")


def quant3_row(p, out_ap, in_ap, row_ap, lo, hi, deps, tmp, chans=None):
    S = p.sb
    o1 = p.op("v", lambda e: e.tensor_mul(tmp, in_ap, row_ap), deps, "qra")
    o2 = p.op("s", lambda e: e.activation(tmp, tmp, AF.Copy, bias=RND), [o1], "qrb1")
    o3 = p.op("s", lambda e: e.activation(tmp, tmp, AF.Copy, bias=-RND), [o2], "qrb2")
    return p.op("v", lambda e: e.tensor_scalar(out_ap, tmp, hi, lo, ALU.min, ALU.max),
                [o3], "qrc")


def bcast_p0(p, src_row, dst, n, deps, tag):
    S = p.sb
    mm = p.op("t", lambda e: e.matmul(S["pT2"][:, 0:n], S["ones"][0:1, :], src_row,
                                      start=True, stop=True), deps, "bc_mm_" + tag)
    return p.op("v", lambda e: e.tensor_copy(dst, S["pT2"][:, 0:n]), [mm], "bc_cp_" + tag)


def row_bcast(p, st, cols16, dst, deps, tag):
    S = p.sb
    guard = getattr(p, "rb_guard", [])
    tp = p.op("t", lambda e: e.transpose(S["pT2"][0:16, 0:128], cols16, S["identF"][:, :]),
              list(deps) + [st["consts"]] + guard, "rb_tp_" + tag)
    cp = p.op("v", lambda e: e.tensor_copy(S["flat16"][0:16, :], S["pT2"][0:16, 0:128]),
              [tp], "rb_cp_" + tag)
    last = None
    allfls = []
    for tq in range(NTQ):
        fls = []
        for r in range(4):
            i = tq * 4 + r
            fls.append(p.dma("gq", S["flatrow"][0:1, ts(r, 128)],
                             S["flat16"][i:i + 1, :], deps=[cp, last], note="dma_flat"))
        mm = p.op("t", lambda e: e.matmul(S["pE"][:, :], S["ones"][0:1, :],
                                          S["flatrow"][0:1, :], start=True, stop=True),
                  fls + [last], "rb_mm_" + tag)
        last = p.op("v", lambda e, tq=tq: e.tensor_copy(dst[:, ts(tq, 512)], S["pE"][:, :]),
                    [mm], "rb_ev_" + tag)
        allfls += fls
    p.rb_guard = allfls
    return last


def global_max_ar(p, st, cols_ap, slot, s_idx, rs_idx, tag, deps):
    S = p.sb
    dr = p.dr
    cl = p.op("v", lambda e: e.reduce_max(S["mxc"][:, 62:63], cols_ap, AX),
              list(deps) + [st["ar_last"]], "cl_" + tag)
    tp = p.op("t", lambda e: e.transpose(S["pT2"][0:1, 0:128], S["mxc"][:, 62:63],
                                         S["identF"][:, :]), [cl], "tp_" + tag)
    rd = p.op("v", lambda e: e.reduce_max(S["mxc"][0:1, 63:64], S["pT2"][0:1, 0:128], AX),
              [tp], "rd_" + tag)
    d1 = p.dma("gq", dr["ar_in"].ap()[slot:slot + 1], S["mxc"][0:1, 63:64], deps=[rd],
               note="dma_" + tag)
    ar = p.op("c", lambda e: e.collective_compute(
        "AllReduce", ALU.max, replica_groups=[list(range(NCORES))],
        ins=[dr["ar_in"].ap()[slot:slot + 1].opt()],
        outs=[dr["ar_out"].ap()[slot:slot + 1].opt()]), [d1, st["ar_last"]], "AR_" + tag)
    st["ar_last"] = ar
    lb = p.dma("gq", S["mxc"][0:1, 62:63], dr["ar_out"].ap()[slot:slot + 1],
               deps=[ar, tp], note="dma_b" + tag)
    bc = bcast_p0(p, S["mxc"][0:1, 62:63], S["scal"][:, s_idx:s_idx + 1], 1, [lb], tag)
    sc1 = p.op("v", lambda e: e.tensor_scalar(S["scal"][:, s_idx:s_idx + 1],
                                              S["scal"][:, s_idx:s_idx + 1],
                                              1.0 / 32767.0, 1e-12,
                                              ALU.mult, ALU.max), [bc], "s_" + tag)
    return p.op("v", lambda e: e.reciprocal(S["scal"][:, rs_idx:rs_idx + 1],
                                            S["scal"][:, s_idx:s_idx + 1]), [sc1],
                "rs_" + tag)


def wts_sel(S, i):
    return S["wt0"] if i % 2 == 0 else S["wt1"]


def ph0(p, st):
    S = p.sb
    dr = p.dr
    x = dr["x"].ap()

    p.alloc("SA", [128, DC, T], F32)
    p.alloc("XT", [128, 16, 1024], F32)
    p.alloc("xn_t0", [128, 1024], F32)
    p.alloc("xn_t1", [128, 1024], F32)
    p.alloc("sqd", [128, 1024], F32)

    prm = {}
    for srcn, dstn in [("bvq_c", "sqk_c"), ("bvv_c", "sv_c"), ("cw0_c", "cw0"),
                       ("cw1_c", "cw1"), ("cw2_c", "cw2"), ("cb_c", "cb"),
                       ("td_c", "w_c"), ("tf_c", "nu_c"), ("tmk_c", "tmk"),
                       ("tmv_c", "tmv"), ("tmr_c", "tmr")]:
        prm[dstn] = p.dma("gq", S[dstn][:, :], dr[srcn].ap(), note="dma_prm")
    bvk = None

    cz1 = p.op("g", lambda e: e.memset(S["identF"][:, :], 0.0), (), "czF")
    ca1 = p.op("g", lambda e: e.affine_select(
        S["identF"][:, :], S["identF"][:, :], [[-1, 128]], ALU.not_equal, 1.0,
        base=0, channel_multiplier=1), [cz1], "caF")
    cz2 = p.op("g", lambda e: e.memset(S["identB"][:, :], 0.0), (), "czB")
    ca2 = p.op("g", lambda e: e.affine_select(
        S["identB"][:, :], S["identB"][:, :], [[-1, 128]], ALU.not_equal, 1.0,
        base=0, channel_multiplier=1), [cz2], "caB")
    cz3 = p.op("g", lambda e: e.memset(S["ones"][:, :], 1.0), (), "czO")
    cz4 = p.op("g", lambda e: e.memset(S["epsc"][:, :], EPS_LN), (), "czE")
    czr = p.op("g", lambda e: e.memset(S["rndc"][:, 0:1], RND), (), "czR1")
    czr2 = p.op("g", lambda e: e.memset(S["rndc"][:, 1:2], -RND), (), "czR2")
    cz5 = p.op("g", lambda e: e.memset(S["scal"][:, :], 0.0), (), "czS")
    cz6 = p.op("g", lambda e: e.memset(S["mxc"][:, :], 0.0), (), "czM")
    cz7 = p.op("g", lambda e: e.memset(S["stat"][:, :], 0.0), (), "czT")
    cz8 = p.op("g", lambda e: e.memset(S["tiny"][:, :], 0.0), (), "czU")
    cz9 = p.op("g", lambda e: e.memset(S["tiny2"][:, :], 0.0), (), "czV")
    consts = p.op("g", lambda e: e.engine_nop(),
                  [ca1, ca2, cz3, cz4, cz5, cz6, cz7, cz8, cz9, czr, czr2], "consts")
    st["consts"] = consts
    bvk = p.dma("gq", S["tiny2"][:, 0:8], dr["bvk_c"].ap(), deps=[cz9], note="dma_prm")

    s1 = p.op("s", lambda e: e.sign(S["sqk_c"][:, :], S["sqk_c"][:, :]), [prm["sqk_c"]], "sgq")
    s2 = p.op("s", lambda e: e.sign(S["tiny2"][:, 0:8], S["tiny2"][:, 0:8]), [bvk], "sgk")
    s3 = p.op("s", lambda e: e.sign(S["sv_c"][:, :], S["sv_c"][:, :]), [prm["sv_c"]], "sgv")
    st["sqk"] = p.op("v", lambda e: e.tensor_mul(S["sqk_c"][:, :], S["sqk_c"][:, :],
                                                 S["tiny2"][:, 0:8]), [s1, s2], "sqkm")
    st["sv"] = s3
    w1e = p.op("s", lambda e: e.activation(S["w_c"][:, :], S["w_c"][:, :], AF.Exp),
               [prm["w_c"]], "wexp")
    st["wc"] = p.op("v", lambda e: e.tensor_scalar(S["w_c"][:, :], S["w_c"][:, :], -1.0,
                                                   None, ALU.mult), [w1e], "wneg")
    st["nu"] = p.op("v", lambda e: e.tensor_scalar(S["nu_c"][:, :], S["nu_c"][:, :], -1.0,
                                                   None, ALU.mult), [prm["nu_c"]], "nuneg")

    loads, sq_ops, sm_ops = [], [], []
    for i in range(16):
        ld = p.dma("q", S["XT"][:, i, :], x[ts(i, 128), :], note="dma_x")
        loads.append(ld)
        sq1 = p.op("v", lambda e, i=i: e.tensor_mul(
            S["sqd"][:, :], S["XT"][:, i, :], S["XT"][:, i, :]), [ld, consts], "sq1")
        sq_ops.append(p.op("v", lambda e, i=i: e.reduce_sum(
            S["stat"][:, 16 + i:17 + i], S["sqd"][:, :], AX), [sq1], "sq2"))
        sm_ops.append(p.op("v", lambda e, i=i: e.reduce_sum(
            S["stat"][:, i:i + 1], S["XT"][:, i, :], AX), [ld, consts], "sm"))
    f1_ = p.op("v", lambda e: e.tensor_scalar(S["stat"][:, 0:16], S["stat"][:, 0:16],
                                              1.0 / D, None, ALU.mult), sm_ops, "mean")
    f2_ = p.op("v", lambda e: e.tensor_scalar(S["stat"][:, 16:32], S["stat"][:, 16:32],
                                              1.0 / D, None, ALU.mult), sq_ops, "ex2")
    f3_ = p.op("v", lambda e: e.tensor_mul(S["tiny"][:, 0:16], S["stat"][:, 0:16],
                                           S["stat"][:, 0:16]), [f1_], "m2")
    f4_ = p.op("v", lambda e: e.tensor_sub(S["stat"][:, 16:32], S["stat"][:, 16:32],
                                           S["tiny"][:, 0:16]), [f2_, f3_], "var")
    f5_ = p.op("s", lambda e: e.activation(S["stat"][:, 16:32], S["stat"][:, 16:32],
                                           AF.Sqrt, bias=S["epsc"][:, 0:1]),
               [f4_, st["consts"]], "sd")
    f6_ = p.op("v", lambda e: e.reciprocal(S["stat"][:, 16:32], S["stat"][:, 16:32]),
               [f5_], "rstd")
    nmean = p.op("v", lambda e: e.tensor_scalar(S["tiny"][:, 0:16], S["stat"][:, 0:16],
                                                -1.0, None, ALU.mult), [f6_], "nmean")

    prev_cp = {}
    last_mx = None
    for i in range(16):
        buf = S["xn_t0"] if i % 2 == 0 else S["xn_t1"]
        nrm = p.op("v", lambda e, i=i, buf=buf: e.tensor_scalar(
            buf[:, :], S["XT"][:, i, :], S["tiny"][:, i:i + 1],
            S["stat"][:, 16 + i:17 + i], ALU.add, ALU.mult),
            [nmean, prev_cp.get((i - 2, DC - 1))], "nrm")
        last_mx = p.op("v", lambda e, i=i, buf=buf: e.reduce_max(
            S["stat"][:, 32 + i:33 + i], buf[:, :], AX, apply_absolute_value=True),
            [nrm], "sxmax")
        for dc in range(DC):
            bank = "pT" if (i * DC + dc) % 2 == 0 else "pT2"
            tp = p.op("t", lambda e, buf=buf, dc=dc, bank=bank: e.transpose(
                S[bank][:, :], buf[:, ts(dc, 128)], S["identF"][:, :]),
                [nrm, consts, prev_cp.get((i, dc - 2) if dc >= 2 else (i - 1, DC - 2 + dc))],
                "tpxn")
            cp = p.op("s", lambda e, i=i, dc=dc, bank=bank: e.copy(
                S["SA"][:, dc, ts(i, 128)], S[bank][:, :]), [tp], "cpxn")
            prev_cp[(i, dc)] = cp
    st["xn_done"] = [prev_cp[(15, DC - 1)], prev_cp[(15, DC - 2)], last_mx]
    st["sx_done"] = last_mx
    st["xt_free"] = [nrm, last_mx, sq_ops[-1], sm_ops[-1]]
    p.free("sqd", "xn_t1", "xn_t0", "XT")


def ph1(p, st):
    S = p.sb
    dr = p.dr
    seg = p.seg
    AGN = p.AGN

    p.alloc("SXB", [128, T], BF16)
    p.alloc("RSXW1B", [128, T], F32)
    p.alloc("ws0", [128, 4096], F32)
    p.alloc("ws2", [128, 4096], F32)
    p.alloc("wq0", [128, 4096], BF16)
    p.alloc("wqt", [128, 32, 128], BF16)
    p.alloc("lt", [128, 96], F32)
    p.alloc("lt2", [128, 96], F32)

    ag_parts = []
    ld_w1 = [p.dma("q", S["ws0"][:, ts(j, 1024)], dr["w1_sh"].ap()[ts(j, 128), :],
                   deps=st["xt_free"], note="dma_w1s") for j in range(4)]
    ld_w2 = p.dma("q", S["ws2"][:, :], dr["w2_sh"].ap()[:, :], deps=st["xt_free"],
                  note="dma_w2s")
    a1 = p.op("v", lambda e: e.reduce_sum(S["tiny"][:, 0:1], S["ws0"][:, :], AX,
                                          apply_absolute_value=True), ld_w1, "absw1")
    a2 = p.op("v", lambda e: e.reduce_sum(S["tiny"][:, 1:2], S["ws2"][:, :], AX,
                                          apply_absolute_value=True), [ld_w2], "absw2")
    mm = p.op("t", lambda e: e.matmul(S["pT2"][:, 0:2], S["ones"][:, :], S["tiny"][:, 0:2],
                                      start=True, stop=True),
              [a1, a2, st["consts"]] + st["xn_done"], "summ")
    cps = p.op("v", lambda e: e.tensor_copy(S["mxc"][:, 56:58], S["pT2"][:, 0:2]), [mm], "sumc")
    d1 = p.dma("gq", dr["ar_in"].ap()[0:2], S["mxc"][0:1, 56:58], deps=[cps], note="dma_ar")
    ar = p.op("c", lambda e: e.collective_compute(
        "AllReduce", ALU.add, replica_groups=[list(range(NCORES))],
        ins=[dr["ar_in"].ap()[0:2].opt()], outs=[dr["ar_out"].ap()[0:2].opt()]),
        [d1], "AR_means")
    st["ar_last"] = ar
    lb = p.dma("gq", S["mxc"][0:1, 58:60], dr["ar_out"].ap()[0:2], deps=[ar], note="dma_arb")
    bc = bcast_p0(p, S["mxc"][0:1, 58:60], S["scal"][:, S_RW1:S_RW1 + 2], 2, [lb], "wm")
    c1 = p.op("v", lambda e: e.tensor_scalar(S["scal"][:, S_RW1:S_RW1 + 2],
                                             S["scal"][:, S_RW1:S_RW1 + 2],
                                             1.0 / (4096.0 * 1024.0),
                                             1e-5, ALU.mult, ALU.max), [bc], "wmc")
    sw = p.op("v", lambda e: e.reciprocal(S["scal"][:, S_SW1:S_SW1 + 2],
                                          S["scal"][:, S_RW1:S_RW1 + 2]), [c1], "wmr")
    st["sw"] = sw

    prev_ag = None
    for j in range(4):
        qd = quant3_col(p, S["wq0"][:, ts(j, 1024)], S["ws0"][:, ts(j, 1024)],
                        S["scal"][:, S_SW1:S_SW1 + 1], -1.0, 1.0, [sw, ld_w1[j]],
                        S["ws0"][:, ts(j, 1024)])
        tps = [p.dma("q", S["wqt"][:, dc, :],
                     S["wq0"][:, j * 1024 + dc * 128:j * 1024 + (dc + 1) * 128],
                     deps=[qd, prev_ag], note="dma_tp", transpose=True) for dc in range(8)]
        prev_ag = p.dma("gq", dap(dr["ag_in"], seg["w1T"] + j * 128,
                                  [[512, 128], [128 * 512, 8], [1, 128]]),
                        S["wqt"][:, 0:8, :], deps=tps, note="dma_agw")
        ag_parts.append(prev_ag)

    qd2 = quant3_col(p, S["wq0"][:, :], S["ws2"][:, :], S["scal"][:, S_SW2:S_SW2 + 1],
                     -1.0, 1.0, [sw, ld_w2, prev_ag], S["ws2"][:, :])
    tps = [p.dma("q", S["wqt"][:, hc, :], S["wq0"][:, ts(hc, 128)],
                 deps=[qd2, prev_ag], note="dma_tp", transpose=True) for hc in range(32)]
    prev_ag = p.dma("gq", dap(dr["ag_in"], seg["w2T"],
                              [[128, 128], [128 * 128, 32], [1, 128]]),
                    S["wqt"][:, 0:32, :], deps=tps, note="dma_agw2")
    ag_parts.append(prev_ag)

    def lpbq(src, rows, width, segnm, prev_ag, prev_m4):
        ntile = rows // 128
        nblk = width // 32
        nw = width // 128
        outs = []
        for j in range(ntile):
            ld = p.dma("q", S["ws0"][:, 0:width], src.ap()[ts(j, 128), :],
                       deps=[prev_m4], note="dma_lw")
            w3 = S["ws0"][:, 0:width].rearrange("p (b i) -> p b i", i=32)
            bm = p.op("v", lambda e, w3=w3, nblk=nblk: e.reduce_max(
                S["lt"][:, 0:nblk], w3, AX, apply_absolute_value=True), [ld], "lpbm")
            sc = p.op("v", lambda e, nblk=nblk: e.tensor_scalar(
                S["lt"][:, 0:nblk], S["lt"][:, 0:nblk], 1.0 / 7.0, 1e-12,
                ALU.mult, ALU.max), [bm], "lps")
            rc = p.op("v", lambda e, nblk=nblk: e.reciprocal(
                S["lt2"][:, 0:nblk], S["lt"][:, 0:nblk]), [sc], "lprs")
            rs_exp = dap(S["lt2"], 0, [[96, 128], [1, nblk], [0, 32]])
            s_exp = dap(S["lt"], 0, [[96, 128], [1, nblk], [0, 32]])
            m1 = p.op("v", lambda e, w3=w3, rs_exp=rs_exp: e.tensor_tensor(
                w3, w3, rs_exp, ALU.mult), [rc], "lpm1")
            m2 = p.op("v", lambda e, width=width: e.tensor_scalar(
                S["ws0"][:, 0:width], S["ws0"][:, 0:width], RND, -RND, ALU.add, ALU.add),
                [m1], "lprnd")
            m3 = p.op("v", lambda e, width=width: e.tensor_scalar(
                S["ws0"][:, 0:width], S["ws0"][:, 0:width], 7.0, -8.0, ALU.min, ALU.max),
                [m2], "lpclip")
            wq3 = S["wq0"][:, 0:width].rearrange("p (b i) -> p b i", i=32)
            m4 = p.op("v", lambda e, wq3=wq3, w3=w3, s_exp=s_exp: e.tensor_tensor(
                wq3, w3, s_exp, ALU.mult), [m3, prev_ag], "lpdeq")
            tps = [p.dma("q", S["wqt"][:, dc, :], S["wq0"][:, ts(dc, 128)],
                         deps=[m4, prev_ag], note="dma_tp", transpose=True)
                   for dc in range(nw)]
            prev_ag = p.dma("gq", dap(dr["ag_in"], seg[segnm] + j * 128,
                                      [[rows, 128], [128 * rows, nw], [1, 128]]),
                            S["wqt"][:, 0:nw, :], deps=tps, note="dma_aglp")
            outs.append(prev_ag)
            prev_m4 = m4
        return outs, prev_ag, prev_m4

    o1, prev_ag, pm = lpbq(dr["wup_sh"], 384, 1024, "upT", prev_ag, qd2)
    ag_parts += o1
    o2, prev_ag, pm = lpbq(dr["wgate_sh"], 384, 1024, "gateT", prev_ag, pm)
    ag_parts += o2
    o3, prev_ag, pm = lpbq(dr["wdown_sh"], 128, 3072, "downT", prev_ag, pm)
    ag_parts += o3

    prev_ldw = pm
    for nm, sg in [("wk_sh", "kT"), ("wv_sh", "vT"), ("wr_sh", "rT"), ("wo_sh", "oT")]:
        ld = p.dma("gq", S["wq0"][:, 0:1024], dr[nm].ap()[:, :], deps=[prev_ag, prev_ldw],
                   note="dma_wc")
        tps = [p.dma("q", S["wqt"][:, dc, :], S["wq0"][:, ts(dc, 128)],
                     deps=[ld, prev_ag], note="dma_tp", transpose=True) for dc in range(8)]
        prev_ag = p.dma("gq", dap(dr["ag_in"], seg[sg],
                                  [[128, 128], [128 * 128, 8], [1, 128]]),
                        S["wqt"][:, 0:8, :], deps=tps, note="dma_agk")
        ag_parts.append(prev_ag)
        prev_ldw = ld

    ag = p.op("c", lambda e: e.collective_compute(
        "AllGather", ALU.bypass, replica_groups=[list(range(NCORES))],
        ins=[dr["ag_in"].ap().opt()], outs=[dr["ag_out"].ap().opt()]),
        [ar] + ag_parts, "AG_w")
    st["ag"] = ag
    st["ar_last"] = ag

    sx1 = p.op("v", lambda e: e.tensor_scalar(S["stat"][:, 48:64], S["stat"][:, 32:48],
                                              1e-5, None, ALU.max), [st["sx_done"]], "sxc")
    sx2 = p.op("v", lambda e: e.reciprocal(S["tiny"][:, 0:16], S["stat"][:, 48:64]),
               [sx1], "sxr")
    sx3 = p.op("v", lambda e: e.tensor_scalar(S["tinyb"][:, 0:16], S["tiny"][:, 0:16],
                                              127.0, None, ALU.mult), [sx2], "sx127")
    sx3b = p.op("v", lambda e: e.tensor_copy(S["tiny"][:, 0:16], S["tinyb"][:, 0:16]),
                [sx3], "sxbk")
    sx4 = p.op("v", lambda e: e.reciprocal(S["tiny2"][:, 0:16], S["tiny"][:, 0:16]),
               [sx3b], "rsx")
    sx5 = p.op("v", lambda e: e.tensor_scalar(S["tiny2"][:, 0:16], S["tiny2"][:, 0:16],
                                              S["scal"][:, S_RW1:S_RW1 + 1], None,
                                              ALU.mult), [sx4, sw], "rsxw")
    st["sxb"] = row_bcast(p, st, S["tiny"][:, 0:16], S["SXB"],
                          [sx3b, sw] + st["xn_done"], "sxb")
    st["rsxw1b"] = row_bcast(p, st, S["tiny2"][:, 0:16], S["RSXW1B"],
                             [sx5, sw] + st["xn_done"], "rsxw")
    p.free("lt2", "lt", "wqt", "wq0", "ws2", "ws0")


def ph2(p, st):
    S = p.sb
    p.alloc("s_bf", [128, DC, T], BF16, side="right")
    p.alloc("cmask", [128, 4, 512], BF16, side="right")
    p.alloc("Qsb", [128, T], BF16)
    p.alloc("Xb", [128, T], BF16)
    p.alloc("Vbf", [128, T], BF16)
    p.alloc("Vtm", [128, 16, 128], BF16)
    p.alloc("attA0", [128, 512], BF16)
    p.alloc("attA1", [128, 512], BF16)
    p.alloc("attB0", [128, 512], BF16)
    p.alloc("attB1", [128, 512], BF16)

    cmk = []
    for rr in range(4):
        mz = p.op("g", lambda e, rr=rr: e.memset(S["cmask"][:, rr, :], 1.0),
                  [st["consts"]], "cmz")
        cmk.append(p.op("g", lambda e, rr=rr: e.affine_select(
            S["cmask"][:, rr, :], S["cmask"][:, rr, :], [[1, 512]], ALU.is_ge, 0.0,
            base=-128 * rr, channel_multiplier=-1), [mz], "cma"))
    evacs = []
    hp_fence = (list(st["xn_done"]) + [st["sqk"], st["sv"], st["consts"],
                st["sxb"], st["rsxw1b"]] + cmk)
    for hp in range(DC):
        qs = p.op("v", lambda e, hp=hp: e.tensor_scalar(
            S["Qsb"][:, :], S["SA"][:, hp, :], S["sqk_c"][:, hp:hp + 1], None, ALU.mult),
            hp_fence, "qs")
        xb = p.op("v", lambda e, hp=hp: e.tensor_copy(S["Xb"][:, :], S["SA"][:, hp, :]),
                  hp_fence, "xb")
        vb = p.op("v", lambda e, hp=hp: e.tensor_scalar(
            S["Vbf"][:, :], S["SA"][:, hp, :], S["sv_c"][:, hp:hp + 1], None, ALU.mult),
            hp_fence, "vb")
        vtm_ev = {}
        prev_ev = None
        for kc in range(16):
            tp = p.op("t", lambda e, kc=kc: e.transpose(
                S["pTb"][:, :], S["Vbf"][:, ts(kc, 128)], S["identB"][:, :]),
                [vb, prev_ev] + (hp_fence if kc == 0 else []), "vtp")
            ev = p.op("s", lambda e, kc=kc: e.copy(S["Vtm"][:, kc, :], S["pTb"][:, :]),
                      [tp], "vev")
            vtm_ev[kc] = ev
            prev_ev = ev
        evacs = []
        prev_sg = {}
        prev_av = {}
        for qt in range(NTQ):
            avbank = "pC"
            nkc = 4 * (qt + 1)
            last_avB = None
            for kc in range(nkc):
                attA = S["attA0"] if kc % 2 == 0 else S["attA1"]
                attB = S["attB0"] if kc % 2 == 0 else S["attB1"]
                bkA = "pA" if kc % 2 == 0 else "pB"
                bkB = "pD" if kc % 2 == 0 else "pE"
                qkA = p.op("t", lambda e, kc=kc, qt=qt, bkA=bkA: e.matmul(
                    S[bkA][:, :], S["Qsb"][0:64, ts(kc, 128)],
                    S["Xb"][0:64, ts(qt, 512)], start=True, stop=True,
                    tile_position=(0, 0), skip_group_check=True),
                    [qs, xb, prev_sg.get(("A", qt, kc - 2)) if kc >= 2
                     else prev_sg.get(("A", qt - 1, 4 * qt - 2 + kc))], "qkA")
                qkB = p.op("t", lambda e, kc=kc, qt=qt, bkB=bkB: e.matmul(
                    S[bkB][:, :], S["Qsb"][64:128, ts(kc, 128)],
                    S["Xb"][64:128, ts(qt, 512)], start=True, stop=True,
                    tile_position=(64, 0), skip_group_check=True),
                    [qs, xb, prev_sg.get(("B", qt, kc - 2)) if kc >= 2
                     else prev_sg.get(("B", qt - 1, 4 * qt - 2 + kc))], "qkB")
                sgA = p.op("s", lambda e, attA=attA, bkA=bkA: e.activation(
                    attA[:, :], S[bkA][:, :], AF.Sigmoid, scale=0.5),
                    [qkA, prev_av.get(("A", qt, kc - 2)) if kc >= 2
                     else prev_av.get(("A", qt - 1, 4 * qt - 1))], "sgA")
                sgB = p.op("s", lambda e, attB=attB, bkB=bkB: e.activation(
                    attB[:, :], S[bkB][:, :], AF.Sigmoid, scale=0.5),
                    [qkB, prev_av.get(("B", qt, kc - 2)) if kc >= 2
                     else prev_av.get(("B", qt - 1, 4 * qt - 1))], "sgB")
                prev_sg[("A", qt, kc)] = sgA
                prev_sg[("B", qt, kc)] = sgB
                depA, depB = sgA, sgB
                if kc >= 4 * qt:
                    rr = kc - 4 * qt
                    depA = p.op("v", lambda e, attA=attA, rr=rr: e.tensor_mul(
                        attA[:, :], attA[:, :], S["cmask"][:, rr, :]), [sgA], "mkA")
                    depB = p.op("v", lambda e, attB=attB, rr=rr: e.tensor_mul(
                        attB[:, :], attB[:, :], S["cmask"][:, rr, :]), [sgB], "mkB")
                avA = p.op("t", lambda e, kc=kc, avbank=avbank, attA=attA, nkc=nkc: e.matmul(
                    S[avbank][0:64, :], S["Vtm"][:, kc, 0:64], attA[:, :],
                    start=(kc == 0), stop=(kc == nkc - 1),
                    tile_position=(0, 0), skip_group_check=True),
                    [depA, vtm_ev[kc], evacs[-1] if evacs and kc == 0 else None],
                    "avA")
                avB = p.op("t", lambda e, kc=kc, avbank=avbank, attB=attB, nkc=nkc: e.matmul(
                    S[avbank][64:128, :], S["Vtm"][:, kc, 64:128], attB[:, :],
                    start=(kc == 0), stop=(kc == nkc - 1),
                    tile_position=(0, 64), skip_group_check=True),
                    [depB, vtm_ev[kc], avA], "avB")
                prev_av[("A", qt, kc)] = avA
                prev_av[("B", qt, kc)] = avB
                last_avB = avB
            ev = p.op("s", lambda e, hp=hp, qt=qt, avbank=avbank: e.copy(
                S["s_bf"][:, hp, ts(qt, 512)], S[avbank][:, :]), [last_avB], "sev")
            evacs.append(ev)
        hp_fence = list(evacs) + [st["sqk"], st["sv"]]
    st["attn_done"] = evacs
    p.free("attB1", "attB0", "attA1", "attA0", "Vtm", "Vbf", "Xb", "Qsb")
    p.free("cmask")


def ph3(p, st):
    S = p.sb
    dr = p.dr
    seg = p.seg
    AGN = p.AGN
    fence = st["attn_done"]

    p.alloc("HMAX", [128, T], F32)
    p.alloc("XQ", [128, DC, T], BF16)
    xq_ops = []
    for dc in range(DC):
        xq_ops.append(quant3_row(p, S["XQ"][:, dc, :], S["SA"][:, dc, :], S["SXB"][:, :],
                                 -128.0, 127.0,
                                 [st["sxb"], st["xn_done"][0], xq_ops[-1] if xq_ops
                                  else None] + fence, S["f1"][:, :]))
    xn_sp_dmas = []
    for dc in range(DC):
        xn_sp_dmas.append(p.dma("q", dap(dr["xn_sp"], dc * 128 * T, [[T, 128], [1, T]]),
                                S["SA"][:, dc, :], deps=[st["xn_done"][0]] + fence,
                                note="dma_xnsp"))
    st["xn_sp"] = xn_sp_dmas

    spill_ops = []
    hmax_ops = []
    sil_prev = {}
    mm_last_of_hc = {}
    prev_ev_ops = None
    for hc in range(HC0):
        r, jc = hc // 4, hc % 4
        wb = wts_sel(S, hc)
        wl = p.dma("q", wb[:, :, :],
                   dap(dr["ag_out"], r * AGN + seg["w1T"] + jc * 128,
                       [[512, 128], [128 * 512, 8], [1, 128]]),
                   deps=[st["ag"], mm_last_of_hc.get(hc - 2)], note="dma_w1t")
        mms = []
        new_ev_ops = []
        for tq in range(NTQ):
            bank = "pA" if tq % 2 == 0 else "pB"
            for dc in range(DC):
                mdep = [wl, xq_ops[dc]]
                if dc == 0:
                    if tq >= 2:
                        mdep.append(new_ev_ops[tq - 2])
                    elif prev_ev_ops is not None:
                        mdep.append(prev_ev_ops[tq + 2])
                m = p.op("t", lambda e, bank=bank, dc=dc, tq=tq, wb=wb: e.matmul(
                    S[bank][:, :], wb[:, dc, :], S["XQ"][:, dc, ts(tq, 512)],
                    start=(dc == 0), stop=(dc == DC - 1)), mdep, "mm1")
                mms.append(m)
            ev = p.op("v", lambda e, bank=bank, tq=tq: e.tensor_mul(
                S["f1"][:, ts(tq, 512)], S[bank][:, :], S["RSXW1B"][:, ts(tq, 512)]),
                [mms[-1], st["rsxw1b"]] + ([sil_prev.get(hc - 1)] if tq == 0 else []),
                "ev1")
            new_ev_ops.append(ev)
        prev_ev_ops = new_ev_ops
        mm_last_of_hc[hc] = mms[-1]
        c1 = p.op("s", lambda e, hc=hc: e.activation(
            S["f2"][:, :], S["f1"][:, :], AF.Copy, scale=S["cw1"][:, hc:hc + 1]),
            new_ev_ops + [sil_prev.get(hc - 1)], "cv1")
        c2 = p.op("v", lambda e, hc=hc: e.scalar_tensor_tensor(
            S["f2"][:, 1:T], S["f1"][:, 0:T - 1], S["cw0"][:, hc:hc + 1],
            S["f2"][:, 1:T], ALU.mult, ALU.add), [c1] + new_ev_ops, "cv0")
        c3 = p.op("v", lambda e, hc=hc: e.scalar_tensor_tensor(
            S["f2"][:, 0:T - 1], S["f1"][:, 1:T], S["cw2"][:, hc:hc + 1],
            S["f2"][:, 0:T - 1], ALU.mult, ALU.add), [c2], "cv2")
        stage = S["g1s"] if hc % 2 == 0 else S["g2s"]
        sil = p.op("s", lambda e, hc=hc, stage=stage: e.activation(
            stage[:, :], S["f2"][:, :], AF.Silu, bias=S["cb"][:, hc:hc + 1]),
            [c3, spill_ops[-2] if hc >= 2 else None,
             hmax_ops[-2] if hc >= 2 else None], "sil")
        sil_prev[hc] = sil
        if hc == 0:
            hm = p.op("g", lambda e, stage=stage: e.tensor_copy(
                S["HMAX"][:, :], stage[:, :]), [sil], "hm0")
        else:
            hm = p.op("v", lambda e, stage=stage: e.tensor_max(
                S["HMAX"][:, :], S["HMAX"][:, :], stage[:, :]), [sil, hmax_ops[-1]], "hm")
        hmax_ops.append(hm)
        spill_ops.append(p.dma("q", dap(dr["hsil_sp"], hc * 128 * T, [[T, 128], [1, T]]),
                               stage[:, :], deps=[sil], note="dma_hsp"))
    p.free("XQ")

    prev_rd = None
    for i in range(16):
        bank = "pT" if i % 2 == 0 else "pT2"
        tp = p.op("t", lambda e, i=i, bank=bank: e.transpose(
            S[bank][:, :], S["HMAX"][:, ts(i, 128)], S["identF"][:, :]),
            [hmax_ops[-1], prev_rd], "h2tp")
        rd = p.op("v", lambda e, i=i, bank=bank: e.reduce_max(
            S["stat"][:, 32 + i:33 + i], S[bank][:, :], AX), [tp], "h2rd")
        prev_rd = rd
    p.free("HMAX")
    p.free("RSXW1B", "SXB", "SA")

    p.alloc("MEAN1B", [128, T], F32)
    p.alloc("RSTD1B", [128, T], F32)
    p.alloc("S2B", [128, T], BF16)
    p.alloc("RS2W2B", [128, T], F32)
    p.alloc("MEAN0B", [128, T], F32)
    p.alloc("STD0B", [128, T], F32)
    p.alloc("XQ2", [128, HC0, TH], BF16)
    p.alloc("xnt", [128, 512], F32)
    sx1 = p.op("v", lambda e: e.tensor_scalar(S["stat"][:, 48:64], S["stat"][:, 32:48],
                                              1e-5, None, ALU.max), [prev_rd], "s2c")
    sx2 = p.op("v", lambda e: e.reciprocal(S["tiny"][:, 0:16], S["stat"][:, 48:64]),
               [sx1], "s2r")
    sx3 = p.op("v", lambda e: e.tensor_scalar(S["tinyb"][:, 0:16], S["tiny"][:, 0:16],
                                              127.0, None, ALU.mult), [sx2], "s2127")
    sx3b = p.op("v", lambda e: e.tensor_copy(S["tiny"][:, 0:16], S["tinyb"][:, 0:16]),
                [sx3], "s2bk")
    sx4 = p.op("v", lambda e: e.reciprocal(S["tiny2"][:, 0:16], S["tiny"][:, 0:16]),
               [sx3b], "rs2")
    sx5 = p.op("v", lambda e: e.tensor_scalar(S["tiny2"][:, 0:16], S["tiny2"][:, 0:16],
                                              S["scal"][:, S_RW2:S_RW2 + 1], None,
                                              ALU.mult), [sx4, st["sw"]], "rs2w")
    s2b = row_bcast(p, st, S["tiny"][:, 0:16], S["S2B"], [sx3b, prev_rd], "s2b")
    rs2b = row_bcast(p, st, S["tiny2"][:, 0:16], S["RS2W2B"], [sx5, prev_rd], "rs2b")
    sd1 = p.op("v", lambda e: e.reciprocal(S["tiny"][:, 0:16], S["stat"][:, 16:32]),
               [s2b, rs2b], "std0")
    m0b = row_bcast(p, st, S["stat"][:, 0:16], S["MEAN0B"], [sd1], "m0b")
    st0b = row_bcast(p, st, S["tiny"][:, 0:16], S["STD0B"], [sd1], "st0b")

    x1_dmas = []
    ln1_var_ops = []
    banks = ["pA", "pB"]
    prev_qz = None
    for th in range(2):
        qz_ops = []
        for hc in range(HC0):
            stage = S["g1s"] if hc % 2 == 0 else S["g2s"]
            if hc >= 2:
                war = qz_ops[-2]
            elif th == 0:
                war = spill_ops[HC0 - 2 + hc]
            else:
                war = prev_qz[HC0 - 2 + hc]
            ld = p.dma("q", stage[:, 0:TH],
                       dap(dr["hsil_sp"], hc * 128 * T + th * TH, [[T, 128], [1, TH]]),
                       deps=[spill_ops[hc], s2b, war],
                       note="dma_hld")
            qz = quant3_row(p, S["XQ2"][:, hc, :], stage[:, 0:TH],
                            S["S2B"][:, ts(th, TH)], -128.0, 127.0, [ld],
                            S["f1"][:, (hc % 2) * TH:(hc % 2) * TH + TH],
                            chans=("v", "g", "g"))
            qz_ops.append(qz)
        wl_prev = {}
        for dc in range(DC):
            mms = {0: [], 1: []}
            wb = S["wt2"] if dc % 2 == 0 else S["wt3"]
            wl = p.dma("q", wb[:, :, :],
                       dap(dr["ag_out"], dc * AGN + seg["w2T"],
                           [[128, 128], [128 * 128, HC0], [1, 128]]),
                       deps=[st["ag"], qz_ops[-1], wl_prev.get(dc - 2)],
                       note="dma_w2t")
            wl_prev[dc] = None
            for hc in range(HC0):
                for tq2 in range(2):
                    m = p.op("t", lambda e, wb=wb, tq2=tq2, hc=hc: e.matmul(
                        S[banks[tq2]][:, :], wb[:, hc, :], S["XQ2"][:, hc, ts(tq2, 512)],
                        start=(hc == 0), stop=(hc == HC0 - 1)),
                        [wl] + ([x1_dmas[-1]] if hc == 0 and x1_dmas else []), "mm2")
                    mms[tq2].append(m)
            wl_prev[dc] = mms[1][-1]
            for tq2 in range(2):
                gt = th * 2 + tq2
                ev = p.op("v", lambda e, tq2=tq2, gt=gt: e.tensor_mul(
                    S["f1"][:, 0:512], S[banks[tq2]][:, :], S["RS2W2B"][:, ts(gt, 512)]),
                    [mms[tq2][-1], rs2b], "ev2")
                sil = p.op("s", lambda e: e.activation(
                    S["f2"][:, 0:512], S["f1"][:, 0:512], AF.Silu), [ev], "sil2")
                sg = p.op("v", lambda e, dc=dc, gt=gt: e.tensor_mul(
                    S["f2"][:, 0:512], S["f2"][:, 0:512], S["s_bf"][:, dc, ts(gt, 512)]),
                    [sil], "sg")
                xn_ld = p.dma("q", S["xnt"][:, :],
                              dap(dr["xn_sp"], dc * 128 * T + gt * 512,
                                  [[T, 128], [1, 512]]),
                              deps=[xn_sp_dmas[dc], m0b, st0b,
                                    x1_dmas[-1] if x1_dmas else None], note="dma_xnld")
                xr = p.op("v", lambda e, gt=gt: e.tensor_mul(
                    S["f3"][:, 0:512], S["xnt"][:, :], S["STD0B"][:, ts(gt, 512)]),
                    [xn_ld], "xstd")
                xr2 = p.op("v", lambda e, gt=gt: e.tensor_add(
                    S["f3"][:, 0:512], S["f3"][:, 0:512], S["MEAN0B"][:, ts(gt, 512)]),
                    [xr], "xmean")
                x1t = p.op("v", lambda e: e.tensor_add(
                    S["f3"][:, 0:512], S["f3"][:, 0:512], S["f2"][:, 0:512]),
                    [xr2, sg], "x1t")
                x1_dmas.append(p.dma("q", dap(dr["x1_sp"], dc * 128 * T + gt * 512,
                                              [[T, 128], [1, 512]]),
                                     S["f3"][:, 0:512], deps=[x1t], note="dma_x1sp"))
        prev_qz = qz_ops
    st["x1_dmas"] = x1_dmas
    st["x1_idx"] = lambda dc, gt: (gt // 2) * 16 + dc * 2 + (gt % 2)
    p.free("xnt", "XQ2", "STD0B", "MEAN0B", "RS2W2B", "S2B")
    p.free("s_bf")


def ph4(p, st):
    S = p.sb
    dr = p.dr
    seg = p.seg
    AGN = p.AGN
    fence = st["x1_dmas"][-2:]

    p.alloc("xnt4", [128, 512], F32)
    ln1_var_ops = []
    sq_hist = {}
    m1_hist = {}
    m2_hist = {}
    mS1 = mS2 = None
    for gt in range(NTQ):
        for dc in range(DC):
            i = gt * DC + dc
            xb = S["xnt4"][:, :] if i % 2 == 0 else S["f3"][:, 0:512]
            sqb = S["f2"][:, 0:512] if i % 2 == 0 else S["f2"][:, 512:1024]
            ld = p.dma("q", xb,
                       dap(dr["x1_sp"], dc * 128 * T + gt * 512, [[T, 128], [1, 512]]),
                       deps=[st["x1_dmas"][st["x1_idx"](dc, gt)],
                             sq_hist.get(i - 2), m1_hist.get(i - 2)] +
                            (fence if i == 0 else []), note="dma_x1st")
            sqz = p.op("v", lambda e, xb=xb, sqb=sqb: e.tensor_mul(
                sqb, xb, xb), [ld, m2_hist.get(i - 2)], "sq1")
            mS1 = p.op("t", lambda e, dc=dc, xb=xb: e.matmul(
                S["pC"][:, :], S["ones"][:, :], xb,
                start=(dc == 0), stop=(dc == DC - 1)),
                [ld, ln1_var_ops[-1] if dc == 0 and ln1_var_ops else None], "ms1")
            mS2 = p.op("t", lambda e, dc=dc, sqb=sqb: e.matmul(
                S["pD"][:, :], S["ones"][:, :], sqb,
                start=(dc == 0), stop=(dc == DC - 1)), [sqz], "ms2")
            sq_hist[i] = sqz
            m1_hist[i] = mS1
            m2_hist[i] = mS2
        e1 = p.op("v", lambda e, gt=gt: e.tensor_scalar(
            S["MEAN1B"][:, ts(gt, 512)], S["pC"][:, :], 1.0 / D, None,
            ALU.mult), [mS1], "ln1m")
        e2 = p.op("v", lambda e: e.tensor_scalar(
            S["f1"][:, 0:512], S["pD"][:, :], 1.0 / D, None, ALU.mult),
            [mS2], "ln1e2")
        e3 = p.op("v", lambda e, gt=gt: e.tensor_mul(
            S["f2"][:, 0:512], S["MEAN1B"][:, ts(gt, 512)],
            S["MEAN1B"][:, ts(gt, 512)]), [e1, e2], "ln1m2")
        e4 = p.op("v", lambda e: e.tensor_sub(
            S["f1"][:, 0:512], S["f1"][:, 0:512], S["f2"][:, 0:512]), [e3], "ln1v")
        e5 = p.op("s", lambda e: e.activation(
            S["f1"][:, 0:512], S["f1"][:, 0:512], AF.Sqrt,
            bias=S["epsc"][:, 0:1]), [e4], "ln1sd")
        e6 = p.op("v", lambda e, gt=gt: e.reciprocal(
            S["RSTD1B"][:, ts(gt, 512)], S["f1"][:, 0:512]), [e5], "ln1rs")
        ln1_var_ops.append(e6)
        prev = e6
    st["ln1"] = ln1_var_ops

    mx_cols = []
    xn1_dmas = []
    n1_hist = {}
    d_prev = st["ln1"][-1]
    for dc in range(DC):
        for gt in range(NTQ):
            xb = S["xnt4"][:, :] if gt % 2 == 0 else S["f2"][:, 0:512]
            ld = p.dma("q", xb,
                       dap(dr["x1_sp"], dc * 128 * T + gt * 512, [[T, 128], [1, 512]]),
                       deps=[st["x1_dmas"][st["x1_idx"](dc, gt)],
                             n1_hist.get((dc, gt - 2)),
                             d_prev if gt < 2 else None], note="dma_x1ld")
            n1 = p.op("v", lambda e, gt=gt, xb=xb: e.tensor_sub(
                S["f1"][:, ts(gt, 512)], xb, S["MEAN1B"][:, ts(gt, 512)]),
                [ld] + st["ln1"], "n1")
            n1_hist[(dc, gt)] = n1
            prev = p.op("v", lambda e, gt=gt: e.tensor_mul(
                S["f1"][:, ts(gt, 512)], S["f1"][:, ts(gt, 512)],
                S["RSTD1B"][:, ts(gt, 512)]), [n1], "n2")
        mx_cols.append(p.op("v", lambda e, dc=dc: e.reduce_max(
            S["mxc"][:, dc:dc + 1], S["f1"][:, :], AX, apply_absolute_value=True),
            [prev], "m0mx"))
        d = p.dma("q", dap(dr["xn1_sp"], dc * 128 * T, [[T, 128], [1, T]]),
                  S["f1"][:, :], deps=[prev], note="dma_xn1")
        xn1_dmas.append(d)
        d_prev = d
    rs0 = global_max_ar(p, st, S["mxc"][:, 0:8], 2, S_S0, S_RS0, "m0", mx_cols)
    p.free("xnt4")
    p.free("RSTD1B", "MEAN1B")

    p.alloc("KT", [128, DC, T], BF16, side="right")
    p.alloc("VT", [128, DC, T], BF16, side="right")
    p.alloc("SRT", [128, DC, T], BF16, side="right")
    p.alloc("MIX", [128, DC, T], BF16, side="right")
    banks = ["pA", "pB", "pC", "pD"]
    st["proj_done"] = {}
    prev_q = None
    for mixn, wseg, dstn, sig in [("tmk", "kT", "KT", False), ("tmv", "vT", "VT", False),
                                  ("tmr", "rT", "SRT", True)]:
        mix_ops = []
        o1_hist = {}
        for dc in range(DC):
            xb = S["f1"] if dc % 2 == 0 else S["f3"]
            ld = p.dma("q", xb[:, :],
                       dap(dr["xn1_sp"], dc * 128 * T, [[T, 128], [1, T]]),
                       deps=[xn1_dmas[dc], xn1_dmas[-1] if dc == 0 else None,
                             mx_cols[-1] if dc < 2 else None,
                             o1_hist.get(dc - 2), prev_q if dc < 2 else None],
                       note="dma_xm")
            dsub = p.op("v", lambda e, xb=xb: e.tensor_sub(
                S["f2"][:, 1:T], xb[:, 1:T], xb[:, 0:T - 1]), [ld], "dmx")
            dcol = p.op("v", lambda e, xb=xb: e.tensor_copy(
                S["f2"][:, 0:1], xb[:, 0:1]), [ld], "dc0")
            o1 = p.op("v", lambda e, mixn=mixn, dc=dc, xb=xb: e.scalar_tensor_tensor(
                S["MIX"][:, dc, 1:T], S["f2"][:, 1:T], S[mixn][:, dc:dc + 1],
                xb[:, 0:T - 1], ALU.mult, ALU.add), [dsub, dcol], "mx1")
            o2 = p.op("v", lambda e, mixn=mixn, dc=dc: e.tensor_scalar(
                S["MIX"][:, dc, 0:1], S["f2"][:, 0:1], S[mixn][:, dc:dc + 1], None,
                ALU.mult), [dcol], "mx0")
            mix_ops += [o1, o2]
            o1_hist[dc] = o1
        ev_prev = {}
        evl = []
        last_mm_cc = {}
        for cc in range(DC):
            mms = {tq: None for tq in range(NTQ)}
            wb = wts_sel(S, cc)
            wl = p.dma("q", wb[:, :, :],
                       dap(dr["ag_out"], cc * AGN + seg[wseg],
                           [[128, 128], [128 * 128, DC], [1, 128]]),
                       deps=[st["ag"], last_mm_cc.get(cc - 2)] +
                            (mix_ops if cc == 0 else []), note="dma_wp")
            for dc in range(DC):
                for tq in range(NTQ):
                    m = p.op("t", lambda e, wb=wb, dc=dc, tq=tq: e.matmul(
                        S[banks[tq]][:, :], wb[:, dc, :], S["MIX"][:, dc, ts(tq, 512)],
                        start=(dc == 0), stop=(dc == DC - 1)),
                        [wl, ev_prev.get((cc - 1, tq))], "mmp")
                    mms[tq] = m
            last_mm_cc[cc] = mms[NTQ - 1]
            for tq in range(NTQ):
                if sig:
                    ev = p.op("s", lambda e, cc=cc, tq=tq, dstn=dstn: e.activation(
                        S[dstn][:, cc, ts(tq, 512)], S[banks[tq]][:, :], AF.Sigmoid),
                        [mms[tq]], "evsig")
                else:
                    ev = p.op("s", lambda e, cc=cc, tq=tq, dstn=dstn: e.copy(
                        S[dstn][:, cc, ts(tq, 512)], S[banks[tq]][:, :]), [mms[tq]], "evp")
                ev_prev[(cc, tq)] = ev
                evl.append(ev)
        st["proj_done"][dstn] = evl
        prev_q = evl[-1]
    p.free("MIX")

    p.alloc("XQ1", [128, DC, T], BF16, side="right")
    xq1_ops = []
    proj_fence = (st["proj_done"]["KT"][-1:] + st["proj_done"]["VT"][-1:]
                  + st["proj_done"]["SRT"][-1:])
    prev = None
    for dc in range(DC):
        ld = p.dma("q", S["f1"][:, :], dap(dr["xn1_sp"], dc * 128 * T, [[T, 128], [1, T]]),
                   deps=[xn1_dmas[dc], prev] + (proj_fence if dc == 0 else []),
                   note="dma_xq1")
        q = quant3_col(p, S["XQ1"][:, dc, :], S["f1"][:, :],
                       S["scal"][:, S_RS0:S_RS0 + 1], -32768.0, 32767.0,
                       [ld, rs0], S["f2"][:, :])
        xq1_ops.append(q)
        prev = q
    st["xq1"] = xq1_ops


def ph5(p, st):
    S = p.sb
    dr = p.dr
    seg = p.seg
    AGN = p.AGN

    TQS = T // 4
    p.alloc("f4", [128, T], F32)
    p.alloc("PP", [128, TQS + 1], F32)
    p.alloc("AAs", [128, TQS + 1], F32)
    p.alloc("BBs", [128, TQS + 1], F32)
    p.alloc("X4b", [128, TQS], F32)
    X1b = S["f1"][:, 0:TQS]
    X2b = S["f2"][:, 0:TQS]
    X3b = S["f3"][:, 0:TQS]

    upgate_spills = {"up": [], "gate": []}
    mx_ops = {"up": [], "gate": []}
    banksUG = {"up": ["pA", "pB"], "gate": ["pC", "pD"]}
    stages = {"up": "f4", "gate": "f4"}
    stages2 = {"up": "f4", "gate": "f4"}
    last_mm = {"up": None, "gate": None}
    last_mx = None
    for Ic in range(IC1):
        r, jc = Ic // 3, Ic % 3
        for nm, wsg, spnm in [("up", "upT", "up_sp"), ("gate", "gateT", "gate_sp")]:
            wb = wts_sel(S, 2 * Ic + (0 if nm == "up" else 1))
            wl = p.dma("q", wb[:, :, :],
                       dap(dr["ag_out"], r * AGN + seg[wsg] + jc * 128,
                           [[384, 128], [128 * 384, 8], [1, 128]]),
                       deps=[st["ag"], last_mm[nm]] + (st["xq1"] if Ic == 0 else []),
                       note="dma_wug")
            stage = S[stages[nm]] if Ic % 2 == 0 else S[stages2[nm]]
            evs = []
            mprev = None
            for tq in range(NTQ):
                bank = banksUG[nm][tq % 2]
                for dc in range(DC):
                    m = p.op("t", lambda e, wb=wb, bank=bank, dc=dc, tq=tq: e.matmul(
                        S[bank][:, :], wb[:, dc, :], S["XQ1"][:, dc, ts(tq, 512)],
                        start=(dc == 0), stop=(dc == DC - 1)),
                        [wl, evs[tq - 2] if tq >= 2 and dc == 0 else None], "mmug")
                    mprev = m
                lastsp = (upgate_spills["gate"] if nm == "up"
                          else upgate_spills["up"])
                ev = p.op("s", lambda e, bank=bank, tq=tq, stage=stage: e.activation(
                    stage[:, ts(tq, 512)], S[bank][:, :], AF.Copy,
                    scale=S["scal"][:, S_S0:S_S0 + 1]),
                    [mprev, st["xq1"][-1],
                     upgate_spills[nm][-1] if upgate_spills[nm] and tq == 0 else None,
                     lastsp[-1] if lastsp and tq == 0 else None,
                     last_mx if tq == 0 else None],
                    "evug")
                evs.append(ev)
            last_mm[nm] = mprev
            last_mx = p.op("v", lambda e, stage=stage, nm=nm, Ic=Ic: e.reduce_max(
                S["mxc"][:, (8 if nm == "up" else 32) + Ic:
                         (9 if nm == "up" else 33) + Ic],
                stage[:, :], AX, apply_absolute_value=True), evs, "mxug")
            mx_ops[nm].append(last_mx)
            upgate_spills[nm].append(p.dma("q", dap(dr[spnm], Ic * 128 * T,
                                                    [[T, 128], [1, T]]),
                                           stage[:, :], deps=evs, note="dma_ugsp"))
    st["upg_spills"] = upgate_spills

    c1 = p.op("v", lambda e: e.reduce_max(S["mxc"][:, 56:57], S["mxc"][:, 8:32], AX),
              mx_ops["up"] + [st["ar_last"]], "m1cl")
    c2 = p.op("v", lambda e: e.reduce_max(S["mxc"][:, 57:58], S["mxc"][:, 32:56], AX),
              mx_ops["gate"], "m2cl")
    tp = p.op("t", lambda e: e.transpose(S["pT2"][0:2, 0:128], S["mxc"][:, 56:58],
                                         S["identF"][:, :]), [c1, c2], "m12tp")
    rd1 = p.op("v", lambda e: e.reduce_max(S["mxc"][0:2, 58:59], S["pT2"][0:2, 0:128], AX),
               [tp], "m12rd")
    dcp = p.dma("gq", dr["ar_in"].ap()[3:5], dap(S["mxc"], 58, [[64, 2], [1, 1]]),
                deps=[rd1], note="dma_m12")
    ar12 = p.op("c", lambda e: e.collective_compute(
        "AllReduce", ALU.max, replica_groups=[list(range(NCORES))],
        ins=[dr["ar_in"].ap()[3:5].opt()], outs=[dr["ar_out"].ap()[3:5].opt()]),
        [dcp, st["ar_last"]], "AR_m12")
    st["ar_last"] = ar12
    lb = p.dma("gq", S["mxc"][0:1, 58:60], dr["ar_out"].ap()[3:5], deps=[ar12, dcp],
               note="dma_m12b")
    bc = bcast_p0(p, S["mxc"][0:1, 58:60], S["mxc"][:, 60:62], 2, [lb], "m12")
    q1 = p.op("v", lambda e: e.tensor_scalar(S["scal"][:, S_S1:S_S1 + 1],
                                             S["mxc"][:, 60:61], 1.0 / 32767.0, 1e-12,
                                             ALU.mult, ALU.max), [bc], "s1s")
    q2 = p.op("v", lambda e: e.reciprocal(S["scal"][:, S_RS1:S_RS1 + 1],
                                          S["scal"][:, S_S1:S_S1 + 1]), [q1], "rs1")
    q3 = p.op("v", lambda e: e.tensor_scalar(S["scal"][:, S_S2:S_S2 + 1],
                                             S["mxc"][:, 61:62], 1.0 / 32767.0, 1e-12,
                                             ALU.mult, ALU.max), [bc], "s2s")
    q4 = p.op("v", lambda e: e.reciprocal(S["scal"][:, S_RS2:S_RS2 + 1],
                                          S["scal"][:, S_S2:S_S2 + 1]), [q3], "rs2s")
    q5 = p.op("v", lambda e: e.tensor_scalar(S["scal"][:, S_S265536:S_S265536 + 1],
                                             S["scal"][:, S_S2:S_S2 + 1], 1.0 / 65536.0,
                                             None, ALU.mult), [q3], "s265")
    st["scales12"] = [q2, q4, q5]

    rwkv_done = []
    rwkv_pair = {}
    prev_half = None
    for cc in range(DC):
        for h in range(4):
            off = h * TQS
            kap = S["KT"][:, cc, off:off + TQS]
            vap = S["VT"][:, cc, off:off + TQS]
            deps0 = [st["proj_done"]["KT"][cc * NTQ + h],
                     st["proj_done"]["VT"][cc * NTQ + h],
                     st["proj_done"]["SRT"][-1],
                     st["wc"], st["nu"], prev_half]
            if h == 0:
                i1 = p.op("v", lambda e: e.memset(S["PP"][:, 0:1], -1e38), deps0, "ppi")
                i2 = p.op("v", lambda e: e.memset(S["AAs"][:, 0:1], 0.0), deps0, "aai")
                i3 = p.op("v", lambda e: e.memset(S["BBs"][:, 0:1], 0.0), deps0, "bbi")
            else:
                i1 = p.op("v", lambda e: e.tensor_copy(S["PP"][:, 0:1],
                                                       S["PP"][:, TQS:TQS + 1]), deps0, "ppc")
                i2 = p.op("v", lambda e: e.tensor_copy(S["AAs"][:, 0:1],
                                                       S["AAs"][:, TQS:TQS + 1]), deps0, "aac")
                i3 = p.op("v", lambda e: e.tensor_copy(S["BBs"][:, 0:1],
                                                       S["BBs"][:, TQS:TQS + 1]), deps0, "bbc")
            wbc = p.op("v", lambda e, cc=cc: e.tensor_copy(
                S["X4b"][:, :], dap(S["w_c"], cc, [[8, 128], [0, TQS]])), deps0, "wbc")
            scn = p.op("v", lambda e, kap=kap: e.tensor_tensor_scan(
                S["PP"][:, 1:], S["X4b"][:, :], kap, S["PP"][:, 0:1], ALU.add, ALU.max),
                [i1, i2, i3, wbc], "ppscan")
            t1 = p.op("v", lambda e, kap=kap: e.tensor_sub(X1b, kap, S["PP"][:, 1:]),
                      [scn], "t1")
            t2 = p.op("v", lambda e: e.tensor_sub(X2b, S["PP"][:, 0:TQS], S["PP"][:, 1:]),
                      [scn], "t2")
            e2 = p.op("s", lambda e: e.activation(X3b, X1b, AF.Exp), [t1], "e2")
            e1 = p.op("s", lambda e, cc=cc: e.activation(
                X2b, X2b, AF.Exp, bias=S["w_c"][:, cc:cc + 1]), [t2], "e1")
            za = p.op("v", lambda e, vap=vap: e.tensor_mul(S["X4b"][:, :], X3b, vap),
                      [e2, e1, scn], "za")
            sa = p.op("v", lambda e: e.tensor_tensor_scan(
                S["AAs"][:, 1:], X2b, S["X4b"][:, :], S["AAs"][:, 0:1],
                ALU.mult, ALU.add), [za], "aascan")
            sb_ = p.op("v", lambda e: e.tensor_tensor_scan(
                S["BBs"][:, 1:], X2b, X3b, S["BBs"][:, 0:1], ALU.mult, ALU.add),
                [sa], "bbscan")
            ra = p.op("v", lambda e, kap=kap: e.tensor_sub(X1b, S["PP"][:, 0:TQS], kap),
                      [sb_, t1], "ra")
            rb = p.op("v", lambda e, cc=cc: e.tensor_scalar(
                X1b, X1b, S["nu_c"][:, cc:cc + 1], CAP, ALU.add, ALU.min), [ra], "rb")
            rr = p.op("s", lambda e: e.activation(X1b, X1b, AF.Exp), [rb], "rr")
            nm1 = p.op("v", lambda e: e.tensor_mul(X2b, X1b, S["AAs"][:, 0:TQS]),
                       [rr], "nm1")
            nm2 = p.op("v", lambda e, vap=vap: e.tensor_add(X2b, X2b, vap), [nm1], "nm2")
            dn1 = p.op("v", lambda e: e.tensor_mul(X3b, X1b, S["BBs"][:, 0:TQS]),
                       [nm1], "dn1")
            dn2 = p.op("v", lambda e: e.tensor_scalar(X3b, X3b, 1.0, None, ALU.add),
                       [dn1], "dn2")
            rcp = p.op("v", lambda e: e.reciprocal(X3b, X3b), [dn2], "rcp")
            yy = p.op("v", lambda e: e.tensor_mul(X2b, X2b, X3b), [rcp, nm2], "yy")
            rw = p.op("v", lambda e, cc=cc, off=off: e.tensor_mul(
                S["KT"][:, cc, off:off + TQS], X2b, S["SRT"][:, cc, off:off + TQS]),
                [yy], "rwkv")
            rwkv_done.append(rw)
            rwkv_pair.setdefault(cc, []).append(rw)
            prev_half = rw
    st["rwkv"] = rwkv_done

    s1_dmas = []
    ev_prev = {}
    banks = ["pA", "pB", "pC", "pD"]
    last_mm_dcg = {}
    for dcg in range(DC):
        mms = {tq: None for tq in range(NTQ)}
        wb = wts_sel(S, dcg)
        wl = p.dma("q", wb[:, :, :],
                   dap(dr["ag_out"], dcg * AGN + seg["oT"],
                       [[128, 128], [128 * 128, DC], [1, 128]]),
                   deps=[st["ag"], last_mm_dcg.get(dcg - 2)] +
                        ([last_mm["up"], last_mm["gate"]] if dcg == 0 else []),
                   note="dma_wo")
        for cc in range(DC):
            for tq in range(NTQ):
                m = p.op("t", lambda e, wb=wb, cc=cc, tq=tq: e.matmul(
                    S[banks[tq]][:, :], wb[:, cc, :], S["KT"][:, cc, ts(tq, 512)],
                    start=(cc == 0), stop=(cc == DC - 1)),
                    [wl, ev_prev.get((dcg - 1, tq)), rwkv_pair[cc][-1]], "mmo")
                mms[tq] = m
        last_mm_dcg[dcg] = mms[NTQ - 1]
        for tq in range(NTQ):
            stage = S["f4"]
            ev = p.op("s", lambda e, tq=tq, stage=stage: e.copy(
                stage[:, ts(tq, 512)], S[banks[tq]][:, :]),
                [mms[tq], s1_dmas[-4] if len(s1_dmas) >= 4 else None,
                 last_mm["up"] if dcg == 0 and tq == 0 else None,
                 last_mm["gate"] if dcg == 0 and tq == 0 else None,
                 upgate_spills["up"][-1] if dcg == 0 and tq == 0 else None,
                 upgate_spills["gate"][-1] if dcg == 0 and tq == 0 else None], "evo")
            ev_prev[(dcg, tq)] = ev
            s1_dmas.append(p.dma("q", dap(dr["s1_sp"], dcg * 128 * T + tq * 512,
                                          [[T, 128], [1, 512]]),
                                 stage[:, ts(tq, 512)], deps=[ev], note="dma_s1sp"))
    st["s1_dmas"] = s1_dmas
    p.free("X4b", "BBs", "AAs", "PP")
    p.free("XQ1", "SRT", "VT", "KT")
    p.alloc("f5", [128, T], F32)


def ph6(p, st):
    S = p.sb
    dr = p.dr
    seg = p.seg
    AGN = p.AGN

    g2_spills = []
    mx3 = []
    gq_l = []
    g2_l = []
    for Ic in range(IC1):
        stage = S["f4"] if Ic % 2 == 0 else S["f5"]
        ld = p.dma("q", stage[:, :], dap(dr["gate_sp"], Ic * 128 * T, [[T, 128], [1, T]]),
                   deps=[st["upg_spills"]["gate"][Ic],
                         gq_l[-2] if len(gq_l) >= 2 else None] + st["scales12"] +
                        (st["s1_dmas"][-4:] if Ic < 2 else []), note="dma_gld")
        gq = quant3_col(p, S["f1"][:, :], stage[:, :], S["scal"][:, S_RS2:S_RS2 + 1],
                        -32768.0, 32767.0, [ld, g2_l[-1] if g2_l else None],
                        S["f1"][:, :], chan3="g")
        gq_l.append(gq)
        sg = p.op("s", lambda e: e.activation(S["f2"][:, :], S["f1"][:, :], AF.Sigmoid,
                                              scale=S["scal"][:, S_S2:S_S2 + 1]),
                  [gq], "sigg")
        u1 = p.op("s", lambda e: e.activation(S["f2"][:, :], S["f2"][:, :], AF.Copy,
                                              scale=65536.0, bias=RND), [sg], "u16a")
        u1b = p.op("s", lambda e: e.activation(S["f2"][:, :], S["f2"][:, :], AF.Copy,
                                               bias=-RND), [u1], "u16ab")
        u2 = p.op("v", lambda e: e.tensor_scalar(S["f2"][:, :], S["f2"][:, :], 65535.0,
                                                 None, ALU.min), [u1b], "u16b")
        g2 = p.op("v", lambda e: e.scalar_tensor_tensor(
            S["f3"][:, :], S["f1"][:, :], S["scal"][:, S_S265536:S_S265536 + 1],
            S["f2"][:, :], ALU.mult, ALU.mult),
            [u2, gq, g2_spills[-1] if g2_spills else None], "g2")
        g2_l.append(g2)
        mx3.append(p.op("v", lambda e, Ic=Ic: e.reduce_max(
            S["mxc"][:, 8 + Ic:9 + Ic], S["f3"][:, :], AX,
            apply_absolute_value=True), [g2], "mx3"))
        sp = p.dma("q", dap(dr["gate2_sp"], Ic * 128 * T, [[T, 128], [1, T]]),
                   S["f3"][:, :], deps=[g2], note="dma_g2sp")
        g2_spills.append(sp)
    ar3 = global_max_ar(p, st, S["mxc"][:, 8:32], 5, S_S3, S_RS3, "m3", mx3)
    ss = p.op("v", lambda e: e.tensor_mul(S["scal"][:, S_S3S1:S_S3S1 + 1],
                                          S["scal"][:, S_S3:S_S3 + 1],
                                          S["scal"][:, S_S1:S_S1 + 1]),
              [ar3] + st["scales12"], "s3s1")

    o_spills = []
    mx4 = []
    q1_l = []
    q2_l = []
    oo_l = []
    for Ic in range(IC1):
        st1 = S["f4"]
        st2 = S["f5"]
        ld1 = p.dma("q", st1[:, :], dap(dr["gate2_sp"], Ic * 128 * T, [[T, 128], [1, T]]),
                    deps=[g2_spills[Ic], ss, q1_l[-1] if q1_l else None], note="dma_g2ld")
        ld2 = p.dma("q", st2[:, :], dap(dr["up_sp"], Ic * 128 * T, [[T, 128], [1, T]]),
                    deps=[st["upg_spills"]["up"][Ic], ss, q2_l[-1] if q2_l else None],
                    note="dma_upld")
        q1 = quant3_col(p, S["f1"][:, :], st1[:, :], S["scal"][:, S_RS3:S_RS3 + 1],
                        -32768.0, 32767.0, [ld1, oo_l[-1] if oo_l else None],
                        S["f1"][:, :])
        q2 = quant3_col(p, S["f2"][:, :], st2[:, :], S["scal"][:, S_RS1:S_RS1 + 1],
                        -32768.0, 32767.0, [ld2, oo_l[-1] if oo_l else None],
                        S["f2"][:, :])
        q1_l.append(q1)
        q2_l.append(q2)
        oo = p.op("v", lambda e: e.scalar_tensor_tensor(
            S["f3"][:, :], S["f1"][:, :], S["scal"][:, S_S3S1:S_S3S1 + 1],
            S["f2"][:, :], ALU.mult, ALU.mult),
            [q1, q2, o_spills[-1] if o_spills else g2_spills[-1]], "oprod")
        oo_l.append(oo)
        mx4.append(p.op("v", lambda e, Ic=Ic: e.reduce_max(
            S["mxc"][:, 8 + Ic:9 + Ic], S["f3"][:, :], AX,
            apply_absolute_value=True), [oo], "mx4"))
        sp = p.dma("q", dap(dr["o_sp"], Ic * 128 * T, [[T, 128], [1, T]]),
                   S["f3"][:, :], deps=[oo], note="dma_osp")
        o_spills.append(sp)
    ar4 = global_max_ar(p, st, S["mxc"][:, 8:32], 6, S_S4, S_RS4, "m4", mx4)

    p.alloc("OQ", [128, IC1, TH], BF16)
    out_dmas = []
    last_mmd = None
    last_x2 = None
    last_sgd = None
    last_tpo = None
    dn_last = {0: None, 1: None}
    banks = ["pA", "pB"]
    for th in range(2):
        oq_ops = []
        for Ic in range(IC1):
            stg = S["f4"][:, 0:TH] if Ic % 2 == 0 else S["f4"][:, TH:T]
            xdeps = [o_spills[Ic], ar4,
                     oq_ops[-2] if len(oq_ops) >= 2 else None]
            if Ic == 0:
                xdeps += st["s1_dmas"][-1:]
            if Ic < 2 and th == 1:
                xdeps.append(last_mmd)
            ld = p.dma("q", stg,
                       dap(dr["o_sp"], Ic * 128 * T + th * TH, [[T, 128], [1, TH]]),
                       deps=xdeps, note="dma_old")
            q = quant3_col(p, S["OQ"][:, Ic, :], stg,
                           S["scal"][:, S_RS4:S_RS4 + 1], -32768.0, 32767.0,
                           [ld, oq_ops[-1] if oq_ops else last_x2],
                           S["f1"][:, 0:TH])
            oq_ops.append(q)
        for dc in range(DC):
            mms = {0: [], 1: []}
            b = dc % 2
            wb = S["wt2"] if b == 0 else S["wt3"]
            wl = p.dma("q", wb[:, 0:IC1, :],
                       dap(dr["ag_out"], dc * AGN + seg["downT"],
                           [[128, 128], [128 * 128, IC1], [1, 128]]),
                       deps=[st["ag"], dn_last[b] if dn_last[b] is not None
                             else st["x1_dmas"][-1]], note="dma_wd")
            for Ic in range(IC1):
                for tq2 in range(2):
                    m = p.op("t", lambda e, wb=wb, tq2=tq2, Ic=Ic: e.matmul(
                        S[banks[tq2]][:, :], wb[:, Ic, :], S["OQ"][:, Ic, ts(tq2, 512)],
                        start=(Ic == 0), stop=(Ic == IC1 - 1)),
                        [wl, oq_ops[Ic]] + ([out_dmas[-1]] if Ic == 0 and out_dmas
                                            else []), "mmd")
                    mms[tq2].append(m)
                    last_mmd = m
            dn_last[b] = last_mmd
            for tq2 in range(2):
                gt = th * 2 + tq2
                sil = p.op("s", lambda e, tq2=tq2: e.activation(
                    S["f1"][:, 0:512], S[banks[tq2]][:, :], AF.Silu,
                    scale=S["scal"][:, S_S4:S_S4 + 1]),
                    [mms[tq2][-1], last_x2], "sild")
                s1ld = p.dma("q", S["f3"][:, 0:512],
                             dap(dr["s1_sp"], dc * 128 * T + gt * 512,
                                 [[T, 128], [1, 512]]),
                             deps=[st["s1_dmas"][dc * NTQ + gt],
                                   out_dmas[-4] if len(out_dmas) >= 4 else None],
                             note="dma_s1ld")
                sgd = p.op("v", lambda e: e.tensor_mul(
                    S["f1"][:, 0:512], S["f1"][:, 0:512], S["f3"][:, 0:512]),
                    [sil, s1ld], "sgd")
                last_sgd = sgd
                x1ld = p.dma("q", S["f2"][:, 0:512],
                             dap(dr["x1_sp"], dc * 128 * T + gt * 512,
                                 [[T, 128], [1, 512]]),
                             deps=[st["x1_dmas"][st["x1_idx"](dc, gt)], sgd, last_tpo],
                             note="dma_x1f")
                x2 = p.op("v", lambda e: e.tensor_add(S["f2"][:, 0:512], S["f2"][:, 0:512],
                                                      S["f1"][:, 0:512]), [x1ld, sgd], "x2")
                last_x2 = x2
                for j in range(4):
                    bank = "pT" if j % 2 == 0 else "pT2"
                    tp = p.op("t", lambda e, j=j, bank=bank: e.transpose(
                        S[bank][:, :], S["f2"][:, ts(j, 128)], S["identF"][:, :]),
                        [x2, out_dmas[-2] if len(out_dmas) >= 2 else None], "tpo")
                    last_tpo = tp
                    cp = p.op("v", lambda e, j=j, bank=bank: e.tensor_copy(
                        S["f3"][:, 512 + j * 128:512 + (j + 1) * 128], S[bank][:, :]),
                        [tp, out_dmas[-4] if len(out_dmas) >= 4 else None], "cpo")
                    out_dmas.append(p.dma("q",
                                          dr["out"].ap()[gt * 512 + j * 128:
                                                         gt * 512 + (j + 1) * 128,
                                                         ts(dc, 128)],
                                          S["f3"][:, 512 + j * 128:512 + (j + 1) * 128],
                                          deps=[cp], note="dma_out"))
    st["out_dmas"] = out_dmas
    p.free("OQ")


def host_inputs(inputs, core):
    def cols(a):
        return np.ascontiguousarray(np.asarray(a, dtype=np.float32).reshape(8, 128).T)
    cw = np.asarray(inputs["L0_ffn_conv_w"], dtype=np.float32)[:, 0, :]
    cwp = cw.reshape(32, 128, 3).transpose(1, 0, 2)
    def sl(a):
        a = np.asarray(a, dtype=np.float32)
        n = a.shape[0] // NCORES
        return np.ascontiguousarray(a[core * n:(core + 1) * n])
    return {
        "x": np.ascontiguousarray(np.asarray(inputs["L0_attn_x"], dtype=np.float32)[core]),
        "bvq_c": cols(np.asarray(inputs["L0_attn_bv_q"]).reshape(-1)),
        "bvk_c": cols(np.asarray(inputs["L0_attn_bv_k"]).reshape(-1)),
        "bvv_c": cols(np.asarray(inputs["L0_attn_bv_v"]).reshape(-1)),
        "cw0_c": np.ascontiguousarray(cwp[:, :, 0]),
        "cw1_c": np.ascontiguousarray(cwp[:, :, 1]),
        "cw2_c": np.ascontiguousarray(cwp[:, :, 2]),
        "cb_c": np.ascontiguousarray(
            np.asarray(inputs["L0_ffn_conv_b"], dtype=np.float32).reshape(32, 128).T),
        "td_c": cols(inputs["L1_attn_time_decay"]),
        "tf_c": cols(inputs["L1_attn_time_first"]),
        "tmk_c": cols(np.asarray(inputs["L1_attn_time_mix_k"]).reshape(-1)),
        "tmv_c": cols(np.asarray(inputs["L1_attn_time_mix_v"]).reshape(-1)),
        "tmr_c": cols(np.asarray(inputs["L1_attn_time_mix_r"]).reshape(-1)),
        "w1_sh": sl(inputs["L0_ffn_w1"]),
        "w2_sh": sl(inputs["L0_ffn_w2"]),
        "wup_sh": sl(inputs["L1_ffn_w_up"]),
        "wgate_sh": sl(inputs["L1_ffn_w_gate"]),
        "wdown_sh": sl(inputs["L1_ffn_w_down"]),
        "wk_sh": sl(inputs["L1_attn_Wk"]),
        "wv_sh": sl(inputs["L1_attn_Wv"]),
        "wr_sh": sl(inputs["L1_attn_Wr"]),
        "wo_sh": sl(inputs["L1_attn_Wo"]),
    }


def kernel(**inputs):
    nc, p = build(debug_stop=DEBUG_STOP, debug_dumps=DEBUG_DUMPS)
    in_maps = [host_inputs(inputs, c) for c in range(NCORES)]
    res = run_bass_kernel_spmd(nc, in_maps, core_ids=list(range(NCORES)))
    return np.stack([np.asarray(res.results[c]["out"]) for c in range(NCORES)], axis=0)


# version 20 (final)
# speedup vs baseline: 1.0397x; 1.0059x over previous
import contextlib
import numpy as np
import concourse.bass as bass
import concourse.mybir as mybir
from concourse.bass_utils import run_bass_kernel_spmd

F32 = mybir.dt.float32
BF16 = mybir.dt.bfloat16
ALU = mybir.AluOpType
AF = mybir.ActivationFunctionType
AX = mybir.AxisListType.X

T = 2048
TH = 1024
D = 1024
DC = 8
H0 = 4096
HC0 = 32
I1 = 3072
IC1 = 24
NTQ = 4
RND = 12582912.0
CAP = 60.0
NCORES = 8
EPS_LN = 1e-5

DEBUG_STOP = None
DEBUG_DUMPS = ()

CHANS = ("t", "v", "s", "g", "q", "gq", "aq", "c")
ENG_OF = {"t": "t", "v": "v", "s": "s", "g": "g", "q": "q", "gq": "g", "aq": "s", "c": "g"}
ASYNC = ("q", "gq", "aq", "c")
NSUB = {"q": 16, "gq": 8, "aq": 4}


class Op:
    __slots__ = ("chan", "fn", "deps", "seq", "sig", "note", "sub")

    def __init__(self, chan, fn, deps, note):
        self.chan = chan
        self.fn = fn
        self.deps = deps
        self.seq = None
        self.sig = False
        self.note = note
        self.sub = 0


def is_dma(chan):
    return chan in ("q", "gq", "aq")


class P:
    def __init__(self, nc):
        self.nc = nc
        self.ops = []
        self.open_cm = {}
        self.sb = {}
        self.dr = {}
        self.seg = {}

    def alloc(self, name, shape, dtype, psum=False, side="left"):
        if psum:
            cm = self.nc.psum_tensor(name, shape, dtype)
        else:
            cm = self.nc.sbuf_tensor(name, shape, dtype, side=side)
        h = cm.__enter__()
        self.open_cm[name] = cm
        self.sb[name] = h
        return h

    def free(self, *names):
        for name in names:
            self.open_cm.pop(name).__exit__(None, None, None)

    def op(self, chan, fn, deps=(), note=""):
        o = Op(chan, fn, [d for d in deps if d is not None], note)
        self.ops.append(o)
        return o

    def dma(self, chan, out, in_, deps=(), note="dma", transpose=False):
        assert is_dma(chan)
        def fn(e):
            return e.dma_start(out=out, in_=in_, transpose=transpose)
        return self.op(chan, fn, deps, note)

    def barrier_deps(self):
        last = {}
        for o in self.ops:
            last[o.chan] = o
        return list(last.values())

    def emit(self, sems):
        for o in self.ops:
            if is_dma(o.chan) or o.chan in ("c", "v", "s", "g"):
                o.sig = True
            for d in o.deps:
                d.sig = True
        counters = {c: 0 for c in CHANS}
        subcnt = {}
        for o in self.ops:
            if not o.sig:
                continue
            if is_dma(o.chan):
                idx = counters[o.chan]
                counters[o.chan] += 1
                ns = NSUB[o.chan]
                o.sub = idx % ns
                k = subcnt.get((o.chan, o.sub), 0) + 1
                subcnt[(o.chan, o.sub)] = k
                o.seq = 16 * k
            else:
                counters[o.chan] += 1
                o.seq = counters[o.chan]
        eng_ops = {e: [] for e in ("t", "v", "s", "g", "q")}
        for o in self.ops:
            eng_ops[ENG_OF[o.chan]].append(o)

        def semof(chan, sub):
            return sems[chan][sub] if is_dma(chan) else sems[chan]

        def emit_engine(e, handle):
            waited = {}
            for o in eng_ops[e]:
                if o.chan in ("v", "s", "g") and o.seq is not None and o.seq > 1:
                    key = (o.chan, 0)
                    if o.seq - 1 > waited.get(key, 0):
                        handle.wait_ge(semof(o.chan, 0), o.seq - 1)
                        waited[key] = o.seq - 1
                for d in o.deps:
                    key = (d.chan, d.sub)
                    if d.seq > waited.get(key, 0):
                        handle.wait_ge(semof(d.chan, d.sub), d.seq)
                        waited[key] = d.seq
                if o.sig and is_dma(o.chan) and o.seq > 16:
                    key = (o.chan, o.sub)
                    if o.seq - 16 > waited.get(key, 0):
                        handle.wait_ge(semof(o.chan, o.sub), o.seq - 16)
                        waited[key] = o.seq - 16
                ins = o.fn(handle)
                if o.sig:
                    assert ins is not None, o.note
                    ins.then_inc(semof(o.chan, o.sub), 16 if is_dma(o.chan) else 1)

        with self.nc.Block() as b:
            @b.tensor
            def _(t):
                emit_engine("t", t)

            @b.vector
            def _(v):
                emit_engine("v", v)

            @b.scalar
            def _(s):
                emit_engine("s", s)

            @b.gpsimd
            def _(g):
                emit_engine("g", g)

            @b.sync
            def _(q):
                emit_engine("q", q)


def dap(dr_tensor, offset, pattern):
    return bass.AP(dr_tensor, offset, pattern)


def ts(i, w):
    return slice(i * w, (i + 1) * w)


S_SW1, S_SW2, S_RW1, S_RW2 = 0, 1, 2, 3
S_RS0, S_S0, S_RS1, S_S1 = 4, 5, 6, 7
S_RS2, S_S2, S_RS3, S_S3 = 8, 9, 10, 11
S_RS4, S_S4, S_S265536, S_S3S1 = 12, 13, 14, 15


def build(debug_stop=None, debug_dumps=(), debug_dram=()):
    nc = bass.Bass("TRN2", target_bir_lowering=False)
    p = P(nc)
    dr = p.dr
    dr["x"] = nc.dram_tensor("x", [T, D], F32, kind="ExternalInput")
    dr["out"] = nc.dram_tensor("out", [T, D], F32, kind="ExternalOutput")
    for nm in ["bvq_c", "bvk_c", "bvv_c", "td_c", "tf_c", "tmk_c", "tmv_c", "tmr_c"]:
        dr[nm] = nc.dram_tensor(nm, [128, 8], F32, kind="ExternalInput")
    for nm in ["cw0_c", "cw1_c", "cw2_c", "cb_c"]:
        dr[nm] = nc.dram_tensor(nm, [128, 32], F32, kind="ExternalInput")
    dr["w1_sh"] = nc.dram_tensor("w1_sh", [512, 1024], F32, kind="ExternalInput")
    dr["w2_sh"] = nc.dram_tensor("w2_sh", [128, 4096], F32, kind="ExternalInput")
    dr["wup_sh"] = nc.dram_tensor("wup_sh", [384, 1024], F32, kind="ExternalInput")
    dr["wgate_sh"] = nc.dram_tensor("wgate_sh", [384, 1024], F32, kind="ExternalInput")
    dr["wdown_sh"] = nc.dram_tensor("wdown_sh", [128, 3072], F32, kind="ExternalInput")
    for nm in ["wk_sh", "wv_sh", "wr_sh", "wo_sh"]:
        dr[nm] = nc.dram_tensor(nm, [128, 1024], F32, kind="ExternalInput")

    seg = p.seg
    off = 0
    for nm, n in [("w1T", 1024 * 512), ("w2T", 4096 * 128), ("upT", 1024 * 384),
                  ("gateT", 1024 * 384), ("downT", 3072 * 128), ("kT", 1024 * 128),
                  ("vT", 1024 * 128), ("rT", 1024 * 128), ("oT", 1024 * 128)]:
        seg[nm] = off
        off += n
    p.AGN = off
    dr["ag_in"] = nc.dram_tensor("ag_in", [p.AGN], BF16)
    dr["ag_out"] = nc.dram_tensor("ag_out", [NCORES * p.AGN], BF16)
    dr["ar_in"] = nc.dram_tensor("ar_in", [8], F32)
    dr["ar_out"] = nc.dram_tensor("ar_out", [8], F32)
    dr["hsil_sp"] = nc.dram_tensor("hsil_sp", [H0 * T], BF16)
    dr["up_sp"] = nc.dram_tensor("up_sp", [I1 * T], F32)
    dr["gate_sp"] = nc.dram_tensor("gate_sp", [I1 * T], F32)
    dr["gate2_sp"] = nc.dram_tensor("gate2_sp", [I1 * T], F32)
    dr["o_sp"] = nc.dram_tensor("o_sp", [I1 * T], F32)
    dr["x1_sp"] = nc.dram_tensor("x1_sp", [D * T], F32)
    dr["xn_sp"] = nc.dram_tensor("xn_sp", [D * T], F32)
    dr["xn1_sp"] = nc.dram_tensor("xn1_sp", [D * T], F32)
    dr["s1_sp"] = nc.dram_tensor("s1_sp", [D * T], F32)

    p.alloc("identF", [128, 128], F32)
    p.alloc("identB", [128, 128], BF16)
    p.alloc("ones", [128, 128], F32)
    for nm in ["sqk_c", "sv_c", "w_c", "nu_c", "tmk", "tmv", "tmr"]:
        p.alloc(nm, [128, 8], F32)
    for nm in ["cw0", "cw1", "cw2", "cb"]:
        p.alloc(nm, [128, 32], F32)
    p.alloc("scal", [128, 16], F32)
    p.alloc("epsc", [128, 1], F32)
    p.alloc("rndc", [128, 2], F32)
    p.alloc("stat", [128, 64], F32)
    p.alloc("tiny", [128, 16], F32)
    p.alloc("tiny2", [128, 16], F32)
    p.alloc("tinyb", [128, 16], BF16)
    p.alloc("mxc", [128, 64], F32)
    p.alloc("flatrow", [1, 512], F32)
    p.alloc("flat16", [16, 128], F32)
    p.alloc("f1", [128, T], F32)
    p.alloc("f2", [128, T], F32)
    p.alloc("f3", [128, T], F32)
    p.alloc("g1s", [128, T], BF16)
    p.alloc("g2s", [128, T], BF16)
    p.alloc("wt0", [128, DC, 128], BF16)
    p.alloc("wt1", [128, DC, 128], BF16)
    p.alloc("wt2", [128, HC0, 128], BF16, side="right")
    p.alloc("wt3", [128, HC0, 128], BF16, side="right")
    p.alloc("pA", [128, 512], F32, psum=True)
    p.alloc("pB", [128, 512], F32, psum=True)
    p.alloc("pC", [128, 512], F32, psum=True)
    p.alloc("pD", [128, 512], F32, psum=True)
    p.alloc("pE", [128, 512], F32, psum=True)
    p.alloc("pT", [128, 128], F32, psum=True)
    p.alloc("pT2", [128, 128], F32, psum=True)
    p.alloc("pTb", [128, 128], BF16, psum=True)

    st = {}
    ph0(p, st)
    ph1(p, st)
    if debug_stop != "p1":
        ph2(p, st)
        if debug_stop != "p2":
            ph3(p, st)
            if debug_stop != "p3":
                ph4(p, st)
                if debug_stop != "p4":
                    ph5(p, st)
                    if debug_stop != "p5":
                        ph6(p, st)

    if debug_dram:
        bd = p.barrier_deps()
        dmp = []
        for nm in debug_dram:
            t = dr[nm]
            n = int(np.prod(t.shape))
            dt = nc.dram_tensor("ddump_" + nm, [n], t.dtype, kind="ExternalOutput")
            dmp.append(p.dma("q", dt.ap(), t.ap(), deps=bd, note="dma_ddump"))
        p.op("g", lambda e: e.nop(), deps=dmp, note="ddump_wait")
    if debug_dumps:
        bd = p.barrier_deps()
        dmp = []
        for nm in debug_dumps:
            t = p.sb[nm]
            shape = list(t.shape)
            flat = [shape[0], int(np.prod(shape[1:]))]
            dt = nc.dram_tensor("dump_" + nm, flat, t.dtype, kind="ExternalOutput")
            ap = t[:, :] if len(shape) == 2 else t.ap().rearrange("p a b -> p (a b)")
            dmp.append(p.dma("q", dt.ap(), ap, deps=bd, note="dma_dump"))
        p.op("g", lambda e: e.nop(), deps=dmp, note="dump_wait")

    if "out_dmas" in st:
        p.op("g", lambda e: e.nop(), deps=st["out_dmas"][-64:], note="final_wait")

    stk = contextlib.ExitStack()
    sems = {}
    for c in CHANS:
        if c in NSUB:
            sems[c] = [stk.enter_context(nc.semaphore(f"sem_{c}{i}"))
                       for i in range(NSUB[c])]
        else:
            sems[c] = stk.enter_context(nc.semaphore("sem_" + c))
    p.emit(sems)
    for name in reversed(list(p.open_cm)):
        p.open_cm[name].__exit__(None, None, None)
    p.open_cm.clear()
    stk.close()
    return nc, p


def quant3_col_g(p, out_ap, in_ap, scale_col, lo, hi, deps, tmp):
    return quant3_col(p, out_ap, in_ap, scale_col, lo, hi, deps, tmp)


def quant3_col(p, out_ap, in_ap, scale_col, lo, hi, deps, tmp, chan3="v"):
    S = p.sb
    o1 = p.op("s", lambda e: e.activation(tmp, in_ap, AF.Copy, scale=scale_col,
                                          bias=RND), deps, "q3a")
    o2 = p.op("s", lambda e: e.activation(tmp, tmp, AF.Copy, bias=-RND), [o1], "q3b")
    return p.op("v", lambda e: e.tensor_scalar(out_ap, tmp, hi, lo, ALU.min, ALU.max),
                [o2], "q3c")


def quant3_row(p, out_ap, in_ap, row_ap, lo, hi, deps, tmp, chans=None):
    S = p.sb
    o1 = p.op("v", lambda e: e.tensor_mul(tmp, in_ap, row_ap), deps, "qra")
    o2 = p.op("s", lambda e: e.activation(tmp, tmp, AF.Copy, bias=RND), [o1], "qrb1")
    o3 = p.op("s", lambda e: e.activation(tmp, tmp, AF.Copy, bias=-RND), [o2], "qrb2")
    return p.op("v", lambda e: e.tensor_scalar(out_ap, tmp, hi, lo, ALU.min, ALU.max),
                [o3], "qrc")


def bcast_p0(p, src_row, dst, n, deps, tag):
    S = p.sb
    mm = p.op("t", lambda e: e.matmul(S["pT2"][:, 0:n], S["ones"][0:1, :], src_row,
                                      start=True, stop=True), deps, "bc_mm_" + tag)
    return p.op("v", lambda e: e.tensor_copy(dst, S["pT2"][:, 0:n]), [mm], "bc_cp_" + tag)


def row_bcast(p, st, cols16, dst, deps, tag):
    S = p.sb
    guard = getattr(p, "rb_guard", [])
    tp = p.op("t", lambda e: e.transpose(S["pT2"][0:16, 0:128], cols16, S["identF"][:, :]),
              list(deps) + [st["consts"]] + guard, "rb_tp_" + tag)
    cp = p.op("v", lambda e: e.tensor_copy(S["flat16"][0:16, :], S["pT2"][0:16, 0:128]),
              [tp], "rb_cp_" + tag)
    last = None
    allfls = []
    for tq in range(NTQ):
        fls = []
        for r in range(4):
            i = tq * 4 + r
            fls.append(p.dma("gq", S["flatrow"][0:1, ts(r, 128)],
                             S["flat16"][i:i + 1, :], deps=[cp, last], note="dma_flat"))
        mm = p.op("t", lambda e: e.matmul(S["pE"][:, :], S["ones"][0:1, :],
                                          S["flatrow"][0:1, :], start=True, stop=True),
                  fls + [last], "rb_mm_" + tag)
        last = p.op("v", lambda e, tq=tq: e.tensor_copy(dst[:, ts(tq, 512)], S["pE"][:, :]),
                    [mm], "rb_ev_" + tag)
        allfls += fls
    p.rb_guard = allfls
    return last


def global_max_ar(p, st, cols_ap, slot, s_idx, rs_idx, tag, deps):
    S = p.sb
    dr = p.dr
    cl = p.op("v", lambda e: e.reduce_max(S["mxc"][:, 62:63], cols_ap, AX),
              list(deps) + [st["ar_last"]], "cl_" + tag)
    tp = p.op("t", lambda e: e.transpose(S["pT2"][0:1, 0:128], S["mxc"][:, 62:63],
                                         S["identF"][:, :]), [cl], "tp_" + tag)
    rd = p.op("v", lambda e: e.reduce_max(S["mxc"][0:1, 63:64], S["pT2"][0:1, 0:128], AX),
              [tp], "rd_" + tag)
    d1 = p.dma("gq", dr["ar_in"].ap()[slot:slot + 1], S["mxc"][0:1, 63:64], deps=[rd],
               note="dma_" + tag)
    ar = p.op("c", lambda e: e.collective_compute(
        "AllReduce", ALU.max, replica_groups=[list(range(NCORES))],
        ins=[dr["ar_in"].ap()[slot:slot + 1].opt()],
        outs=[dr["ar_out"].ap()[slot:slot + 1].opt()]), [d1, st["ar_last"]], "AR_" + tag)
    st["ar_last"] = ar
    lb = p.dma("gq", S["mxc"][0:1, 62:63], dr["ar_out"].ap()[slot:slot + 1],
               deps=[ar, tp], note="dma_b" + tag)
    bc = bcast_p0(p, S["mxc"][0:1, 62:63], S["scal"][:, s_idx:s_idx + 1], 1, [lb], tag)
    sc1 = p.op("v", lambda e: e.tensor_scalar(S["scal"][:, s_idx:s_idx + 1],
                                              S["scal"][:, s_idx:s_idx + 1],
                                              1.0 / 32767.0, 1e-12,
                                              ALU.mult, ALU.max), [bc], "s_" + tag)
    return p.op("v", lambda e: e.reciprocal(S["scal"][:, rs_idx:rs_idx + 1],
                                            S["scal"][:, s_idx:s_idx + 1]), [sc1],
                "rs_" + tag)


def wts_sel(S, i):
    return S["wt0"] if i % 2 == 0 else S["wt1"]


def ph0(p, st):
    S = p.sb
    dr = p.dr
    x = dr["x"].ap()

    p.alloc("SA", [128, DC, T], F32)
    p.alloc("XT", [128, 16, 1024], F32)
    p.alloc("xn_t0", [128, 1024], F32)
    p.alloc("xn_t1", [128, 1024], F32)
    p.alloc("sqd", [128, 1024], F32)

    prm = {}
    for srcn, dstn in [("bvq_c", "sqk_c"), ("bvv_c", "sv_c"), ("cw0_c", "cw0"),
                       ("cw1_c", "cw1"), ("cw2_c", "cw2"), ("cb_c", "cb"),
                       ("td_c", "w_c"), ("tf_c", "nu_c"), ("tmk_c", "tmk"),
                       ("tmv_c", "tmv"), ("tmr_c", "tmr")]:
        prm[dstn] = p.dma("gq", S[dstn][:, :], dr[srcn].ap(), note="dma_prm")
    bvk = None

    cz1 = p.op("g", lambda e: e.memset(S["identF"][:, :], 0.0), (), "czF")
    ca1 = p.op("g", lambda e: e.affine_select(
        S["identF"][:, :], S["identF"][:, :], [[-1, 128]], ALU.not_equal, 1.0,
        base=0, channel_multiplier=1), [cz1], "caF")
    cz2 = p.op("g", lambda e: e.memset(S["identB"][:, :], 0.0), (), "czB")
    ca2 = p.op("g", lambda e: e.affine_select(
        S["identB"][:, :], S["identB"][:, :], [[-1, 128]], ALU.not_equal, 1.0,
        base=0, channel_multiplier=1), [cz2], "caB")
    cz3 = p.op("g", lambda e: e.memset(S["ones"][:, :], 1.0), (), "czO")
    cz4 = p.op("g", lambda e: e.memset(S["epsc"][:, :], EPS_LN), (), "czE")
    czr = p.op("g", lambda e: e.memset(S["rndc"][:, 0:1], RND), (), "czR1")
    czr2 = p.op("g", lambda e: e.memset(S["rndc"][:, 1:2], -RND), (), "czR2")
    cz5 = p.op("g", lambda e: e.memset(S["scal"][:, :], 0.0), (), "czS")
    cz6 = p.op("g", lambda e: e.memset(S["mxc"][:, :], 0.0), (), "czM")
    cz7 = p.op("g", lambda e: e.memset(S["stat"][:, :], 0.0), (), "czT")
    cz8 = p.op("g", lambda e: e.memset(S["tiny"][:, :], 0.0), (), "czU")
    cz9 = p.op("g", lambda e: e.memset(S["tiny2"][:, :], 0.0), (), "czV")
    consts = p.op("g", lambda e: e.engine_nop(),
                  [ca1, ca2, cz3, cz4, cz5, cz6, cz7, cz8, cz9, czr, czr2], "consts")
    st["consts"] = consts
    bvk = p.dma("gq", S["tiny2"][:, 0:8], dr["bvk_c"].ap(), deps=[cz9], note="dma_prm")

    s1 = p.op("s", lambda e: e.sign(S["sqk_c"][:, :], S["sqk_c"][:, :]), [prm["sqk_c"]], "sgq")
    s2 = p.op("s", lambda e: e.sign(S["tiny2"][:, 0:8], S["tiny2"][:, 0:8]), [bvk], "sgk")
    s3 = p.op("s", lambda e: e.sign(S["sv_c"][:, :], S["sv_c"][:, :]), [prm["sv_c"]], "sgv")
    st["sqk"] = p.op("v", lambda e: e.tensor_mul(S["sqk_c"][:, :], S["sqk_c"][:, :],
                                                 S["tiny2"][:, 0:8]), [s1, s2], "sqkm")
    st["sv"] = s3
    w1e = p.op("s", lambda e: e.activation(S["w_c"][:, :], S["w_c"][:, :], AF.Exp),
               [prm["w_c"]], "wexp")
    st["wc"] = p.op("v", lambda e: e.tensor_scalar(S["w_c"][:, :], S["w_c"][:, :], -1.0,
                                                   None, ALU.mult), [w1e], "wneg")
    st["nu"] = p.op("v", lambda e: e.tensor_scalar(S["nu_c"][:, :], S["nu_c"][:, :], -1.0,
                                                   None, ALU.mult), [prm["nu_c"]], "nuneg")

    loads, sq_ops, sm_ops = [], [], []
    for i in range(16):
        ld = p.dma("q", S["XT"][:, i, :], x[ts(i, 128), :], note="dma_x")
        loads.append(ld)
        sq1 = p.op("v", lambda e, i=i: e.tensor_mul(
            S["sqd"][:, :], S["XT"][:, i, :], S["XT"][:, i, :]), [ld, consts], "sq1")
        sq_ops.append(p.op("v", lambda e, i=i: e.reduce_sum(
            S["stat"][:, 16 + i:17 + i], S["sqd"][:, :], AX), [sq1], "sq2"))
        sm_ops.append(p.op("v", lambda e, i=i: e.reduce_sum(
            S["stat"][:, i:i + 1], S["XT"][:, i, :], AX), [ld, consts], "sm"))
    f1_ = p.op("v", lambda e: e.tensor_scalar(S["stat"][:, 0:16], S["stat"][:, 0:16],
                                              1.0 / D, None, ALU.mult), sm_ops, "mean")
    f2_ = p.op("v", lambda e: e.tensor_scalar(S["stat"][:, 16:32], S["stat"][:, 16:32],
                                              1.0 / D, None, ALU.mult), sq_ops, "ex2")
    f3_ = p.op("v", lambda e: e.tensor_mul(S["tiny"][:, 0:16], S["stat"][:, 0:16],
                                           S["stat"][:, 0:16]), [f1_], "m2")
    f4_ = p.op("v", lambda e: e.tensor_sub(S["stat"][:, 16:32], S["stat"][:, 16:32],
                                           S["tiny"][:, 0:16]), [f2_, f3_], "var")
    f5_ = p.op("s", lambda e: e.activation(S["stat"][:, 16:32], S["stat"][:, 16:32],
                                           AF.Sqrt, bias=S["epsc"][:, 0:1]),
               [f4_, st["consts"]], "sd")
    f6_ = p.op("v", lambda e: e.reciprocal(S["stat"][:, 16:32], S["stat"][:, 16:32]),
               [f5_], "rstd")
    nmean = p.op("v", lambda e: e.tensor_scalar(S["tiny"][:, 0:16], S["stat"][:, 0:16],
                                                -1.0, None, ALU.mult), [f6_], "nmean")

    prev_cp = {}
    last_mx = None
    for i in range(16):
        buf = S["xn_t0"] if i % 2 == 0 else S["xn_t1"]
        nrm = p.op("v", lambda e, i=i, buf=buf: e.tensor_scalar(
            buf[:, :], S["XT"][:, i, :], S["tiny"][:, i:i + 1],
            S["stat"][:, 16 + i:17 + i], ALU.add, ALU.mult),
            [nmean, prev_cp.get((i - 2, DC - 1))], "nrm")
        last_mx = p.op("v", lambda e, i=i, buf=buf: e.reduce_max(
            S["stat"][:, 32 + i:33 + i], buf[:, :], AX, apply_absolute_value=True),
            [nrm], "sxmax")
        for dc in range(DC):
            bank = "pT" if (i * DC + dc) % 2 == 0 else "pT2"
            tp = p.op("t", lambda e, buf=buf, dc=dc, bank=bank: e.transpose(
                S[bank][:, :], buf[:, ts(dc, 128)], S["identF"][:, :]),
                [nrm, consts, prev_cp.get((i, dc - 2) if dc >= 2 else (i - 1, DC - 2 + dc))],
                "tpxn")
            cp = p.op("s", lambda e, i=i, dc=dc, bank=bank: e.copy(
                S["SA"][:, dc, ts(i, 128)], S[bank][:, :]), [tp], "cpxn")
            prev_cp[(i, dc)] = cp
    st["xn_done"] = [prev_cp[(15, DC - 1)], prev_cp[(15, DC - 2)], last_mx]
    st["sx_done"] = last_mx
    st["xt_free"] = [nrm, last_mx, sq_ops[-1], sm_ops[-1]]
    p.free("sqd", "xn_t1", "xn_t0", "XT")


def ph1(p, st):
    S = p.sb
    dr = p.dr
    seg = p.seg
    AGN = p.AGN

    p.alloc("SXB", [128, T], BF16)
    p.alloc("RSXW1B", [128, T], F32)
    p.alloc("ws0", [128, 4096], F32)
    p.alloc("ws2", [128, 4096], F32)
    p.alloc("wq0", [128, 4096], BF16)
    p.alloc("wqt", [128, 32, 128], BF16)
    p.alloc("lt", [128, 96], F32)
    p.alloc("lt2", [128, 96], F32)

    ag_parts = []
    ld_w1 = [p.dma("q", S["ws0"][:, ts(j, 1024)], dr["w1_sh"].ap()[ts(j, 128), :],
                   deps=st["xt_free"], note="dma_w1s") for j in range(4)]
    ld_w2 = p.dma("q", S["ws2"][:, :], dr["w2_sh"].ap()[:, :], deps=st["xt_free"],
                  note="dma_w2s")
    a1 = p.op("v", lambda e: e.reduce_sum(S["tiny"][:, 0:1], S["ws0"][:, :], AX,
                                          apply_absolute_value=True), ld_w1, "absw1")
    a2 = p.op("v", lambda e: e.reduce_sum(S["tiny"][:, 1:2], S["ws2"][:, :], AX,
                                          apply_absolute_value=True), [ld_w2], "absw2")
    mm = p.op("t", lambda e: e.matmul(S["pT2"][:, 0:2], S["ones"][:, :], S["tiny"][:, 0:2],
                                      start=True, stop=True),
              [a1, a2, st["consts"]] + st["xn_done"], "summ")
    cps = p.op("v", lambda e: e.tensor_copy(S["mxc"][:, 56:58], S["pT2"][:, 0:2]), [mm], "sumc")
    d1 = p.dma("gq", dr["ar_in"].ap()[0:2], S["mxc"][0:1, 56:58], deps=[cps], note="dma_ar")
    ar = p.op("c", lambda e: e.collective_compute(
        "AllReduce", ALU.add, replica_groups=[list(range(NCORES))],
        ins=[dr["ar_in"].ap()[0:2].opt()], outs=[dr["ar_out"].ap()[0:2].opt()]),
        [d1], "AR_means")
    st["ar_last"] = ar
    lb = p.dma("gq", S["mxc"][0:1, 58:60], dr["ar_out"].ap()[0:2], deps=[ar], note="dma_arb")
    bc = bcast_p0(p, S["mxc"][0:1, 58:60], S["scal"][:, S_RW1:S_RW1 + 2], 2, [lb], "wm")
    c1 = p.op("v", lambda e: e.tensor_scalar(S["scal"][:, S_RW1:S_RW1 + 2],
                                             S["scal"][:, S_RW1:S_RW1 + 2],
                                             1.0 / (4096.0 * 1024.0),
                                             1e-5, ALU.mult, ALU.max), [bc], "wmc")
    sw = p.op("v", lambda e: e.reciprocal(S["scal"][:, S_SW1:S_SW1 + 2],
                                          S["scal"][:, S_RW1:S_RW1 + 2]), [c1], "wmr")
    st["sw"] = sw

    prev_ag = None
    for j in range(4):
        qd = quant3_col(p, S["wq0"][:, ts(j, 1024)], S["ws0"][:, ts(j, 1024)],
                        S["scal"][:, S_SW1:S_SW1 + 1], -1.0, 1.0, [sw, ld_w1[j]],
                        S["ws0"][:, ts(j, 1024)])
        tps = [p.dma("q", S["wqt"][:, dc, :],
                     S["wq0"][:, j * 1024 + dc * 128:j * 1024 + (dc + 1) * 128],
                     deps=[qd, prev_ag], note="dma_tp", transpose=True) for dc in range(8)]
        prev_ag = p.dma("gq", dap(dr["ag_in"], seg["w1T"] + j * 128,
                                  [[512, 128], [128 * 512, 8], [1, 128]]),
                        S["wqt"][:, 0:8, :], deps=tps, note="dma_agw")
        ag_parts.append(prev_ag)

    qd2 = quant3_col(p, S["wq0"][:, :], S["ws2"][:, :], S["scal"][:, S_SW2:S_SW2 + 1],
                     -1.0, 1.0, [sw, ld_w2, prev_ag], S["ws2"][:, :])
    tps = [p.dma("q", S["wqt"][:, hc, :], S["wq0"][:, ts(hc, 128)],
                 deps=[qd2, prev_ag], note="dma_tp", transpose=True) for hc in range(32)]
    prev_ag = p.dma("gq", dap(dr["ag_in"], seg["w2T"],
                              [[128, 128], [128 * 128, 32], [1, 128]]),
                    S["wqt"][:, 0:32, :], deps=tps, note="dma_agw2")
    ag_parts.append(prev_ag)

    def lpbq(src, rows, width, segnm, prev_ag, prev_m4):
        ntile = rows // 128
        nblk = width // 32
        nw = width // 128
        outs = []
        for j in range(ntile):
            ld = p.dma("q", S["ws0"][:, 0:width], src.ap()[ts(j, 128), :],
                       deps=[prev_m4], note="dma_lw")
            w3 = S["ws0"][:, 0:width].rearrange("p (b i) -> p b i", i=32)
            bm = p.op("v", lambda e, w3=w3, nblk=nblk: e.reduce_max(
                S["lt"][:, 0:nblk], w3, AX, apply_absolute_value=True), [ld], "lpbm")
            sc = p.op("v", lambda e, nblk=nblk: e.tensor_scalar(
                S["lt"][:, 0:nblk], S["lt"][:, 0:nblk], 1.0 / 7.0, 1e-12,
                ALU.mult, ALU.max), [bm], "lps")
            rc = p.op("v", lambda e, nblk=nblk: e.reciprocal(
                S["lt2"][:, 0:nblk], S["lt"][:, 0:nblk]), [sc], "lprs")
            rs_exp = dap(S["lt2"], 0, [[96, 128], [1, nblk], [0, 32]])
            s_exp = dap(S["lt"], 0, [[96, 128], [1, nblk], [0, 32]])
            m1 = p.op("v", lambda e, w3=w3, rs_exp=rs_exp: e.tensor_tensor(
                w3, w3, rs_exp, ALU.mult), [rc], "lpm1")
            m2 = p.op("v", lambda e, width=width: e.tensor_scalar(
                S["ws0"][:, 0:width], S["ws0"][:, 0:width], RND, -RND, ALU.add, ALU.add),
                [m1], "lprnd")
            m3 = p.op("v", lambda e, width=width: e.tensor_scalar(
                S["ws0"][:, 0:width], S["ws0"][:, 0:width], 7.0, -8.0, ALU.min, ALU.max),
                [m2], "lpclip")
            wq3 = S["wq0"][:, 0:width].rearrange("p (b i) -> p b i", i=32)
            m4 = p.op("v", lambda e, wq3=wq3, w3=w3, s_exp=s_exp: e.tensor_tensor(
                wq3, w3, s_exp, ALU.mult), [m3, prev_ag], "lpdeq")
            tps = [p.dma("q", S["wqt"][:, dc, :], S["wq0"][:, ts(dc, 128)],
                         deps=[m4, prev_ag], note="dma_tp", transpose=True)
                   for dc in range(nw)]
            prev_ag = p.dma("gq", dap(dr["ag_in"], seg[segnm] + j * 128,
                                      [[rows, 128], [128 * rows, nw], [1, 128]]),
                            S["wqt"][:, 0:nw, :], deps=tps, note="dma_aglp")
            outs.append(prev_ag)
            prev_m4 = m4
        return outs, prev_ag, prev_m4

    o1, prev_ag, pm = lpbq(dr["wup_sh"], 384, 1024, "upT", prev_ag, qd2)
    ag_parts += o1
    o2, prev_ag, pm = lpbq(dr["wgate_sh"], 384, 1024, "gateT", prev_ag, pm)
    ag_parts += o2
    o3, prev_ag, pm = lpbq(dr["wdown_sh"], 128, 3072, "downT", prev_ag, pm)
    ag_parts += o3

    prev_ldw = pm
    for nm, sg in [("wk_sh", "kT"), ("wv_sh", "vT"), ("wr_sh", "rT"), ("wo_sh", "oT")]:
        ld = p.dma("gq", S["wq0"][:, 0:1024], dr[nm].ap()[:, :], deps=[prev_ag, prev_ldw],
                   note="dma_wc")
        tps = [p.dma("q", S["wqt"][:, dc, :], S["wq0"][:, ts(dc, 128)],
                     deps=[ld, prev_ag], note="dma_tp", transpose=True) for dc in range(8)]
        prev_ag = p.dma("gq", dap(dr["ag_in"], seg[sg],
                                  [[128, 128], [128 * 128, 8], [1, 128]]),
                        S["wqt"][:, 0:8, :], deps=tps, note="dma_agk")
        ag_parts.append(prev_ag)
        prev_ldw = ld

    ag = p.op("c", lambda e: e.collective_compute(
        "AllGather", ALU.bypass, replica_groups=[list(range(NCORES))],
        ins=[dr["ag_in"].ap().opt()], outs=[dr["ag_out"].ap().opt()]),
        [ar] + ag_parts, "AG_w")
    st["ag"] = ag
    st["ar_last"] = ag

    sx1 = p.op("v", lambda e: e.tensor_scalar(S["stat"][:, 48:64], S["stat"][:, 32:48],
                                              1e-5, None, ALU.max), [st["sx_done"]], "sxc")
    sx2 = p.op("v", lambda e: e.reciprocal(S["tiny"][:, 0:16], S["stat"][:, 48:64]),
               [sx1], "sxr")
    sx3 = p.op("v", lambda e: e.tensor_scalar(S["tinyb"][:, 0:16], S["tiny"][:, 0:16],
                                              127.0, None, ALU.mult), [sx2], "sx127")
    sx3b = p.op("v", lambda e: e.tensor_copy(S["tiny"][:, 0:16], S["tinyb"][:, 0:16]),
                [sx3], "sxbk")
    sx4 = p.op("v", lambda e: e.reciprocal(S["tiny2"][:, 0:16], S["tiny"][:, 0:16]),
               [sx3b], "rsx")
    sx5 = p.op("v", lambda e: e.tensor_scalar(S["tiny2"][:, 0:16], S["tiny2"][:, 0:16],
                                              S["scal"][:, S_RW1:S_RW1 + 1], None,
                                              ALU.mult), [sx4, sw], "rsxw")
    st["sxb"] = row_bcast(p, st, S["tiny"][:, 0:16], S["SXB"],
                          [sx3b, sw] + st["xn_done"], "sxb")
    st["rsxw1b"] = row_bcast(p, st, S["tiny2"][:, 0:16], S["RSXW1B"],
                             [sx5, sw] + st["xn_done"], "rsxw")
    p.free("lt2", "lt", "wqt", "wq0", "ws2", "ws0")


def ph2(p, st):
    S = p.sb
    p.alloc("s_bf", [128, DC, T], BF16, side="right")
    p.alloc("cmask", [128, 4, 512], BF16, side="right")
    p.alloc("Qsb", [128, T], BF16)
    p.alloc("Xb", [128, T], BF16)
    p.alloc("Vbf", [128, T], BF16)
    p.alloc("Vtm", [128, 16, 128], BF16)
    p.alloc("attA0", [128, 512], BF16)
    p.alloc("attA1", [128, 512], BF16)
    p.alloc("attB0", [128, 512], BF16)
    p.alloc("attB1", [128, 512], BF16)

    cmk = []
    for rr in range(4):
        mz = p.op("g", lambda e, rr=rr: e.memset(S["cmask"][:, rr, :], 1.0),
                  [st["consts"]], "cmz")
        cmk.append(p.op("g", lambda e, rr=rr: e.affine_select(
            S["cmask"][:, rr, :], S["cmask"][:, rr, :], [[1, 512]], ALU.is_ge, 0.0,
            base=-128 * rr, channel_multiplier=-1), [mz], "cma"))
    evacs = []
    hp_fence = (list(st["xn_done"]) + [st["sqk"], st["sv"], st["consts"],
                st["sxb"], st["rsxw1b"]] + cmk)
    for hp in range(DC):
        qs = p.op("v", lambda e, hp=hp: e.tensor_scalar(
            S["Qsb"][:, :], S["SA"][:, hp, :], S["sqk_c"][:, hp:hp + 1], None, ALU.mult),
            hp_fence, "qs")
        xb = p.op("v", lambda e, hp=hp: e.tensor_copy(S["Xb"][:, :], S["SA"][:, hp, :]),
                  hp_fence, "xb")
        vb = p.op("v", lambda e, hp=hp: e.tensor_scalar(
            S["Vbf"][:, :], S["SA"][:, hp, :], S["sv_c"][:, hp:hp + 1], None, ALU.mult),
            hp_fence, "vb")
        vtm_ev = {}
        prev_ev = None
        for kc in range(16):
            tp = p.op("t", lambda e, kc=kc: e.transpose(
                S["pTb"][:, :], S["Vbf"][:, ts(kc, 128)], S["identB"][:, :]),
                [vb, prev_ev] + (hp_fence if kc == 0 else []), "vtp")
            ev = p.op("s", lambda e, kc=kc: e.copy(S["Vtm"][:, kc, :], S["pTb"][:, :]),
                      [tp], "vev")
            vtm_ev[kc] = ev
            prev_ev = ev
        evacs = []
        prev_sg = {}
        prev_av = {}
        for qt in range(NTQ):
            avbank = "pC"
            nkc = 4 * (qt + 1)
            last_avB = None
            for kc in range(nkc):
                attA = S["attA0"] if kc % 2 == 0 else S["attA1"]
                attB = S["attB0"] if kc % 2 == 0 else S["attB1"]
                bkA = "pA" if kc % 2 == 0 else "pB"
                bkB = "pD" if kc % 2 == 0 else "pE"
                qkA = p.op("t", lambda e, kc=kc, qt=qt, bkA=bkA: e.matmul(
                    S[bkA][:, :], S["Qsb"][0:64, ts(kc, 128)],
                    S["Xb"][0:64, ts(qt, 512)], start=True, stop=True,
                    tile_position=(0, 0), skip_group_check=True),
                    [qs, xb, prev_sg.get(("A", qt, kc - 2)) if kc >= 2
                     else prev_sg.get(("A", qt - 1, 4 * qt - 2 + kc))], "qkA")
                qkB = p.op("t", lambda e, kc=kc, qt=qt, bkB=bkB: e.matmul(
                    S[bkB][:, :], S["Qsb"][64:128, ts(kc, 128)],
                    S["Xb"][64:128, ts(qt, 512)], start=True, stop=True,
                    tile_position=(64, 0), skip_group_check=True),
                    [qs, xb, prev_sg.get(("B", qt, kc - 2)) if kc >= 2
                     else prev_sg.get(("B", qt - 1, 4 * qt - 2 + kc))], "qkB")
                sgA = p.op("s", lambda e, attA=attA, bkA=bkA: e.activation(
                    attA[:, :], S[bkA][:, :], AF.Sigmoid, scale=0.5),
                    [qkA, prev_av.get(("A", qt, kc - 2)) if kc >= 2
                     else prev_av.get(("A", qt - 1, 4 * qt - 1))], "sgA")
                sgB = p.op("s", lambda e, attB=attB, bkB=bkB: e.activation(
                    attB[:, :], S[bkB][:, :], AF.Sigmoid, scale=0.5),
                    [qkB, prev_av.get(("B", qt, kc - 2)) if kc >= 2
                     else prev_av.get(("B", qt - 1, 4 * qt - 1))], "sgB")
                prev_sg[("A", qt, kc)] = sgA
                prev_sg[("B", qt, kc)] = sgB
                depA, depB = sgA, sgB
                if kc >= 4 * qt:
                    rr = kc - 4 * qt
                    depA = p.op("v", lambda e, attA=attA, rr=rr: e.tensor_mul(
                        attA[:, :], attA[:, :], S["cmask"][:, rr, :]), [sgA], "mkA")
                    depB = p.op("v", lambda e, attB=attB, rr=rr: e.tensor_mul(
                        attB[:, :], attB[:, :], S["cmask"][:, rr, :]), [sgB], "mkB")
                avA = p.op("t", lambda e, kc=kc, avbank=avbank, attA=attA, nkc=nkc: e.matmul(
                    S[avbank][0:64, :], S["Vtm"][:, kc, 0:64], attA[:, :],
                    start=(kc == 0), stop=(kc == nkc - 1),
                    tile_position=(0, 0), skip_group_check=True),
                    [depA, vtm_ev[kc], evacs[-1] if evacs and kc == 0 else None],
                    "avA")
                avB = p.op("t", lambda e, kc=kc, avbank=avbank, attB=attB, nkc=nkc: e.matmul(
                    S[avbank][64:128, :], S["Vtm"][:, kc, 64:128], attB[:, :],
                    start=(kc == 0), stop=(kc == nkc - 1),
                    tile_position=(0, 64), skip_group_check=True),
                    [depB, vtm_ev[kc], avA], "avB")
                prev_av[("A", qt, kc)] = avA
                prev_av[("B", qt, kc)] = avB
                last_avB = avB
            ev = p.op("s", lambda e, hp=hp, qt=qt, avbank=avbank: e.copy(
                S["s_bf"][:, hp, ts(qt, 512)], S[avbank][:, :]), [last_avB], "sev")
            evacs.append(ev)
        hp_fence = list(evacs) + [st["sqk"], st["sv"]]
    st["attn_done"] = evacs
    p.free("attB1", "attB0", "attA1", "attA0", "Vtm", "Vbf", "Xb", "Qsb")
    p.free("cmask")


def ph3(p, st):
    S = p.sb
    dr = p.dr
    seg = p.seg
    AGN = p.AGN
    fence = st["attn_done"]

    p.alloc("HMAX", [128, T], F32)
    p.alloc("XQ", [128, DC, T], BF16)
    xq_ops = []
    for dc in range(DC):
        xq_ops.append(quant3_row(p, S["XQ"][:, dc, :], S["SA"][:, dc, :], S["SXB"][:, :],
                                 -128.0, 127.0,
                                 [st["sxb"], st["xn_done"][0], xq_ops[-1] if xq_ops
                                  else None] + fence, S["f1"][:, :]))
    xn_sp_dmas = []
    for dc in range(DC):
        xn_sp_dmas.append(p.dma("q", dap(dr["xn_sp"], dc * 128 * T, [[T, 128], [1, T]]),
                                S["SA"][:, dc, :], deps=[st["xn_done"][0]] + fence,
                                note="dma_xnsp"))
    st["xn_sp"] = xn_sp_dmas

    spill_ops = []
    hmax_ops = []
    sil_prev = {}
    mm_last_of_hc = {}
    prev_ev_ops = None
    for hc in range(HC0):
        r, jc = hc // 4, hc % 4
        wb = wts_sel(S, hc)
        wl = p.dma("q", wb[:, :, :],
                   dap(dr["ag_out"], r * AGN + seg["w1T"] + jc * 128,
                       [[512, 128], [128 * 512, 8], [1, 128]]),
                   deps=[st["ag"], mm_last_of_hc.get(hc - 2)], note="dma_w1t")
        mms = []
        new_ev_ops = []
        for tq in range(NTQ):
            bank = "pA" if tq % 2 == 0 else "pB"
            for dc in range(DC):
                mdep = [wl, xq_ops[dc]]
                if dc == 0:
                    if tq >= 2:
                        mdep.append(new_ev_ops[tq - 2])
                    elif prev_ev_ops is not None:
                        mdep.append(prev_ev_ops[tq + 2])
                m = p.op("t", lambda e, bank=bank, dc=dc, tq=tq, wb=wb: e.matmul(
                    S[bank][:, :], wb[:, dc, :], S["XQ"][:, dc, ts(tq, 512)],
                    start=(dc == 0), stop=(dc == DC - 1)), mdep, "mm1")
                mms.append(m)
            ev = p.op("v", lambda e, bank=bank, tq=tq: e.tensor_mul(
                S["f1"][:, ts(tq, 512)], S[bank][:, :], S["RSXW1B"][:, ts(tq, 512)]),
                [mms[-1], st["rsxw1b"]] + ([sil_prev.get(hc - 1)] if tq == 0 else []),
                "ev1")
            new_ev_ops.append(ev)
        prev_ev_ops = new_ev_ops
        mm_last_of_hc[hc] = mms[-1]
        c1 = p.op("s", lambda e, hc=hc: e.activation(
            S["f2"][:, :], S["f1"][:, :], AF.Copy, scale=S["cw1"][:, hc:hc + 1]),
            new_ev_ops + [sil_prev.get(hc - 1)], "cv1")
        c2 = p.op("v", lambda e, hc=hc: e.scalar_tensor_tensor(
            S["f2"][:, 1:T], S["f1"][:, 0:T - 1], S["cw0"][:, hc:hc + 1],
            S["f2"][:, 1:T], ALU.mult, ALU.add), [c1] + new_ev_ops, "cv0")
        c3 = p.op("v", lambda e, hc=hc: e.scalar_tensor_tensor(
            S["f2"][:, 0:T - 1], S["f1"][:, 1:T], S["cw2"][:, hc:hc + 1],
            S["f2"][:, 0:T - 1], ALU.mult, ALU.add), [c2], "cv2")
        stage = S["g1s"] if hc % 2 == 0 else S["g2s"]
        sil = p.op("s", lambda e, hc=hc, stage=stage: e.activation(
            stage[:, :], S["f2"][:, :], AF.Silu, bias=S["cb"][:, hc:hc + 1]),
            [c3, spill_ops[-2] if hc >= 2 else None,
             hmax_ops[-2] if hc >= 2 else None], "sil")
        sil_prev[hc] = sil
        if hc == 0:
            hm = p.op("g", lambda e, stage=stage: e.tensor_copy(
                S["HMAX"][:, :], stage[:, :]), [sil], "hm0")
        else:
            hm = p.op("v", lambda e, stage=stage: e.tensor_max(
                S["HMAX"][:, :], S["HMAX"][:, :], stage[:, :]), [sil, hmax_ops[-1]], "hm")
        hmax_ops.append(hm)
        spill_ops.append(p.dma("q", dap(dr["hsil_sp"], hc * 128 * T, [[T, 128], [1, T]]),
                               stage[:, :], deps=[sil], note="dma_hsp"))
    p.free("XQ")

    prev_rd = None
    for i in range(16):
        bank = "pT" if i % 2 == 0 else "pT2"
        tp = p.op("t", lambda e, i=i, bank=bank: e.transpose(
            S[bank][:, :], S["HMAX"][:, ts(i, 128)], S["identF"][:, :]),
            [hmax_ops[-1], prev_rd], "h2tp")
        rd = p.op("v", lambda e, i=i, bank=bank: e.reduce_max(
            S["stat"][:, 32 + i:33 + i], S[bank][:, :], AX), [tp], "h2rd")
        prev_rd = rd
    p.free("HMAX")
    p.free("RSXW1B", "SXB", "SA")

    p.alloc("MEAN1B", [128, T], F32)
    p.alloc("RSTD1B", [128, T], F32)
    p.alloc("S2B", [128, T], BF16)
    p.alloc("RS2W2B", [128, T], F32)
    p.alloc("MEAN0B", [128, T], F32)
    p.alloc("STD0B", [128, T], F32)
    p.alloc("XQ2", [128, HC0, TH], BF16)
    p.alloc("xnt", [128, 512], F32)
    sx1 = p.op("v", lambda e: e.tensor_scalar(S["stat"][:, 48:64], S["stat"][:, 32:48],
                                              1e-5, None, ALU.max), [prev_rd], "s2c")
    sx2 = p.op("v", lambda e: e.reciprocal(S["tiny"][:, 0:16], S["stat"][:, 48:64]),
               [sx1], "s2r")
    sx3 = p.op("v", lambda e: e.tensor_scalar(S["tinyb"][:, 0:16], S["tiny"][:, 0:16],
                                              127.0, None, ALU.mult), [sx2], "s2127")
    sx3b = p.op("v", lambda e: e.tensor_copy(S["tiny"][:, 0:16], S["tinyb"][:, 0:16]),
                [sx3], "s2bk")
    sx4 = p.op("v", lambda e: e.reciprocal(S["tiny2"][:, 0:16], S["tiny"][:, 0:16]),
               [sx3b], "rs2")
    sx5 = p.op("v", lambda e: e.tensor_scalar(S["tiny2"][:, 0:16], S["tiny2"][:, 0:16],
                                              S["scal"][:, S_RW2:S_RW2 + 1], None,
                                              ALU.mult), [sx4, st["sw"]], "rs2w")
    s2b = row_bcast(p, st, S["tiny"][:, 0:16], S["S2B"], [sx3b, prev_rd], "s2b")
    rs2b = row_bcast(p, st, S["tiny2"][:, 0:16], S["RS2W2B"], [sx5, prev_rd], "rs2b")
    sd1 = p.op("v", lambda e: e.reciprocal(S["tiny"][:, 0:16], S["stat"][:, 16:32]),
               [s2b, rs2b], "std0")
    m0b = row_bcast(p, st, S["stat"][:, 0:16], S["MEAN0B"], [sd1], "m0b")
    st0b = row_bcast(p, st, S["tiny"][:, 0:16], S["STD0B"], [sd1], "st0b")

    x1_dmas = []
    ln1_var_ops = []
    banks = ["pA", "pB"]
    prev_qz = None
    for th in range(2):
        qz_ops = []
        for hc in range(HC0):
            stage = S["g1s"] if hc % 2 == 0 else S["g2s"]
            if hc >= 2:
                war = qz_ops[-2]
            elif th == 0:
                war = spill_ops[HC0 - 2 + hc]
            else:
                war = prev_qz[HC0 - 2 + hc]
            ld = p.dma("q", stage[:, 0:TH],
                       dap(dr["hsil_sp"], hc * 128 * T + th * TH, [[T, 128], [1, TH]]),
                       deps=[spill_ops[hc], s2b, war],
                       note="dma_hld")
            qz = quant3_row(p, S["XQ2"][:, hc, :], stage[:, 0:TH],
                            S["S2B"][:, ts(th, TH)], -128.0, 127.0, [ld],
                            S["f1"][:, (hc % 2) * TH:(hc % 2) * TH + TH],
                            chans=("v", "g", "g"))
            qz_ops.append(qz)
        wl_prev = {}
        for dc in range(DC):
            mms = {0: [], 1: []}
            wb = S["wt2"] if dc % 2 == 0 else S["wt3"]
            wl = p.dma("q", wb[:, :, :],
                       dap(dr["ag_out"], dc * AGN + seg["w2T"],
                           [[128, 128], [128 * 128, HC0], [1, 128]]),
                       deps=[st["ag"], qz_ops[-1], wl_prev.get(dc - 2)],
                       note="dma_w2t")
            wl_prev[dc] = None
            for hc in range(HC0):
                for tq2 in range(2):
                    m = p.op("t", lambda e, wb=wb, tq2=tq2, hc=hc: e.matmul(
                        S[banks[tq2]][:, :], wb[:, hc, :], S["XQ2"][:, hc, ts(tq2, 512)],
                        start=(hc == 0), stop=(hc == HC0 - 1)),
                        [wl] + ([x1_dmas[-1]] if hc == 0 and x1_dmas else []), "mm2")
                    mms[tq2].append(m)
            wl_prev[dc] = mms[1][-1]
            for tq2 in range(2):
                gt = th * 2 + tq2
                ev = p.op("v", lambda e, tq2=tq2, gt=gt: e.tensor_mul(
                    S["f1"][:, 0:512], S[banks[tq2]][:, :], S["RS2W2B"][:, ts(gt, 512)]),
                    [mms[tq2][-1], rs2b], "ev2")
                sil = p.op("s", lambda e: e.activation(
                    S["f2"][:, 0:512], S["f1"][:, 0:512], AF.Silu), [ev], "sil2")
                sg = p.op("v", lambda e, dc=dc, gt=gt: e.tensor_mul(
                    S["f2"][:, 0:512], S["f2"][:, 0:512], S["s_bf"][:, dc, ts(gt, 512)]),
                    [sil], "sg")
                xn_ld = p.dma("q", S["xnt"][:, :],
                              dap(dr["xn_sp"], dc * 128 * T + gt * 512,
                                  [[T, 128], [1, 512]]),
                              deps=[xn_sp_dmas[dc], m0b, st0b,
                                    x1_dmas[-1] if x1_dmas else None], note="dma_xnld")
                xr = p.op("v", lambda e, gt=gt: e.tensor_mul(
                    S["f3"][:, 0:512], S["xnt"][:, :], S["STD0B"][:, ts(gt, 512)]),
                    [xn_ld], "xstd")
                xr2 = p.op("v", lambda e, gt=gt: e.tensor_add(
                    S["f3"][:, 0:512], S["f3"][:, 0:512], S["MEAN0B"][:, ts(gt, 512)]),
                    [xr], "xmean")
                x1t = p.op("v", lambda e: e.tensor_add(
                    S["f3"][:, 0:512], S["f3"][:, 0:512], S["f2"][:, 0:512]),
                    [xr2, sg], "x1t")
                x1_dmas.append(p.dma("q", dap(dr["x1_sp"], dc * 128 * T + gt * 512,
                                              [[T, 128], [1, 512]]),
                                     S["f3"][:, 0:512], deps=[x1t], note="dma_x1sp"))
        prev_qz = qz_ops
    st["x1_dmas"] = x1_dmas
    st["x1_idx"] = lambda dc, gt: (gt // 2) * 16 + dc * 2 + (gt % 2)
    p.free("xnt", "XQ2", "STD0B", "MEAN0B", "RS2W2B", "S2B")
    p.free("s_bf")


def ph4(p, st):
    S = p.sb
    dr = p.dr
    seg = p.seg
    AGN = p.AGN
    fence = st["x1_dmas"][-2:]

    p.alloc("xnt4", [128, 512], F32)
    ln1_var_ops = []
    sq_hist = {}
    m1_hist = {}
    m2_hist = {}
    mS1 = mS2 = None
    for gt in range(NTQ):
        for dc in range(DC):
            i = gt * DC + dc
            xb = S["xnt4"][:, :] if i % 2 == 0 else S["f3"][:, 0:512]
            sqb = S["f2"][:, 0:512] if i % 2 == 0 else S["f2"][:, 512:1024]
            ld = p.dma("q", xb,
                       dap(dr["x1_sp"], dc * 128 * T + gt * 512, [[T, 128], [1, 512]]),
                       deps=[st["x1_dmas"][st["x1_idx"](dc, gt)],
                             sq_hist.get(i - 2), m1_hist.get(i - 2)] +
                            (fence if i == 0 else []), note="dma_x1st")
            sqz = p.op("v", lambda e, xb=xb, sqb=sqb: e.tensor_mul(
                sqb, xb, xb), [ld, m2_hist.get(i - 2)], "sq1")
            mS1 = p.op("t", lambda e, dc=dc, xb=xb: e.matmul(
                S["pC"][:, :], S["ones"][:, :], xb,
                start=(dc == 0), stop=(dc == DC - 1)),
                [ld, ln1_var_ops[-1] if dc == 0 and ln1_var_ops else None], "ms1")
            mS2 = p.op("t", lambda e, dc=dc, sqb=sqb: e.matmul(
                S["pD"][:, :], S["ones"][:, :], sqb,
                start=(dc == 0), stop=(dc == DC - 1)), [sqz], "ms2")
            sq_hist[i] = sqz
            m1_hist[i] = mS1
            m2_hist[i] = mS2
        e1 = p.op("v", lambda e, gt=gt: e.tensor_scalar(
            S["MEAN1B"][:, ts(gt, 512)], S["pC"][:, :], 1.0 / D, None,
            ALU.mult), [mS1], "ln1m")
        e2 = p.op("v", lambda e: e.tensor_scalar(
            S["f1"][:, 0:512], S["pD"][:, :], 1.0 / D, None, ALU.mult),
            [mS2], "ln1e2")
        e3 = p.op("v", lambda e, gt=gt: e.tensor_mul(
            S["f2"][:, 0:512], S["MEAN1B"][:, ts(gt, 512)],
            S["MEAN1B"][:, ts(gt, 512)]), [e1, e2], "ln1m2")
        e4 = p.op("v", lambda e: e.tensor_sub(
            S["f1"][:, 0:512], S["f1"][:, 0:512], S["f2"][:, 0:512]), [e3], "ln1v")
        e5 = p.op("s", lambda e: e.activation(
            S["f1"][:, 0:512], S["f1"][:, 0:512], AF.Sqrt,
            bias=S["epsc"][:, 0:1]), [e4], "ln1sd")
        e6 = p.op("v", lambda e, gt=gt: e.reciprocal(
            S["RSTD1B"][:, ts(gt, 512)], S["f1"][:, 0:512]), [e5], "ln1rs")
        ln1_var_ops.append(e6)
        prev = e6
    st["ln1"] = ln1_var_ops

    mx_cols = []
    xn1_dmas = []
    n1_hist = {}
    d_prev = st["ln1"][-1]
    for dc in range(DC):
        for gt in range(NTQ):
            xb = S["xnt4"][:, :] if gt % 2 == 0 else S["f2"][:, 0:512]
            ld = p.dma("q", xb,
                       dap(dr["x1_sp"], dc * 128 * T + gt * 512, [[T, 128], [1, 512]]),
                       deps=[st["x1_dmas"][st["x1_idx"](dc, gt)],
                             n1_hist.get((dc, gt - 2)),
                             d_prev if gt < 2 else None], note="dma_x1ld")
            n1 = p.op("v", lambda e, gt=gt, xb=xb: e.tensor_sub(
                S["f1"][:, ts(gt, 512)], xb, S["MEAN1B"][:, ts(gt, 512)]),
                [ld] + st["ln1"], "n1")
            n1_hist[(dc, gt)] = n1
            prev = p.op("v", lambda e, gt=gt: e.tensor_mul(
                S["f1"][:, ts(gt, 512)], S["f1"][:, ts(gt, 512)],
                S["RSTD1B"][:, ts(gt, 512)]), [n1], "n2")
        mx_cols.append(p.op("v", lambda e, dc=dc: e.reduce_max(
            S["mxc"][:, dc:dc + 1], S["f1"][:, :], AX, apply_absolute_value=True),
            [prev], "m0mx"))
        d = p.dma("q", dap(dr["xn1_sp"], dc * 128 * T, [[T, 128], [1, T]]),
                  S["f1"][:, :], deps=[prev], note="dma_xn1")
        xn1_dmas.append(d)
        d_prev = d
    rs0 = global_max_ar(p, st, S["mxc"][:, 0:8], 2, S_S0, S_RS0, "m0", mx_cols)
    p.free("xnt4")
    p.free("RSTD1B", "MEAN1B")

    p.alloc("KT", [128, DC, T], BF16, side="right")
    p.alloc("VT", [128, DC, T], BF16, side="right")
    p.alloc("SRT", [128, DC, T], BF16, side="right")
    p.alloc("MIX", [128, DC, T], BF16, side="right")
    banks = ["pA", "pB", "pC", "pD"]
    st["proj_done"] = {}
    prev_q = None
    for mixn, wseg, dstn, sig in [("tmk", "kT", "KT", False), ("tmv", "vT", "VT", False),
                                  ("tmr", "rT", "SRT", True)]:
        mix_ops = []
        o1_hist = {}
        for dc in range(DC):
            xb = S["f1"] if dc % 2 == 0 else S["f3"]
            ld = p.dma("q", xb[:, :],
                       dap(dr["xn1_sp"], dc * 128 * T, [[T, 128], [1, T]]),
                       deps=[xn1_dmas[dc], xn1_dmas[-1] if dc == 0 else None,
                             mx_cols[-1] if dc < 2 else None,
                             o1_hist.get(dc - 2), prev_q if dc < 2 else None],
                       note="dma_xm")
            dsub = p.op("v", lambda e, xb=xb: e.tensor_sub(
                S["f2"][:, 1:T], xb[:, 1:T], xb[:, 0:T - 1]), [ld], "dmx")
            dcol = p.op("v", lambda e, xb=xb: e.tensor_copy(
                S["f2"][:, 0:1], xb[:, 0:1]), [ld], "dc0")
            o1 = p.op("v", lambda e, mixn=mixn, dc=dc, xb=xb: e.scalar_tensor_tensor(
                S["MIX"][:, dc, 1:T], S["f2"][:, 1:T], S[mixn][:, dc:dc + 1],
                xb[:, 0:T - 1], ALU.mult, ALU.add), [dsub, dcol], "mx1")
            o2 = p.op("v", lambda e, mixn=mixn, dc=dc: e.tensor_scalar(
                S["MIX"][:, dc, 0:1], S["f2"][:, 0:1], S[mixn][:, dc:dc + 1], None,
                ALU.mult), [dcol], "mx0")
            mix_ops += [o1, o2]
            o1_hist[dc] = o1
        ev_prev = {}
        evl = []
        last_mm_cc = {}
        for cc in range(DC):
            mms = {tq: None for tq in range(NTQ)}
            wb = wts_sel(S, cc)
            wl = p.dma("q", wb[:, :, :],
                       dap(dr["ag_out"], cc * AGN + seg[wseg],
                           [[128, 128], [128 * 128, DC], [1, 128]]),
                       deps=[st["ag"], last_mm_cc.get(cc - 2)] +
                            (mix_ops if cc == 0 else []), note="dma_wp")
            for dc in range(DC):
                for tq in range(NTQ):
                    m = p.op("t", lambda e, wb=wb, dc=dc, tq=tq: e.matmul(
                        S[banks[tq]][:, :], wb[:, dc, :], S["MIX"][:, dc, ts(tq, 512)],
                        start=(dc == 0), stop=(dc == DC - 1)),
                        [wl, ev_prev.get((cc - 1, tq))], "mmp")
                    mms[tq] = m
            last_mm_cc[cc] = mms[NTQ - 1]
            for tq in range(NTQ):
                if sig:
                    ev = p.op("s", lambda e, cc=cc, tq=tq, dstn=dstn: e.activation(
                        S[dstn][:, cc, ts(tq, 512)], S[banks[tq]][:, :], AF.Sigmoid),
                        [mms[tq]], "evsig")
                else:
                    ev = p.op("s", lambda e, cc=cc, tq=tq, dstn=dstn: e.copy(
                        S[dstn][:, cc, ts(tq, 512)], S[banks[tq]][:, :]), [mms[tq]], "evp")
                ev_prev[(cc, tq)] = ev
                evl.append(ev)
        st["proj_done"][dstn] = evl
        prev_q = evl[-1]
    p.free("MIX")

    p.alloc("XQ1", [128, DC, T], BF16, side="right")
    xq1_ops = []
    proj_fence = (st["proj_done"]["KT"][-1:] + st["proj_done"]["VT"][-1:]
                  + st["proj_done"]["SRT"][-1:])
    prev = None
    for dc in range(DC):
        ld = p.dma("q", S["f1"][:, :], dap(dr["xn1_sp"], dc * 128 * T, [[T, 128], [1, T]]),
                   deps=[xn1_dmas[dc], prev] + (proj_fence if dc == 0 else []),
                   note="dma_xq1")
        q = quant3_col(p, S["XQ1"][:, dc, :], S["f1"][:, :],
                       S["scal"][:, S_RS0:S_RS0 + 1], -32768.0, 32767.0,
                       [ld, rs0], S["f2"][:, :])
        xq1_ops.append(q)
        prev = q
    st["xq1"] = xq1_ops


def ph5(p, st):
    S = p.sb
    dr = p.dr
    seg = p.seg
    AGN = p.AGN

    TQS = T // 4
    p.alloc("f4", [128, T], F32)
    p.alloc("PP", [128, TQS + 1], F32)
    p.alloc("AAs", [128, TQS + 1], F32)
    p.alloc("BBs", [128, TQS + 1], F32)
    p.alloc("X4b", [128, TQS], F32)
    X1b = S["f1"][:, 0:TQS]
    X2b = S["f2"][:, 0:TQS]
    X3b = S["f3"][:, 0:TQS]

    upgate_spills = {"up": [], "gate": []}
    mx_ops = {"up": [], "gate": []}
    banksUG = {"up": ["pA", "pB"], "gate": ["pC", "pD"]}
    stages = {"up": "f4", "gate": "f4"}
    stages2 = {"up": "f4", "gate": "f4"}
    last_mm = {"up": None, "gate": None}
    last_mx = None
    for Ic in range(IC1):
        r, jc = Ic // 3, Ic % 3
        for nm, wsg, spnm in [("up", "upT", "up_sp"), ("gate", "gateT", "gate_sp")]:
            wb = wts_sel(S, 2 * Ic + (0 if nm == "up" else 1))
            wl = p.dma("q", wb[:, :, :],
                       dap(dr["ag_out"], r * AGN + seg[wsg] + jc * 128,
                           [[384, 128], [128 * 384, 8], [1, 128]]),
                       deps=[st["ag"], last_mm[nm]] + (st["xq1"] if Ic == 0 else []),
                       note="dma_wug")
            stage = S[stages[nm]] if Ic % 2 == 0 else S[stages2[nm]]
            evs = []
            mprev = None
            for tq in range(NTQ):
                bank = banksUG[nm][tq % 2]
                for dc in range(DC):
                    m = p.op("t", lambda e, wb=wb, bank=bank, dc=dc, tq=tq: e.matmul(
                        S[bank][:, :], wb[:, dc, :], S["XQ1"][:, dc, ts(tq, 512)],
                        start=(dc == 0), stop=(dc == DC - 1)),
                        [wl, evs[tq - 2] if tq >= 2 and dc == 0 else None], "mmug")
                    mprev = m
                lastsp = (upgate_spills["gate"] if nm == "up"
                          else upgate_spills["up"])
                ev = p.op("s", lambda e, bank=bank, tq=tq, stage=stage: e.activation(
                    stage[:, ts(tq, 512)], S[bank][:, :], AF.Copy,
                    scale=S["scal"][:, S_S0:S_S0 + 1]),
                    [mprev, st["xq1"][-1],
                     upgate_spills[nm][-1] if upgate_spills[nm] and tq == 0 else None,
                     lastsp[-1] if lastsp and tq == 0 else None,
                     last_mx if tq == 0 else None],
                    "evug")
                evs.append(ev)
            last_mm[nm] = mprev
            last_mx = p.op("v", lambda e, stage=stage, nm=nm, Ic=Ic: e.reduce_max(
                S["mxc"][:, (8 if nm == "up" else 32) + Ic:
                         (9 if nm == "up" else 33) + Ic],
                stage[:, :], AX, apply_absolute_value=True), evs, "mxug")
            mx_ops[nm].append(last_mx)
            upgate_spills[nm].append(p.dma("q", dap(dr[spnm], Ic * 128 * T,
                                                    [[T, 128], [1, T]]),
                                           stage[:, :], deps=evs, note="dma_ugsp"))
    st["upg_spills"] = upgate_spills

    c1 = p.op("v", lambda e: e.reduce_max(S["mxc"][:, 56:57], S["mxc"][:, 8:32], AX),
              mx_ops["up"] + [st["ar_last"]], "m1cl")
    c2 = p.op("v", lambda e: e.reduce_max(S["mxc"][:, 57:58], S["mxc"][:, 32:56], AX),
              mx_ops["gate"], "m2cl")
    tp = p.op("t", lambda e: e.transpose(S["pT2"][0:2, 0:128], S["mxc"][:, 56:58],
                                         S["identF"][:, :]), [c1, c2], "m12tp")
    rd1 = p.op("v", lambda e: e.reduce_max(S["mxc"][0:2, 58:59], S["pT2"][0:2, 0:128], AX),
               [tp], "m12rd")
    dcp = p.dma("gq", dr["ar_in"].ap()[3:5], dap(S["mxc"], 58, [[64, 2], [1, 1]]),
                deps=[rd1], note="dma_m12")
    ar12 = p.op("c", lambda e: e.collective_compute(
        "AllReduce", ALU.max, replica_groups=[list(range(NCORES))],
        ins=[dr["ar_in"].ap()[3:5].opt()], outs=[dr["ar_out"].ap()[3:5].opt()]),
        [dcp, st["ar_last"]], "AR_m12")
    st["ar_last"] = ar12
    lb = p.dma("gq", S["mxc"][0:1, 58:60], dr["ar_out"].ap()[3:5], deps=[ar12, dcp],
               note="dma_m12b")
    bc = bcast_p0(p, S["mxc"][0:1, 58:60], S["mxc"][:, 60:62], 2, [lb], "m12")
    q1 = p.op("v", lambda e: e.tensor_scalar(S["scal"][:, S_S1:S_S1 + 1],
                                             S["mxc"][:, 60:61], 1.0 / 32767.0, 1e-12,
                                             ALU.mult, ALU.max), [bc], "s1s")
    q2 = p.op("v", lambda e: e.reciprocal(S["scal"][:, S_RS1:S_RS1 + 1],
                                          S["scal"][:, S_S1:S_S1 + 1]), [q1], "rs1")
    q3 = p.op("v", lambda e: e.tensor_scalar(S["scal"][:, S_S2:S_S2 + 1],
                                             S["mxc"][:, 61:62], 1.0 / 32767.0, 1e-12,
                                             ALU.mult, ALU.max), [bc], "s2s")
    q4 = p.op("v", lambda e: e.reciprocal(S["scal"][:, S_RS2:S_RS2 + 1],
                                          S["scal"][:, S_S2:S_S2 + 1]), [q3], "rs2s")
    q5 = p.op("v", lambda e: e.tensor_scalar(S["scal"][:, S_S265536:S_S265536 + 1],
                                             S["scal"][:, S_S2:S_S2 + 1], 1.0 / 65536.0,
                                             None, ALU.mult), [q3], "s265")
    st["scales12"] = [q2, q4, q5]

    rwkv_done = []
    rwkv_pair = {}
    prev_half = None
    for cc in range(DC):
        for h in range(4):
            off = h * TQS
            kap = S["KT"][:, cc, off:off + TQS]
            vap = S["VT"][:, cc, off:off + TQS]
            deps0 = [st["proj_done"]["KT"][cc * NTQ + h],
                     st["proj_done"]["VT"][cc * NTQ + h],
                     st["proj_done"]["SRT"][-1],
                     st["wc"], st["nu"], prev_half]
            if h == 0:
                i1 = p.op("v", lambda e: e.memset(S["PP"][:, 0:1], -1e38), deps0, "ppi")
                i2 = p.op("v", lambda e: e.memset(S["AAs"][:, 0:1], 0.0), deps0, "aai")
                i3 = p.op("v", lambda e: e.memset(S["BBs"][:, 0:1], 0.0), deps0, "bbi")
            else:
                i1 = p.op("v", lambda e: e.tensor_copy(S["PP"][:, 0:1],
                                                       S["PP"][:, TQS:TQS + 1]), deps0, "ppc")
                i2 = p.op("v", lambda e: e.tensor_copy(S["AAs"][:, 0:1],
                                                       S["AAs"][:, TQS:TQS + 1]), deps0, "aac")
                i3 = p.op("v", lambda e: e.tensor_copy(S["BBs"][:, 0:1],
                                                       S["BBs"][:, TQS:TQS + 1]), deps0, "bbc")
            wbc = p.op("v", lambda e, cc=cc: e.tensor_copy(
                S["X4b"][:, :], dap(S["w_c"], cc, [[8, 128], [0, TQS]])), deps0, "wbc")
            scn = p.op("v", lambda e, kap=kap: e.tensor_tensor_scan(
                S["PP"][:, 1:], S["X4b"][:, :], kap, S["PP"][:, 0:1], ALU.add, ALU.max),
                [i1, i2, i3, wbc], "ppscan")
            t1 = p.op("v", lambda e, kap=kap: e.tensor_sub(X1b, kap, S["PP"][:, 1:]),
                      [scn], "t1")
            t2 = p.op("v", lambda e: e.tensor_sub(X2b, S["PP"][:, 0:TQS], S["PP"][:, 1:]),
                      [scn], "t2")
            e2 = p.op("s", lambda e: e.activation(X3b, X1b, AF.Exp), [t1], "e2")
            e1 = p.op("s", lambda e, cc=cc: e.activation(
                X2b, X2b, AF.Exp, bias=S["w_c"][:, cc:cc + 1]), [t2], "e1")
            za = p.op("v", lambda e, vap=vap: e.tensor_mul(S["X4b"][:, :], X3b, vap),
                      [e2, e1, scn], "za")
            sa = p.op("v", lambda e: e.tensor_tensor_scan(
                S["AAs"][:, 1:], X2b, S["X4b"][:, :], S["AAs"][:, 0:1],
                ALU.mult, ALU.add), [za], "aascan")
            sb_ = p.op("v", lambda e: e.tensor_tensor_scan(
                S["BBs"][:, 1:], X2b, X3b, S["BBs"][:, 0:1], ALU.mult, ALU.add),
                [sa], "bbscan")
            ra = p.op("v", lambda e, kap=kap: e.tensor_sub(X1b, S["PP"][:, 0:TQS], kap),
                      [sb_, t1], "ra")
            rb = p.op("v", lambda e, cc=cc: e.tensor_scalar(
                X1b, X1b, S["nu_c"][:, cc:cc + 1], CAP, ALU.add, ALU.min), [ra], "rb")
            rr = p.op("s", lambda e: e.activation(X1b, X1b, AF.Exp), [rb], "rr")
            nm1 = p.op("v", lambda e: e.tensor_mul(X2b, X1b, S["AAs"][:, 0:TQS]),
                       [rr], "nm1")
            nm2 = p.op("v", lambda e, vap=vap: e.tensor_add(X2b, X2b, vap), [nm1], "nm2")
            dn1 = p.op("v", lambda e: e.tensor_mul(X3b, X1b, S["BBs"][:, 0:TQS]),
                       [nm1], "dn1")
            dn2 = p.op("v", lambda e: e.tensor_scalar(X3b, X3b, 1.0, None, ALU.add),
                       [dn1], "dn2")
            rcp = p.op("v", lambda e: e.reciprocal(X3b, X3b), [dn2], "rcp")
            yy = p.op("v", lambda e: e.tensor_mul(X2b, X2b, X3b), [rcp, nm2], "yy")
            rw = p.op("v", lambda e, cc=cc, off=off: e.tensor_mul(
                S["KT"][:, cc, off:off + TQS], X2b, S["SRT"][:, cc, off:off + TQS]),
                [yy], "rwkv")
            rwkv_done.append(rw)
            rwkv_pair.setdefault(cc, []).append(rw)
            prev_half = rw
    st["rwkv"] = rwkv_done

    s1_dmas = []
    ev_prev = {}
    banks = ["pA", "pB", "pC", "pD"]
    last_mm_dcg = {}
    for dcg in range(DC):
        mms = {tq: None for tq in range(NTQ)}
        wb = wts_sel(S, dcg)
        wl = p.dma("q", wb[:, :, :],
                   dap(dr["ag_out"], dcg * AGN + seg["oT"],
                       [[128, 128], [128 * 128, DC], [1, 128]]),
                   deps=[st["ag"], last_mm_dcg.get(dcg - 2)] +
                        ([last_mm["up"], last_mm["gate"]] if dcg == 0 else []),
                   note="dma_wo")
        for cc in range(DC):
            for tq in range(NTQ):
                m = p.op("t", lambda e, wb=wb, cc=cc, tq=tq: e.matmul(
                    S[banks[tq]][:, :], wb[:, cc, :], S["KT"][:, cc, ts(tq, 512)],
                    start=(cc == 0), stop=(cc == DC - 1)),
                    [wl, ev_prev.get((dcg - 1, tq)), rwkv_pair[cc][-1]], "mmo")
                mms[tq] = m
        last_mm_dcg[dcg] = mms[NTQ - 1]
        for tq in range(NTQ):
            stage = S["f4"]
            ev = p.op("s", lambda e, tq=tq, stage=stage: e.copy(
                stage[:, ts(tq, 512)], S[banks[tq]][:, :]),
                [mms[tq], s1_dmas[-4] if len(s1_dmas) >= 4 else None,
                 last_mm["up"] if dcg == 0 and tq == 0 else None,
                 last_mm["gate"] if dcg == 0 and tq == 0 else None,
                 upgate_spills["up"][-1] if dcg == 0 and tq == 0 else None,
                 upgate_spills["gate"][-1] if dcg == 0 and tq == 0 else None], "evo")
            ev_prev[(dcg, tq)] = ev
            s1_dmas.append(p.dma("q", dap(dr["s1_sp"], dcg * 128 * T + tq * 512,
                                          [[T, 128], [1, 512]]),
                                 stage[:, ts(tq, 512)], deps=[ev], note="dma_s1sp"))
    st["s1_dmas"] = s1_dmas
    p.free("X4b", "BBs", "AAs", "PP")
    p.free("XQ1", "SRT", "VT", "KT")
    p.alloc("f5", [128, T], F32)


def ph6(p, st):
    S = p.sb
    dr = p.dr
    seg = p.seg
    AGN = p.AGN

    g2_spills = []
    mx3 = []
    gq_l = []
    g2_l = []
    for Ic in range(IC1):
        stage = S["f4"] if Ic % 2 == 0 else S["f5"]
        ld = p.dma("q", stage[:, :], dap(dr["gate_sp"], Ic * 128 * T, [[T, 128], [1, T]]),
                   deps=[st["upg_spills"]["gate"][Ic],
                         gq_l[-2] if len(gq_l) >= 2 else None] + st["scales12"] +
                        (st["s1_dmas"][-4:] if Ic < 2 else []), note="dma_gld")
        gq = quant3_col(p, S["f1"][:, :], stage[:, :], S["scal"][:, S_RS2:S_RS2 + 1],
                        -32768.0, 32767.0, [ld, g2_l[-1] if g2_l else None],
                        S["f1"][:, :], chan3="g")
        gq_l.append(gq)
        sg = p.op("s", lambda e: e.activation(S["f2"][:, :], S["f1"][:, :], AF.Sigmoid,
                                              scale=S["scal"][:, S_S2:S_S2 + 1]),
                  [gq], "sigg")
        u1 = p.op("s", lambda e: e.activation(S["f2"][:, :], S["f2"][:, :], AF.Copy,
                                              scale=65536.0, bias=RND), [sg], "u16a")
        u1b = p.op("s", lambda e: e.activation(S["f2"][:, :], S["f2"][:, :], AF.Copy,
                                               bias=-RND), [u1], "u16ab")
        u2 = p.op("v", lambda e: e.tensor_scalar(S["f2"][:, :], S["f2"][:, :], 65535.0,
                                                 None, ALU.min), [u1b], "u16b")
        g2 = p.op("v", lambda e: e.scalar_tensor_tensor(
            S["f3"][:, :], S["f1"][:, :], S["scal"][:, S_S265536:S_S265536 + 1],
            S["f2"][:, :], ALU.mult, ALU.mult),
            [u2, gq, g2_spills[-1] if g2_spills else None], "g2")
        g2_l.append(g2)
        mx3.append(p.op("v", lambda e, Ic=Ic: e.reduce_max(
            S["mxc"][:, 8 + Ic:9 + Ic], S["f3"][:, :], AX,
            apply_absolute_value=True), [g2], "mx3"))
        sp = p.dma("q", dap(dr["gate2_sp"], Ic * 128 * T, [[T, 128], [1, T]]),
                   S["f3"][:, :], deps=[g2], note="dma_g2sp")
        g2_spills.append(sp)
    ar3 = global_max_ar(p, st, S["mxc"][:, 8:32], 5, S_S3, S_RS3, "m3", mx3)
    ss = p.op("v", lambda e: e.tensor_mul(S["scal"][:, S_S3S1:S_S3S1 + 1],
                                          S["scal"][:, S_S3:S_S3 + 1],
                                          S["scal"][:, S_S1:S_S1 + 1]),
              [ar3] + st["scales12"], "s3s1")

    o_spills = []
    mx4 = []
    q1_l = []
    q2_l = []
    oo_l = []
    for Ic in range(IC1):
        st1 = S["f4"]
        st2 = S["f5"]
        ld1 = p.dma("q", st1[:, :], dap(dr["gate2_sp"], Ic * 128 * T, [[T, 128], [1, T]]),
                    deps=[g2_spills[Ic], ss, q1_l[-1] if q1_l else None], note="dma_g2ld")
        ld2 = p.dma("q", st2[:, :], dap(dr["up_sp"], Ic * 128 * T, [[T, 128], [1, T]]),
                    deps=[st["upg_spills"]["up"][Ic], ss, q2_l[-1] if q2_l else None],
                    note="dma_upld")
        q1 = quant3_col(p, S["f1"][:, :], st1[:, :], S["scal"][:, S_RS3:S_RS3 + 1],
                        -32768.0, 32767.0, [ld1, oo_l[-1] if oo_l else None],
                        S["f1"][:, :])
        q2 = quant3_col(p, S["f2"][:, :], st2[:, :], S["scal"][:, S_RS1:S_RS1 + 1],
                        -32768.0, 32767.0, [ld2, oo_l[-1] if oo_l else None],
                        S["f2"][:, :])
        q1_l.append(q1)
        q2_l.append(q2)
        oo = p.op("v", lambda e: e.scalar_tensor_tensor(
            S["f3"][:, :], S["f1"][:, :], S["scal"][:, S_S3S1:S_S3S1 + 1],
            S["f2"][:, :], ALU.mult, ALU.mult),
            [q1, q2, o_spills[-1] if o_spills else g2_spills[-1]], "oprod")
        oo_l.append(oo)
        mx4.append(p.op("v", lambda e, Ic=Ic: e.reduce_max(
            S["mxc"][:, 8 + Ic:9 + Ic], S["f3"][:, :], AX,
            apply_absolute_value=True), [oo], "mx4"))
        sp = p.dma("q", dap(dr["o_sp"], Ic * 128 * T, [[T, 128], [1, T]]),
                   S["f3"][:, :], deps=[oo], note="dma_osp")
        o_spills.append(sp)
    ar4 = global_max_ar(p, st, S["mxc"][:, 8:32], 6, S_S4, S_RS4, "m4", mx4)

    p.alloc("OQ", [128, IC1, TH], BF16)
    out_dmas = []
    last_mmd = None
    last_x2 = None
    last_sgd = None
    last_tpo = None
    dn_last = {0: None, 1: None}
    banks = ["pA", "pB"]
    for th in range(2):
        oq_ops = []
        for Ic in range(IC1):
            slot = Ic % 4
            sbuf4 = S["f4"] if slot in (0, 1) else S["f5"]
            stg = sbuf4[:, (slot % 2) * TH:(slot % 2) * TH + TH]
            xdeps = [o_spills[Ic], ar4,
                     oq_ops[-4] if len(oq_ops) >= 4 else None]
            if Ic == 0:
                xdeps += st["s1_dmas"][-1:]
            if Ic < 4 and th == 1:
                xdeps.append(last_mmd)
            ld = p.dma("q", stg,
                       dap(dr["o_sp"], Ic * 128 * T + th * TH, [[T, 128], [1, TH]]),
                       deps=xdeps, note="dma_old")
            q = quant3_col(p, S["OQ"][:, Ic, :], stg,
                           S["scal"][:, S_RS4:S_RS4 + 1], -32768.0, 32767.0,
                           [ld, oq_ops[-2] if len(oq_ops) >= 2 else last_x2,
                            last_x2 if Ic < 2 else None],
                           S["f1"][:, (Ic % 2) * TH:(Ic % 2) * TH + TH])
            oq_ops.append(q)
        for dc in range(DC):
            mms = {0: [], 1: []}
            b = dc % 2
            wb = S["wt2"] if b == 0 else S["wt3"]
            wl = p.dma("q", wb[:, 0:IC1, :],
                       dap(dr["ag_out"], dc * AGN + seg["downT"],
                           [[128, 128], [128 * 128, IC1], [1, 128]]),
                       deps=[st["ag"], dn_last[b] if dn_last[b] is not None
                             else st["x1_dmas"][-1]], note="dma_wd")
            for Ic in range(IC1):
                for tq2 in range(2):
                    m = p.op("t", lambda e, wb=wb, tq2=tq2, Ic=Ic: e.matmul(
                        S[banks[tq2]][:, :], wb[:, Ic, :], S["OQ"][:, Ic, ts(tq2, 512)],
                        start=(Ic == 0), stop=(Ic == IC1 - 1)),
                        [wl, oq_ops[Ic]] + ([out_dmas[-1]] if Ic == 0 and out_dmas
                                            else []), "mmd")
                    mms[tq2].append(m)
                    last_mmd = m
            dn_last[b] = last_mmd
            for tq2 in range(2):
                gt = th * 2 + tq2
                sil = p.op("s", lambda e, tq2=tq2: e.activation(
                    S["f1"][:, 0:512], S[banks[tq2]][:, :], AF.Silu,
                    scale=S["scal"][:, S_S4:S_S4 + 1]),
                    [mms[tq2][-1], last_x2], "sild")
                s1ld = p.dma("q", S["f3"][:, 0:512],
                             dap(dr["s1_sp"], dc * 128 * T + gt * 512,
                                 [[T, 128], [1, 512]]),
                             deps=[st["s1_dmas"][dc * NTQ + gt],
                                   out_dmas[-4] if len(out_dmas) >= 4 else None],
                             note="dma_s1ld")
                sgd = p.op("v", lambda e: e.tensor_mul(
                    S["f1"][:, 0:512], S["f1"][:, 0:512], S["f3"][:, 0:512]),
                    [sil, s1ld], "sgd")
                last_sgd = sgd
                x1ld = p.dma("q", S["f2"][:, 0:512],
                             dap(dr["x1_sp"], dc * 128 * T + gt * 512,
                                 [[T, 128], [1, 512]]),
                             deps=[st["x1_dmas"][st["x1_idx"](dc, gt)], sgd, last_tpo],
                             note="dma_x1f")
                x2 = p.op("v", lambda e: e.tensor_add(S["f2"][:, 0:512], S["f2"][:, 0:512],
                                                      S["f1"][:, 0:512]), [x1ld, sgd], "x2")
                last_x2 = x2
                for j in range(4):
                    bank = "pT" if j % 2 == 0 else "pT2"
                    tp = p.op("t", lambda e, j=j, bank=bank: e.transpose(
                        S[bank][:, :], S["f2"][:, ts(j, 128)], S["identF"][:, :]),
                        [x2, out_dmas[-2] if len(out_dmas) >= 2 else None], "tpo")
                    last_tpo = tp
                    cp = p.op("v", lambda e, j=j, bank=bank: e.tensor_copy(
                        S["f3"][:, 512 + j * 128:512 + (j + 1) * 128], S[bank][:, :]),
                        [tp, out_dmas[-4] if len(out_dmas) >= 4 else None], "cpo")
                    out_dmas.append(p.dma("q",
                                          dr["out"].ap()[gt * 512 + j * 128:
                                                         gt * 512 + (j + 1) * 128,
                                                         ts(dc, 128)],
                                          S["f3"][:, 512 + j * 128:512 + (j + 1) * 128],
                                          deps=[cp], note="dma_out"))
    st["out_dmas"] = out_dmas
    p.free("OQ")


def host_inputs(inputs, core):
    def cols(a):
        return np.ascontiguousarray(np.asarray(a, dtype=np.float32).reshape(8, 128).T)
    cw = np.asarray(inputs["L0_ffn_conv_w"], dtype=np.float32)[:, 0, :]
    cwp = cw.reshape(32, 128, 3).transpose(1, 0, 2)
    def sl(a):
        a = np.asarray(a, dtype=np.float32)
        n = a.shape[0] // NCORES
        return np.ascontiguousarray(a[core * n:(core + 1) * n])
    return {
        "x": np.ascontiguousarray(np.asarray(inputs["L0_attn_x"], dtype=np.float32)[core]),
        "bvq_c": cols(np.asarray(inputs["L0_attn_bv_q"]).reshape(-1)),
        "bvk_c": cols(np.asarray(inputs["L0_attn_bv_k"]).reshape(-1)),
        "bvv_c": cols(np.asarray(inputs["L0_attn_bv_v"]).reshape(-1)),
        "cw0_c": np.ascontiguousarray(cwp[:, :, 0]),
        "cw1_c": np.ascontiguousarray(cwp[:, :, 1]),
        "cw2_c": np.ascontiguousarray(cwp[:, :, 2]),
        "cb_c": np.ascontiguousarray(
            np.asarray(inputs["L0_ffn_conv_b"], dtype=np.float32).reshape(32, 128).T),
        "td_c": cols(inputs["L1_attn_time_decay"]),
        "tf_c": cols(inputs["L1_attn_time_first"]),
        "tmk_c": cols(np.asarray(inputs["L1_attn_time_mix_k"]).reshape(-1)),
        "tmv_c": cols(np.asarray(inputs["L1_attn_time_mix_v"]).reshape(-1)),
        "tmr_c": cols(np.asarray(inputs["L1_attn_time_mix_r"]).reshape(-1)),
        "w1_sh": sl(inputs["L0_ffn_w1"]),
        "w2_sh": sl(inputs["L0_ffn_w2"]),
        "wup_sh": sl(inputs["L1_ffn_w_up"]),
        "wgate_sh": sl(inputs["L1_ffn_w_gate"]),
        "wdown_sh": sl(inputs["L1_ffn_w_down"]),
        "wk_sh": sl(inputs["L1_attn_Wk"]),
        "wv_sh": sl(inputs["L1_attn_Wv"]),
        "wr_sh": sl(inputs["L1_attn_Wr"]),
        "wo_sh": sl(inputs["L1_attn_Wo"]),
    }


def kernel(**inputs):
    nc, p = build(debug_stop=DEBUG_STOP, debug_dumps=DEBUG_DUMPS)
    in_maps = [host_inputs(inputs, c) for c in range(NCORES)]
    res = run_bass_kernel_spmd(nc, in_maps, core_ids=list(range(NCORES)))
    return np.stack([np.asarray(res.results[c]["out"]) for c in range(NCORES)], axis=0)
